# version 32
# baseline (speedup 1.0000x reference)
import contextlib
import numpy as np
import ml_dtypes
import concourse.bass as bass
import concourse.mybir as mybir
from concourse.bass_utils import run_bass_kernel_spmd

F32 = mybir.dt.float32
I32 = mybir.dt.int32
AF = mybir.ActivationFunctionType
ALU = mybir.AluOpType
AX = mybir.AxisListType

C = 1024
T = 2048
E = 8
EPS_LN = 1e-5
EPS_MM = 1e-6
PI = float(np.pi)
INV2PI = float(1.0 / (2.0 * np.pi))

DEBUG_TAPS = []
TRUNC = None

XN0 = 0
ACC0 = 16384
V0 = 32768
W0 = 49152
ARENA_N = 53248
SC_O = XN0 + 0
SSC_O = XN0 + 2048
MT_O = XN0 + 2048
ST_O = XN0 + 2064
SRS_O = XN0 + 2080
S11_O = XN0 + 4096
U_O = XN0 + 8192
A0_O = XN0 + 10240
A1_O = XN0 + 11264
M0_O = XN0 + 12288
M1_O = XN0 + 13312
SROW_O = XN0 + 14336
SSCH_O = XN0 + 4096
SSCL_O = XN0 + 5120
SROWH_O = XN0 + 6144
SROWL_O = XN0 + 7168
GHI_O = XN0 + 8192
GLO_O = XN0 + 12288
H1S_O = XN0 + 4096
GT_O = XN0 + 6656
STG3_O = XN0 + 4608
W2T_O = XN0 + 6656
HTMP_O = XN0 + 7680
HBH_O = XN0 + 0
HBL_O = XN0 + 2048
P7X_O = XN0 + 0
P7O_O = XN0 + 2048
CW2_O = ACC0 + 0
P1_O = ACC0 + 2048
PSP_O = ACC0 + 2048
XSP_O = ACC0 + 4096
OSP_O = ACC0 + 6144
OS2_O = ACC0 + 8192
WFY_O = V0 + 0
WFX_O = V0 + 512
WFY2_O = V0 + 1024
WFX2_O = V0 + 2048
OWT_O = V0 + 3072

_BIAS_ORDER = [("bv", 8), ("projb", 8), ("sb1", 16), ("sb3", 16), ("sb2c", 8),
               ("b1", 64), ("b2q", 64), ("ob", 8), ("bq", 8), ("id", 8),
               ("obc", 8), ("fb2", 1), ("pihalf", 1), ("twopi", 1),
               ("invC", 1), ("epsln", 1)]
BC = {}
_c = 0
for _n, _k in _BIAS_ORDER:
    BC[_n] = _c
    _c += _k
NBIAS = _c

RA = [(3 * e) % 8 for e in range(8)]
RB = [(3 * e + 3) % 8 for e in range(8)]


class Op:
    __slots__ = ("eng", "fn", "deps", "chan", "val", "needed", "idx")


class Prog:
    def __init__(self):
        self.ops = []

    def add(self, eng, fn, deps=(), is_dma=False):
        op = Op()
        op.eng = eng
        op.fn = fn
        op.deps = [d for d in deps if d is not None]
        if is_dma:
            op.chan = {"sync": "d", "gpsimd": "g", "scalar": "s2"}[eng]
        else:
            op.chan = {"tensor": "t", "vector": "v", "scalar": "s",
                       "gpsimd": "p", "sync": "y"}[eng]
        op.val = None
        op.needed = is_dma
        op.idx = len(self.ops)
        self.ops.append(op)
        return op

    def emit(self, blk, sems):
        dma_ch = ("d", "g", "s2")
        for op in self.ops:
            for dp in op.deps:
                assert dp.idx < op.idx
                if dp.eng != op.eng or dp.chan in dma_ch:
                    dp.needed = True
        counters = {}
        for op in self.ops:
            if op.needed:
                step = 16 if op.chan in dma_ch else 1
                counters[op.chan] = counters.get(op.chan, 0) + step
                op.val = counters[op.chan]
        watermark = {}
        secmap = {"sync": blk.sync, "tensor": blk.tensor, "vector": blk.vector,
                  "scalar": blk.scalar, "gpsimd": blk.gpsimd}
        by_eng = {}
        for op in self.ops:
            by_eng.setdefault(op.eng, []).append(op)
        for eng, run in by_eng.items():
            def section(e, run=run, eng=eng):
                for op in run:
                    for dp in op.deps:
                        if dp.eng == op.eng and dp.chan not in dma_ch:
                            continue
                        key = (eng, dp.chan)
                        if watermark.get(key, 0) >= dp.val:
                            continue
                        e.wait_ge(sems[dp.chan], dp.val)
                        watermark[key] = dp.val
                    last = op.fn(e)
                    if eng == "vector" and op.chan == "v":
                        last = e.drain()
                    if op.needed:
                        if last is None:
                            last = e.nop()
                        last.then_inc(sems[op.chan], 16 if op.chan in dma_ch else 1)
            secmap[eng](section)
        return counters


def build_nc():
    nc = bass.Bass("TRN2", target_bir_lowering=False,
                   dynamic_dma_scratch_size=256)

    dx = nc.dram_tensor("x", [T, C], F32, kind="ExternalInput").ap()
    dout = nc.dram_tensor("out", [T, C], F32, kind="ExternalOutput").ap()
    dwv = nc.dram_tensor("wv", [8, 128, 1024], F32, kind="ExternalInput").ap()
    dwqT = nc.dram_tensor("wqT", [8, 128, 1024], F32, kind="ExternalInput").ap()
    dwfy = nc.dram_tensor("wfy", [128, 512], F32, kind="ExternalInput").ap()
    dwfx = nc.dram_tensor("wfx", [128, 512], F32, kind="ExternalInput").ap()
    dwfy2 = nc.dram_tensor("wfy2", [128, 1024], F32, kind="ExternalInput").ap()
    dwfx2 = nc.dram_tensor("wfx2", [128, 1024], F32, kind="ExternalInput").ap()
    doutw = nc.dram_tensor("outw", [128, 1024], F32, kind="ExternalInput").ap()
    doutwT = nc.dram_tensor("outwT", [128, 1024], F32, kind="ExternalInput").ap()
    dprojw = nc.dram_tensor("projw", [8, 128, 1024], F32, kind="ExternalInput").ap()
    dsw1 = nc.dram_tensor("sw1", [16, 128, 1024], F32, kind="ExternalInput").ap()
    dsw3 = nc.dram_tensor("sw3", [16, 128, 1024], F32, kind="ExternalInput").ap()
    dsw2 = nc.dram_tensor("sw2", [8, 128, 2048], F32, kind="ExternalInput").ap()
    dw1 = nc.dram_tensor("w1", [8, 128, 8192], F32, kind="ExternalInput").ap()
    dw2 = nc.dram_tensor("w2", [8, 128, 8192], F32, kind="ExternalInput").ap()
    dbias = nc.dram_tensor("biaspack", [128, NBIAS], F32, kind="ExternalInput").ap()
    dident = nc.dram_tensor("ident", [128, 128], F32, kind="ExternalInput").ap()
    dones = nc.dram_tensor("ones1", [1, 128], F32, kind="ExternalInput").ap()
    donesb = nc.dram_tensor("ones1b", [1, 128], mybir.dt.bfloat16,
                            kind="ExternalInput").ap()
    dxnsp = nc.dram_tensor("xnsp", [128, 16384], F32, kind="Internal").ap()
    dswsp = nc.dram_tensor("swsp", [128, 49152], F32, kind="Internal").ap()
    dtaps = {}
    for tp in DEBUG_TAPS:
        dtaps[tp] = nc.dram_tensor("tap_" + tp, [128, 16384], F32,
                                   kind="ExternalOutput").ap()

    st = contextlib.ExitStack()
    with st:
        AR = st.enter_context(nc.sbuf_tensor("AR", [128, ARENA_N], F32))
        ART = AR
        KI = st.enter_context(nc.sbuf_tensor("KI", [128, 1024], I32))
        BP = st.enter_context(nc.sbuf_tensor("BP", [128, NBIAS], F32))
        IDN = st.enter_context(nc.sbuf_tensor("IDN", [128, 128], F32))
        ONER = st.enter_context(nc.sbuf_tensor("ONER", [1, 128], F32))
        ONEB = st.enter_context(nc.sbuf_tensor("ONEB", [1, 128],
                                               mybir.dt.bfloat16))
        OUTW = st.enter_context(nc.sbuf_tensor("OUTW", [128, 1024], F32))
        OW2Y = st.enter_context(nc.sbuf_tensor("OW2Y", [128, 128], F32))
        OW2X = st.enter_context(nc.sbuf_tensor("OW2X", [128, 128], F32))
        CBY2 = st.enter_context(nc.sbuf_tensor("CBY2", [1, 128], F32))
        CBX2 = st.enter_context(nc.sbuf_tensor("CBX2", [1, 128], F32))
        SM = st.enter_context(nc.sbuf_tensor("SM", [128, 16], F32))
        PF = st.enter_context(nc.psum_tensor("PF", [128, 2048], F32))
        PZA = st.enter_context(nc.psum_tensor("PZA", [128, 1024], F32))
        PZB = st.enter_context(nc.psum_tensor("PZB", [128, 1024], F32))
        sems = {}
        for ch in ["d", "t", "v", "s", "g", "p", "y", "s2"]:
            sems[ch] = st.enter_context(nc.semaphore("sem_" + ch))
        blk = st.enter_context(nc.Block())

        P = Prog()

        def _finish():
            P.emit(blk, sems)


        def ar(off, n, p0=0, p1=128):
            return AR[p0:p1, off:off + n]

        BF16 = mybir.dt.bfloat16

        def bf(off, n_bf16):
            return AR[:, off:off + n_bf16 // 2].bitcast(BF16)

        def bp(name, i=0):
            c0 = BC[name] + i
            return BP[:, c0:c0 + 1]

        def xn_ap(ch, t0, tn):
            return ar(XN0 + ch * 2048 + t0, tn)

        def xn2_ap(ch, t0, tn):
            return ar(V0 + ch * 2048 + t0, tn)

        def v_ap(ch, t0, tn):
            return ar(V0 + ch * 2048 + t0, tn)

        def acc_ap(ch, t0, tn):
            return ar(ACC0 + ch * 2048 + t0, tn)

        bank8 = [lambda: PF[:, 0:512], lambda: PF[:, 512:1024],
                 lambda: PF[:, 1024:1536], lambda: PF[:, 1536:2048],
                 lambda: PZA[:, 0:512], lambda: PZA[:, 512:1024],
                 lambda: PZB[:, 0:512], lambda: PZB[:, 512:1024]]

        def tap(name, off, dep_ops, n=16384):
            if name in DEBUG_TAPS:
                return P.add("sync", (lambda e, name=name, off=off, n=n: e.dma_start(
                    out=dtaps[name][:, 0:n], in_=ar(off, n))), deps=dep_ops,
                    is_dma=True)
            return None

        ld = {}
        for nm, dst, src in [
                ("bias", BP[:], dbias), ("ident", IDN[:], dident),
                ("ones", ONER[:], dones), ("onesb", ONEB[:], donesb),
                ("outw", OUTW[:], doutw),
                ("wfy", ar(WFY_O, 512), dwfy), ("wfx", ar(WFX_O, 512), dwfx),
                ("wfy2", ar(WFY2_O, 1024), dwfy2), ("wfx2", ar(WFX2_O, 1024), dwfx2),
                ("outwT", ar(OWT_O, 1024), doutwT)]:
            ld[nm] = P.add("sync", (lambda e, dst=dst, src=src:
                                    e.dma_start(out=dst, in_=src)), is_dma=True)

        wq_ld = []
        cw_ev = []
        for cch in range(8):
            slot = W0 + (cch % 2) * 1024
            deps = [cw_ev[cch - 2]] if cch >= 2 else []
            wq_ld.append(P.add("sync", (lambda e, s=slot, c=cch: e.dma_start(
                out=ar(s, 1024), in_=dwqT[c])), deps=deps, is_dma=True))

            def cwmm(e, s=slot):
                for d in range(8):
                    e.matmul(PZA[:, 0:64], ar(s + d * 128, 128),
                             ar(WFY_O + d * 64, 64), start=(d == 0), stop=False)
                for d in range(8):
                    last = e.matmul(PZA[:, 64:128], ar(s + d * 128, 128),
                                    ar(WFX_O + d * 64, 64), start=(d == 0), stop=(d == 7))
                return last
            deps = [wq_ld[cch], ld["wfy"], ld["wfx"]]
            if cch >= 1:
                deps.append(cw_ev[cch - 1])
            mm = P.add("tensor", cwmm, deps=deps)

            def cwev(e, c=cch):
                e.copy(ar(CW2_O + c * 128, 64), PZA[:, 0:64])
                e.copy(ar(CW2_O + c * 128 + 64, 64), PZA[:, 0:64])
                e.copy(ar(CW2_O + 1024 + c * 128, 64), PZA[:, 64:128])
                return e.copy(ar(CW2_O + 1024 + c * 128 + 64, 64), PZA[:, 64:128])
            cw_ev.append(P.add("scalar", cwev, deps=[mm]))

        def ow_mm(e):
            for cch in range(8):
                e.matmul(PZB[:, 0:64], ar(OWT_O + cch * 128, 128),
                         ar(WFY_O + cch * 64, 64), start=(cch == 0), stop=False)
            for cch in range(8):
                last = e.matmul(PZB[:, 64:128], ar(OWT_O + cch * 128, 128),
                                ar(WFX_O + cch * 64, 64), start=(cch == 0), stop=(cch == 7))
            return last
        ow_op = P.add("tensor", ow_mm, deps=[ld["outwT"], ld["wfy"], ld["wfx"]])

        def ow_ev(e):
            for half, srccols in ((0, PZB[:, 0:64]), (1, PZB[:, 0:64])):
                e.copy(OW2Y[:, 0:64].bitcast(BF16)[:, half * 64:half * 64 + 64],
                       srccols)
            for half, srccols in ((0, PZB[:, 64:128]), (1, PZB[:, 64:128])):
                last = e.copy(OW2X[:, 0:64].bitcast(BF16)[:, half * 64:half * 64 + 64],
                              srccols)
            return last
        ow_ev_op = P.add("scalar", ow_ev, deps=[ow_op])

        def ow_lo(e):
            for half in range(2):
                e.tensor_tensor(OW2Y[:, 64:128].bitcast(BF16)[:, half * 64:half * 64 + 64],
                                PZB[:, 0:64],
                                OW2Y[:, 0:64].bitcast(BF16)[:, half * 64:half * 64 + 64],
                                ALU.subtract)
                e.drain()
                last = e.tensor_tensor(OW2X[:, 64:128].bitcast(BF16)[:, half * 64:half * 64 + 64],
                                       PZB[:, 64:128],
                                       OW2X[:, 0:64].bitcast(BF16)[:, half * 64:half * 64 + 64],
                                       ALU.subtract)
                if half == 0:
                    e.drain()
            return last
        ow_lo_op = P.add("vector", ow_lo, deps=[ow_ev_op])

        def cb_mm(e):
            for cch in range(8):
                e.matmul(PZB[0:1, 128:256], BP[:, BC["obc"] + cch:BC["obc"] + cch + 1],
                         ar(WFY2_O + cch * 128, 128), start=(cch == 0), stop=False)
            for cch in range(8):
                e.matmul(PZB[0:1, 256:384], BP[:, BC["obc"] + cch:BC["obc"] + cch + 1],
                         ar(WFX2_O + cch * 128, 128), start=(cch == 0), stop=False)
            for cch in range(8):
                e.matmul(PZB[:, 384:385], ar(WFY2_O + cch * 128, 128),
                         BP[:, BC["id"] + cch:BC["id"] + cch + 1],
                         start=(cch == 0), stop=False)
            for cch in range(8):
                e.matmul(PZB[:, 385:386], ar(WFY2_O + cch * 128, 128),
                         BP[:, BC["bq"] + cch:BC["bq"] + cch + 1],
                         start=(cch == 0), stop=False)
            for cch in range(8):
                last = e.matmul(PZB[:, 386:387], ar(WFX2_O + cch * 128, 128),
                                BP[:, BC["bq"] + cch:BC["bq"] + cch + 1],
                                start=(cch == 0), stop=(cch == 7))
            return last
        cb_op = P.add("tensor", cb_mm,
                      deps=[ld["bias"], ld["wfy2"], ld["wfx2"], ow_ev_op,
                            ow_lo_op])

        def cb_ev(e):
            e.copy(CBY2[:, 0:64].bitcast(BF16), PZB[0:1, 128:256])
            e.copy(CBX2[:, 0:64].bitcast(BF16), PZB[0:1, 256:384])
            e.copy(SM[:, 2:3], PZB[:, 384:385])
            e.copy(SM[:, 4:5], PZB[:, 385:386])
            return e.copy(SM[:, 5:6], PZB[:, 386:387])
        cb_ev_op = P.add("scalar", cb_ev, deps=[cb_op])

        def cb_lo(e):
            e.tensor_tensor(CBY2[:, 64:128].bitcast(BF16), PZB[0:1, 128:256],
                            CBY2[:, 0:64].bitcast(BF16), ALU.subtract)
            e.drain()
            return e.tensor_tensor(CBX2[:, 64:128].bitcast(BF16),
                                   PZB[0:1, 256:384],
                                   CBX2[:, 0:64].bitcast(BF16), ALU.subtract)
        cb_lo_op = P.add("vector", cb_lo, deps=[cb_ev_op])

        def cc_build(e):
            e.tensor_tensor(SM[:, 0:1], bp("fb2"), bp("pihalf"), ALU.add)
            e.drain()
            e.tensor_tensor(SM[:, 1:2], SM[:, 0:1], SM[:, 4:5], ALU.add)
            e.drain()
            e.tensor_tensor(SM[:, 1:2], SM[:, 1:2], SM[:, 5:6], ALU.add)
            e.drain()
            e.tensor_tensor(SM[:, 3:4], SM[:, 2:3], SM[:, 4:5], ALU.subtract)
            e.scalar_tensor_tensor(BP[:, BC["sb2c"]:BC["sb2c"] + 8],
                                   BP[:, BC["sb2c"]:BC["sb2c"] + 8], 0.5,
                                   BP[:, BC["projb"]:BC["projb"] + 8],
                                   ALU.mult, ALU.add)
            return e.tensor_scalar(BP[:, BC["b2q"]:BC["b2q"] + 64],
                                   BP[:, BC["b2q"]:BC["b2q"] + 64],
                                   0.25, None, ALU.mult)
        cc_op = P.add("vector", cc_build, deps=[cb_ev_op, cb_lo_op, ld["bias"]])

        if TRUNC == "p0":
            _finish()
            return nc

        XT_O = [P1_O, P1_O + 1024]
        XCB_O = [P1_O + 2048, P1_O + 3072]
        XNT_O = [P1_O + 4096, P1_O + 5120]
        TRH_O = P1_O + 6144
        MC = [8, 9]
        VC = [10, 11]
        x_ld, ln_vst, ln_vn, ln_ttr, ln_ev = [], [], [], [], []

        def emit_xld(j, deps):
            x_ld.append(P.add("sync", (lambda e, j=j: e.dma_start(
                out=ar(XT_O[j % 2], 1024), in_=dx[j * 128:(j + 1) * 128, :])),
                deps=deps, is_dma=True))
        emit_xld(0, [])
        emit_xld(1, [])
        for j in range(16):
            xt_o, xc_o, xnt_o = XT_O[j % 2], XCB_O[j % 2], XNT_O[j % 2]
            mcol, vcol = MC[j % 2], VC[j % 2]

            def v_stats(e, xt_o=xt_o, xc_o=xc_o, mcol=mcol):
                e.tensor_reduce(SM[:, mcol:mcol + 1], ar(xt_o, 1024), AX.X, ALU.add)
                e.drain()
                e.tensor_scalar(SM[:, mcol:mcol + 1], SM[:, mcol:mcol + 1],
                                1.0 / C, None, ALU.mult)
                e.drain()
                return e.tensor_scalar(ar(xc_o, 1024), ar(xt_o, 1024),
                                       SM[:, mcol:mcol + 1], None, ALU.subtract)
            deps = [x_ld[j], cc_op]
            if j >= 2:
                deps += [ln_vn[j - 2]]
            vst = P.add("vector", v_stats, deps=deps)
            ln_vst.append(vst)
            if j + 2 < 16:
                emit_xld(j + 2, [vst])

            def s_sq(e, xc_o=xc_o, vcol=vcol):
                e.activation(ar(TRH_O, 1024), ar(xc_o, 1024), AF.Square,
                             accum_out=SM[:, vcol:vcol + 1])
                return e.activation(SM[:, vcol:vcol + 1], SM[:, vcol:vcol + 1],
                                    AF.Sqrt, bias=bp("epsln"), scale=bp("invC"))
            sq = P.add("scalar", s_sq, deps=[vst])

            def v_norm(e, xc_o=xc_o, xnt_o=xnt_o, vcol=vcol):
                e.reciprocal(SM[:, vcol:vcol + 1], SM[:, vcol:vcol + 1])
                e.drain()
                return e.tensor_scalar(ar(xnt_o, 1024), ar(xc_o, 1024),
                                       SM[:, vcol:vcol + 1], None, ALU.mult)
            vdeps = [sq]
            if j >= 2:
                vdeps.append(ln_ttr[j - 2])
            vn = P.add("vector", v_norm, deps=vdeps)
            ln_vn.append(vn)

            def t_tr(e, xnt_o=xnt_o, j=j):
                for q in range(8):
                    if j % 2 == 0:
                        ps = PZA if q < 4 else PZB
                        dst = ps[:, (q % 4) * 128:(q % 4) * 128 + 128]
                    else:
                        dst = PF[:, q * 128:q * 128 + 128]
                    last = e.transpose(dst, ar(xnt_o + q * 128, 128), IDN[:])
                return last
            tdeps = [vn, ld["ident"]]
            if j >= 2:
                tdeps.append(ln_ev[j - 2])
            if j <= 1:
                tdeps += [cw_ev[7], cb_ev_op, ow_ev_op, cb_lo_op]
            ttr = P.add("tensor", t_tr, deps=tdeps)
            ln_ttr.append(ttr)

            def s_ev(e, j=j):
                for h in range(2):
                    if j % 2 == 0:
                        src = (PZA if h == 0 else PZB)[:, 0:512] \
                            .rearrange("p (a b) -> p a b", b=128)
                    else:
                        src = PF[:, h * 512:h * 512 + 512] \
                            .rearrange("p (a b) -> p a b", b=128)
                    dst = bass.AP(ART, XN0 + h * 8192 + j * 128,
                                  [[ARENA_N, 128], [2048, 4], [1, 128]])
                    last = e.copy(dst, src)
                return last
            ln_ev.append(P.add("scalar", s_ev, deps=[ttr]))
        p1_last = ln_ev[15]
        tap_xn = tap("xn", XN0, [p1_last, ln_ev[14]])

        if TRUNC == "p1":
            _finish()
            return nc

        wv_ld = []
        vmm = {}
        v_ev = {}
        for m in range(8):
            deps = [cw_ev[7]]
            if m >= 2:
                deps.append(vmm[(m - 2, 3)])
            wv_ld.append(P.add("sync", (lambda e, m=m: e.dma_start(
                out=ar(W0 + (m % 2) * 1024, 1024), in_=dwv[m])),
                deps=deps, is_dma=True))
            for q in range(4):
                bk = bank8[4 + q]

                def f_mm(e, m=m, q=q, bk=bk):
                    for k in range(8):
                        last = e.matmul(bk(), ar(W0 + (m % 2) * 1024 + k * 128, 128),
                                        xn_ap(k, q * 512, 512),
                                        start=(k == 0), stop=(k == 7))
                    return last
                deps = [wv_ld[m], p1_last, ln_ev[14]]
                if (m - 1, q) in v_ev:
                    deps.append(v_ev[(m - 1, q)])
                vmm[(m, q)] = P.add("tensor", f_mm, deps=deps)

                def f_ev(e, m=m, q=q, bk=bk):
                    return e.activation(v_ap(m, q * 512, 512), bk(), AF.Identity,
                                        bias=bp("bv", m))
                dps = [vmm[(m, q)]]
                if m <= 1:
                    dps.append(cc_op)
                v_ev[(m, q)] = P.add("scalar", f_ev, deps=dps)
        tap("v", V0, [v_ev[(7, 3)], v_ev[(6, 3)]])

        q0_mm = []
        for b in range(4):
            def f_q0(e, b=b):
                lo, hi = b * 512, (b + 1) * 512
                for d in range(8):
                    e.matmul(PF[:, lo:hi], ar(CW2_O + 1024 + d * 128, 128),
                             xn_ap(d, lo, 512), start=(d == 0), stop=False)
                ylo = max(1, lo)
                for d in range(8):
                    last = e.matmul(PF[:, ylo:hi], ar(CW2_O + d * 128, 128),
                                    xn_ap(d, ylo - 1, hi - ylo),
                                    start=False, stop=(d == 7))
                return last
            q0_mm.append(P.add("tensor", f_q0,
                               deps=[p1_last, ln_ev[14], cw_ev[7]]))

        presplit_done = {}
        reload_gate = [None]
        ps_pieces = []
        for m in range(16):
            ps_pieces.append(("sw1", m, dsw1[m], m * 1024, 1024, False))
            ps_pieces.append(("sw3", m, dsw3[m], 16384 + m * 1024, 1024, False))
        for c_ in range(8):
            ps_pieces.append(("sw2", c_, dsw2[c_][:, 0:1024],
                              32768 + c_ * 2048, 1024, True))
            ps_pieces.append(("sw2b", c_, dsw2[c_][:, 1024:2048],
                              32768 + c_ * 2048 + 512, 1024, True))
        ps_ops = []
        ps_dmas = []
        ps_queue = list(enumerate(ps_pieces))
        ps_pos = [0]

        def emit_presplit(nmax, slot_guard=()):
            done = 0
            while ps_pos[0] < len(ps_queue) and done < nmax:
                pi, (nm_, idx_, src_, dsto_, n_, half_) = ps_queue[ps_pos[0]]
                ps_pos[0] += 1
                done += 1
                stg = None if (pi % 2 == 0) else 0
                if pi < 32:
                    m_ = pi // 2
                    slot = W0 + (m_ % 2) * 2048
                    ohi = slot + (pi % 2) * 1024
                    olo = ohi + 512
                else:
                    c_ = (pi - 32) // 2
                    slot = W0 + (c_ % 2) * 2048
                    ohi = slot + (pi % 2) * 512
                    olo = ohi + 1024

                def stg_ap(stg):
                    return KI[:].bitcast(F32) if stg is None else ar(stg, 1024)
                din_deps = [ln_vst[15], reload_gate[0]]
                if pi >= 2:
                    din_deps.append(ps_ops[pi - 2])
                din = P.add("scalar", (lambda e, stg=stg, src_=src_:
                                       e.dma_start(out=stg_ap(stg), in_=src_)),
                            deps=din_deps, is_dma=True)

                def f_ps_hi(e, stg=stg, ohi=ohi):
                    return e.copy(bf(ohi, 1024), stg_ap(stg))
                hi_op = P.add("scalar", f_ps_hi, deps=[din] + list(slot_guard))

                def f_ps(e, stg=stg, ohi=ohi, olo=olo):
                    return e.tensor_tensor(bf(olo, 1024), stg_ap(stg),
                                           bf(ohi, 1024), ALU.subtract)
                vop = P.add("gpsimd", f_ps, deps=[din, hi_op])
                ps_ops.append(vop)
                if nm_ in ("sw2", "sw2b"):
                    P.add("sync", (lambda e, ohi=ohi, dsto_=dsto_:
                                   e.dma_start(out=dswsp[:, dsto_:dsto_ + 512],
                                               in_=ar(ohi, 512))),
                          deps=[vop], is_dma=True)
                    dout_ = P.add("sync", (lambda e, olo=olo, dsto_=dsto_:
                                  e.dma_start(out=dswsp[:, dsto_ + 1024:dsto_ + 1536],
                                              in_=ar(olo, 512))),
                                  deps=[vop], is_dma=True)
                else:
                    dout_ = P.add("sync", (lambda e, ohi=ohi, dsto_=dsto_:
                                  e.dma_start(out=dswsp[:, dsto_:dsto_ + 1024],
                                              in_=ar(ohi, 1024))),
                                  deps=[vop], is_dma=True)
                ps_dmas.append(dout_)
                key = ("sw2", idx_) if nm_ in ("sw2", "sw2b") else (nm_, idx_)
                presplit_done[key] = dout_


        xs_ops = []
        xs_dmas = []
        for kk in range(8):
            def f_xs(e, kk=kk):
                hi = bf(XSP_O, 2048).rearrange("p (r g) -> p g r", g=256)
                lo = bf(XSP_O + 1024, 2048).rearrange("p (r g) -> p g r", g=256)
                xsrc = ar(XN0 + kk * 2048, 2048).rearrange("p (g r) -> p g r", r=8)
                e.tensor_copy(hi, xsrc)
                e.drain()
                return e.scalar_tensor_tensor(lo, hi, -1.0, xsrc,
                                              ALU.mult, ALU.add)
            vdeps = [p1_last, ln_ev[14]]
            if kk >= 1:
                vdeps.append(xs_dmas[-1])
            vop = P.add("vector", f_xs, deps=vdeps)
            xs_ops.append(vop)
            P.add("sync", (lambda e, kk=kk: e.dma_start(
                out=dxnsp[:, kk * 1024:kk * 1024 + 1024],
                in_=ar(XSP_O, 1024))), deps=[vop, tap_xn], is_dma=True)
            xs_dmas.append(P.add("sync", (lambda e, kk=kk: e.dma_start(
                out=dxnsp[:, 8192 + kk * 1024:8192 + kk * 1024 + 1024],
                in_=ar(XSP_O + 1024, 1024))), deps=[vop, tap_xn], is_dma=True))
        spill = P.add("sync", lambda e: None,
                      deps=[xs_dmas[-1], q0_mm[3]], is_dma=False)

        if TRUNC == "p4":
            _finish()
            return nc

        state = {"ssch": None, "sscl": None, "srh": None, "srl": None,
                 "pf_free": None, "sc": None,
                 "sc_cons": [], "s11": None, "ystt": []}

        def emit_iter(k):
            n = 1 << (k - 1)
            first = (k == 1)
            last = (k == 11)
            if first:
                fmm = q0_mm
                hb = SM[:, 3:4]
                cc = SM[:, 1:2]
            else:
                hb = SM[:, 2:3]
                cc = SM[:, 0:1]
                fmm = []
                SSCH = bf(SSCH_O, 2048)
                SSCL = bf(SSCL_O, 2048)
                SRH = AR[0:1, SROWH_O:SROWH_O + 1024].bitcast(BF16)
                SRL = AR[0:1, SROWL_O:SROWL_O + 1024].bitcast(BF16)
                OWXH = OW2X[:, 0:64].bitcast(BF16)
                OWXL = OW2X[:, 64:128].bitcast(BF16)
                OWYH = OW2Y[:, 0:64].bitcast(BF16)
                OWYL = OW2Y[:, 64:128].bitcast(BF16)
                CBXH = CBX2[:, 0:64].bitcast(BF16)
                CBXL = CBX2[:, 64:128].bitcast(BF16)
                CBYH = CBY2[:, 0:64].bitcast(BF16)
                CBYL = CBY2[:, 64:128].bitcast(BF16)
                fmm = []
                fh_prev = []
                for bp_ in range(2):
                    def f_ab_h(e, n=n, bp_=bp_):
                        last = None
                        for b in (2 * bp_, 2 * bp_ + 1):
                            lo, hi = b * 512, (b + 1) * 512
                            ylo = max(n, lo)
                            e.matmul(PF[:, lo:hi], OWXH, SSCH[:, lo:hi],
                                     start=True, stop=False)
                            last = e.matmul(PF[:, lo:hi], CBXH, SRH[:, lo:hi],
                                            start=False, stop=False)
                            if ylo < hi:
                                e.matmul(PF[:, ylo:hi], OWYH,
                                         SSCH[:, ylo - n:hi - n],
                                         start=False, stop=False)
                                last = e.matmul(PF[:, ylo:hi], CBYH,
                                                SRH[:, ylo - n:hi - n],
                                                start=False, stop=False)
                        return last
                    hdeps = [state["ssch"][bp_], state["srh"]] \
                        + state["pf_free"] + fh_prev
                    if bp_ == 1:
                        hdeps.append(state["ssch"][0])
                    fh = P.add("tensor", f_ab_h, deps=hdeps)
                    fh_prev = [fh]
                    fmm.append(fh)

                def f_ab_l(e, n=n):
                    last = None
                    for b in range(4):
                        lo, hi = b * 512, (b + 1) * 512
                        ylo = max(n, lo)
                        e.matmul(PF[:, lo:hi], OWXH, SSCL[:, lo:hi],
                                 start=False, stop=False)
                        e.matmul(PF[:, lo:hi], OWXL, SSCH[:, lo:hi],
                                 start=False, stop=False)
                        e.matmul(PF[:, lo:hi], CBXH, SRL[:, lo:hi],
                                 start=False, stop=False)
                        last = e.matmul(PF[:, lo:hi], CBXL, SRH[:, lo:hi],
                                        start=False, stop=(ylo >= hi))
                        if ylo < hi:
                            e.matmul(PF[:, ylo:hi], OWYH,
                                     SSCL[:, ylo - n:hi - n],
                                     start=False, stop=False)
                            e.matmul(PF[:, ylo:hi], OWYL,
                                     SSCH[:, ylo - n:hi - n],
                                     start=False, stop=False)
                            e.matmul(PF[:, ylo:hi], CBYH,
                                     SRL[:, ylo - n:hi - n],
                                     start=False, stop=False)
                            last = e.matmul(PF[:, ylo:hi], CBYL,
                                            SRH[:, ylo - n:hi - n],
                                            start=False, stop=True)
                    return last
                fl = P.add("tensor", f_ab_l,
                           deps=fmm + state["sscl"] + [state["srl"]])
                fmm = fmm + [fl]
            def f_head(e, n=n, hb=hb):
                return e.activation(PF[:, 0:n], PF[:, 0:n], AF.Identity, bias=hb)
            head = P.add("scalar", f_head, deps=list(fmm))

            vr_ops = []
            for h in range(2):
                def f_u(e, h=h, cc=cc):
                    return e.tensor_scalar(ar(U_O + h * 1024, 1024),
                                           PF[:, h * 1024:(h + 1) * 1024],
                                           cc, INV2PI, ALU.add, ALU.mult)
                du = [head] + list(fmm)
                if first:
                    du.append(spill)
                uop = P.add("vector", f_u, deps=du)

                def f_k(e, h=h):
                    return e.tensor_copy(KI[:], ar(U_O + h * 1024, 1024))
                kop = P.add("vector", f_k, deps=[uop])

                def f_w(e, h=h):
                    return e.scalar_tensor_tensor(ar(U_O + h * 1024, 1024), KI[:],
                                                  -1.0, ar(U_O + h * 1024, 1024),
                                                  ALU.mult, ALU.add)
                vr_ops.append(P.add("vector", f_w, deps=[kop]))

            sin_h = []
            for h in range(2):
                def f_sin(e, h=h):
                    return e.activation(ar(SC_O + h * 1024, 1024),
                                        ar(U_O + h * 1024, 1024), AF.Sin,
                                        scale=bp("twopi"))
                sdeps = [vr_ops[h]] + state["sc_cons"]
                if first:
                    sdeps.append(spill)
                sin_h.append(P.add("scalar", f_sin, deps=sdeps))
            new_sc = sin_h[1]

            sc_cons = []
            piece_free = {}
            vmax = {}
            aev = {}
            for c in range(8):
                for h in range(2):
                    p = c * 2 + h
                    pz = PZA if p % 2 == 0 else PZB

                    def f_z2(e, c=c, h=h, pz=pz):
                        e.matmul(pz[:, 0:512], OUTW[:, c * 128:(c + 1) * 128],
                                 ar(SC_O + h * 1024, 512), start=True, stop=True)
                        return e.matmul(pz[:, 512:1024],
                                        OUTW[:, c * 128:(c + 1) * 128],
                                        ar(SC_O + h * 1024 + 512, 512),
                                        start=True, stop=True)
                    zdeps = [sin_h[h]]
                    if p >= 2:
                        zdeps += piece_free[p - 2]
                    zmm = P.add("tensor", f_z2, deps=zdeps)
                    sc_cons.append(zmm)

                    def f_abs(e, c=c, h=h, pz=pz):
                        dst = (M0_O if h == 0 else M1_O) if c == 0 else \
                              (A0_O if h == 0 else A1_O)
                        return e.activation(ar(dst, 1024), pz[:], AF.Abs,
                                            bias=bp("ob", c))
                    adeps = [zmm]
                    if c >= 2:
                        adeps.append(vmax[(c - 1, h)])
                    if first and c == 0:
                        adeps.append(spill)
                    a = P.add("scalar", f_abs, deps=adeps)
                    aev[(c, h)] = a
                    frees = [a]
                    if last:
                        def f_y(e, c=c, h=h, pz=pz):
                            return e.scalar_tensor_tensor(
                                v_ap(c, h * 1024, 1024), pz[:], bp("ob", c),
                                v_ap(c, h * 1024, 1024), ALU.add, ALU.mult)
                        yop = P.add("vector", f_y,
                                    deps=[zmm, a, v_ev[(c, 2 * h)],
                                          v_ev[(c, 2 * h + 1)]])
                        state["ystt"].append(yop)
                        frees.append(yop)
                    piece_free[p] = frees
                    if c >= 1:
                        def f_max(e, h=h):
                            mo = M0_O if h == 0 else M1_O
                            ao = A0_O if h == 0 else A1_O
                            return e.tensor_tensor(ar(mo, 1024), ar(mo, 1024),
                                                   ar(ao, 1024), ALU.max)
                        mdeps = [a, aev[(c - 1, h)] if c == 1 else vmax[(c - 1, h)]]
                        vmax[(c, h)] = P.add("vector", f_max, deps=mdeps)
            state["sc_cons"] = sc_cons

            rd_ops = []
            for g in range(4):
                def f_trg(e, g=g):
                    for q in range(4):
                        jt = g * 4 + q
                        h, jb = jt // 8, jt % 8
                        mo = M0_O if h == 0 else M1_O
                        last = e.transpose(PF[:, jt * 128:jt * 128 + 128],
                                           ar(mo + jb * 128, 128), IDN[:])
                    return last
                tdeps = [vmax[(7, g // 2)]] + vr_ops
                trg = P.add("tensor", f_trg, deps=tdeps)

                def f_rdg(e, g=g):
                    return e.tensor_reduce(
                        ar(MT_O + g * 4, 4),
                        PF[:, g * 512:(g + 1) * 512].rearrange(
                            "p (j q) -> p j q", q=128),
                        AX.X, ALU.max)
                rd_ops.append(P.add("vector", f_rdg, deps=[trg]))

            def f_s(e):
                e.tensor_scalar(ar(ST_O, 16), ar(MT_O, 16), EPS_MM, None, ALU.add)
                e.drain()
                return e.reciprocal(ar(ST_O, 16), ar(ST_O, 16))
            sop = P.add("vector", f_s, deps=rd_ops)

            def f_str(e):
                return e.transpose(PF[0:16, 0:128], ar(ST_O, 16), IDN[:])
            strop = P.add("tensor", f_str, deps=[sop] + rd_ops)

            def f_sre(e):
                return e.copy(AR[0:16, SRS_O:SRS_O + 64].bitcast(BF16),
                              PF[0:16, 0:128])
            sre = P.add("scalar", f_sre, deps=[strop])

            def f_srl(e):
                return e.scalar_tensor_tensor(
                    AR[0:16, SRS_O + 64:SRS_O + 128].bitcast(BF16),
                    AR[0:16, SRS_O:SRS_O + 64].bitcast(BF16), -1.0,
                    PF[0:16, 0:128], ALU.mult, ALU.add)
            srl_v = P.add("vector", f_srl, deps=[sre])

            dma_h = P.add("scalar", (lambda e: e.dma_start(
                out=AR[0:1, SROWH_O:SROWH_O + 1024].bitcast(BF16),
                in_=AR[0:16, SRS_O:SRS_O + 64].bitcast(BF16))),
                deps=[sre], is_dma=True)
            dma_l = P.add("sync", (lambda e: e.dma_start(
                out=AR[0:1, SROWL_O:SROWL_O + 1024].bitcast(BF16),
                in_=AR[0:16, SRS_O + 64:SRS_O + 128].bitcast(BF16))),
                deps=[srl_v], is_dma=True)

            spk_h = []
            for hh in range(2):
                def f_spk(e, hh=hh):
                    SRH = AR[0:1, SROWH_O:SROWH_O + 1024].bitcast(BF16)
                    SRL = AR[0:1, SROWL_O:SROWL_O + 1024].bitcast(BF16)
                    for b in (2 * hh, 2 * hh + 1):
                        sl = slice(b * 512, (b + 1) * 512)
                        e.matmul(PF[:, sl], ONEB[:], SRH[:, sl],
                                 start=True, stop=False)
                        last_ = e.matmul(PF[:, sl], ONEB[:], SRL[:, sl],
                                         start=False, stop=True)
                    return last_
                spk_h.append(P.add("tensor", f_spk,
                                   deps=[dma_h, dma_l, ld["onesb"]]))

            if not last:
                ssc32_h = []
                ssch_h = []
                sscl_h = []
                for hh in range(2):
                    def f_ssc(e, hh=hh):
                        return e.tensor_tensor(ar(SSC_O + hh * 1024, 1024),
                                               ar(SC_O + hh * 1024, 1024),
                                               PF[:, hh * 1024:(hh + 1) * 1024],
                                               ALU.mult)
                    ssc32_h.append(P.add("vector", f_ssc,
                                         deps=[spk_h[hh]] + sc_cons))

                    def f_ssch(e, hh=hh):
                        return e.copy(bf(SSCH_O + hh * 512, 1024),
                                      ar(SSC_O + hh * 1024, 1024))
                    ssch_h.append(P.add("scalar", f_ssch, deps=[ssc32_h[hh]]))

                    def f_sscl(e, hh=hh):
                        return e.tensor_tensor(bf(SSCL_O + hh * 512, 1024),
                                               ar(SSC_O + hh * 1024, 1024),
                                               bf(SSCH_O + hh * 512, 1024),
                                               ALU.subtract)
                    sscl_h.append(P.add("gpsimd", f_sscl, deps=[ssch_h[hh]]))
                state["ssch"] = ssch_h
                state["sscl"] = sscl_h
                state["srh"] = dma_h
                state["srl"] = dma_l
                state["pf_free"] = ssc32_h
                state["sc_cons"] = sc_cons + ssc32_h
            else:
                def f_s11(e):
                    return e.copy(ar(S11_O, 2048), PF[:])
                state["s11"] = P.add("scalar", f_s11, deps=spk_h)
                state["pf_free"] = [state["s11"]]
            if ("q%d" % k) in DEBUG_TAPS:
                pass
            tap("sc%d" % k, SC_O, [new_sc] + sc_cons, n=2048)

            state["sc"] = new_sc

        for k in range(1, 12):
            emit_iter(k)
            if TRUNC == ("scan%d" % k):
                _finish()
                return nc
        s11_ev = state["s11"]
        y_done = state["ystt"]

        pw_ld = []
        proj_ev = {}
        pj_ops = {}
        for m in range(8):
            deps = [q0_mm[3]]
            if m >= 2:
                deps.append(pj_ops[(m - 2, 3)])
            pw_ld.append(P.add("sync", (lambda e, m=m: e.dma_start(
                out=ar(W0 + 2048 + (m % 2) * 1024, 1024), in_=dprojw[m])),
                deps=deps, is_dma=True))
            for tt in range(4):
                bk = bank8[(m % 2) * 4 + tt]

                def f_pj(e, tt=tt, m=m, bk=bk):
                    for cch in range(8):
                        last = e.matmul(bk(), ar(W0 + 2048 + (m % 2) * 1024 + cch * 128, 128),
                                        v_ap(cch, tt * 512, 512),
                                        start=(cch == 0), stop=(cch == 7))
                    return last
                deps = [pw_ld[m], s11_ev] + y_done
                if (m - 2, tt) in proj_ev:
                    deps.append(proj_ev[(m - 2, tt)])
                pj_ops[(m, tt)] = P.add("tensor", f_pj, deps=deps)

                def f_pje(e, tt=tt, m=m, bk=bk):
                    return e.tensor_tensor(acc_ap(m, tt * 512, 512), bk(),
                                           ar(S11_O + tt * 512, 512), ALU.mult)
                proj_ev[(m, tt)] = P.add("vector", f_pje,
                                         deps=[pj_ops[(m, tt)], s11_ev])
        proj_last = proj_ev[(7, 3)]
        pj_last = pj_ops[(7, 3)]
        tap("attn", ACC0, [proj_last, proj_ev[(7, 2)]])

        if TRUNC == "p6":
            _finish()
            return nc

        reload1 = P.add("sync", (lambda e: e.dma_start(
            out=ar(V0, 8192), in_=dxnsp[:, 0:8192])), deps=[proj_last], is_dma=True)
        reload2 = P.add("sync", (lambda e: e.dma_start(
            out=ar(V0 + 8192, 8192), in_=dxnsp[:, 8192:16384])),
            deps=[proj_last], is_dma=True)
        reload_gate[0] = reload2

        def xnh(k, t0, tn):
            return bf(V0 + k * 1024, 2048)[:, t0:t0 + tn]

        def xnl(k, t0, tn):
            return bf(V0 + 8192 + k * 1024, 2048)[:, t0:t0 + tn]

        def mm3(e, psum, wslot, nk, rhs_hi, rhs_lo, tsz):
            first = True
            last = None
            for kk in range(nk):
                whi = bf(wslot + kk * 64, 128)
                wlo = bf(wslot + nk * 64 + kk * 64, 128)
                last = e.matmul(psum, whi, rhs_hi(kk), start=first, stop=False)
                first = False
                e.matmul(psum, whi, rhs_lo(kk), start=False, stop=False)
                last = e.matmul(psum, wlo, rhs_hi(kk), start=False,
                                stop=(kk == nk - 1))
            return last

        sw_ld = {}
        g_ops = {}
        silu_ops = {}
        s2mm_ops = {}
        s2ev_ops = {}
        for tt in range(4):
            for m in range(16):
                slot = W0 + (m % 2) * 2048
                if tt == 0:
                    if m == 0:
                        sg = [vmm[(7, 3)]]
                    elif m == 1:
                        sg = [pj_last]
                    else:
                        sg = [g_ops[(0, m - 2)], ps_dmas[2 * (m - 2)],
                              ps_dmas[2 * (m - 2) + 1]]
                    emit_presplit(2, slot_guard=sg)
                else:
                    ldd = []
                    if m >= 2:
                        ldd.append(g_ops[(tt, m - 2)])
                    else:
                        ldd.append(s2mm_ops[(tt - 1, 7)])
                    ldd.append(presplit_done[("sw1", m)])
                    ldd.append(presplit_done[("sw3", m)])
                    sw_ld[(tt, m, 1)] = P.add("sync", (lambda e, m=m, slot=slot:
                                                       e.dma_start(out=ar(slot, 1024),
                                                                   in_=dswsp[:, m * 1024:(m + 1) * 1024])),
                                              deps=ldd, is_dma=True)
                    sw_ld[(tt, m, 3)] = P.add("sync", (lambda e, m=m, slot=slot:
                                                       e.dma_start(out=ar(slot + 1024, 1024),
                                                                   in_=dswsp[:, 16384 + m * 1024:16384 + (m + 1) * 1024])),
                                              deps=ldd, is_dma=True)

                def f_h1(e, tt=tt, m=m, slot=slot):
                    return mm3(e, PZA[:, 0:512], slot, 8,
                               lambda k: xnh(k, tt * 512, 512),
                               lambda k: xnl(k, tt * 512, 512), 512)
                d1 = [ps_ops[2 * m] if tt == 0 else sw_ld[(tt, m, 1)],
                      reload1, reload2]
                if (tt, m - 1) in silu_ops:
                    d1.append(silu_ops[(tt, m - 1)])
                elif tt >= 1:
                    d1.append(silu_ops[(tt - 1, 15)])
                h1 = P.add("tensor", f_h1, deps=d1)

                def f_h3(e, tt=tt, m=m, slot=slot):
                    return mm3(e, PZB[:, 0:512], slot + 1024, 8,
                               lambda k: xnh(k, tt * 512, 512),
                               lambda k: xnl(k, tt * 512, 512), 512)
                d3 = [ps_ops[2 * m + 1] if tt == 0 else sw_ld[(tt, m, 3)],
                      reload1, reload2]
                if (tt, m - 1) in g_ops:
                    d3.append(g_ops[(tt, m - 1)])
                elif tt >= 1:
                    d3.append(g_ops[(tt - 1, 15)])
                h3 = P.add("tensor", f_h3, deps=d3)

                def f_silu(e, m=m):
                    return e.activation(ar(H1S_O, 512), PZA[:, 0:512], AF.Silu,
                                        bias=bp("sb1", m))
                sdeps = [h1]
                if (tt, m - 1) in g_ops:
                    sdeps.append(g_ops[(tt, m - 1)])
                silu_ops[(tt, m)] = P.add("scalar", f_silu, deps=sdeps)

                def f_g(e, tt=tt, m=m):
                    e.scalar_tensor_tensor(ar(GT_O, 512), PZB[:, 0:512],
                                           bp("sb3", m), ar(H1S_O, 512),
                                           ALU.add, ALU.mult)
                    e.drain()
                    e.tensor_copy(bf(GHI_O + m * 256, 512), ar(GT_O, 512))
                    e.drain()
                    return e.scalar_tensor_tensor(bf(GLO_O + m * 256, 512),
                                                  bf(GHI_O + m * 256, 512), -1.0,
                                                  ar(GT_O, 512), ALU.mult, ALU.add)
                gd = [h3, silu_ops[(tt, m)]]
                if tt >= 1:
                    gd.append(s2mm_ops[(tt - 1, 7)])
                if (tt, m - 1) in g_ops:
                    gd.append(g_ops[(tt, m - 1)])
                g_ops[(tt, m)] = P.add("vector", f_g, deps=gd)
            for cc_ in range(8):
                slot = W0 + (cc_ % 2) * 2048
                if tt == 0:
                    if cc_ <= 1:
                        sg = [g_ops[(0, 14 + cc_)], ps_dmas[28 + 2 * cc_],
                              ps_dmas[29 + 2 * cc_]]
                    else:
                        sg = [s2mm_ops[(0, cc_ - 2)], ps_dmas[32 + 2 * (cc_ - 2)],
                              ps_dmas[33 + 2 * (cc_ - 2)]]
                    emit_presplit(2, slot_guard=sg)
                else:
                    ldd = [g_ops[(tt, 15)] if cc_ <= 1 else s2mm_ops[(tt, cc_ - 2)],
                           presplit_done[("sw2", cc_)]]
                    sw_ld[(tt, cc_, 2)] = P.add("sync", (lambda e, c=cc_, slot=slot:
                                                         e.dma_start(out=ar(slot, 2048),
                                                                     in_=dswsp[:, 32768 + c * 2048:32768 + (c + 1) * 2048])),
                                                deps=ldd, is_dma=True)

                def f_s2(e, tt=tt, c=cc_, slot=slot):
                    return mm3(e, bank8[c % 4](), slot, 16,
                               lambda k: bf(GHI_O + k * 256, 512),
                               lambda k: bf(GLO_O + k * 256, 512), 512)
                d2 = [ps_ops[33 + 2 * cc_] if tt == 0 else sw_ld[(tt, cc_, 2)],
                      g_ops[(tt, 15)]]
                if (tt, cc_ - 4) in s2ev_ops:
                    d2.append(s2ev_ops[(tt, cc_ - 4)])
                elif tt >= 1 and (tt - 1, cc_ + 4) in s2ev_ops:
                    d2.append(s2ev_ops[(tt - 1, cc_ + 4)])
                s2mm_ops[(tt, cc_)] = P.add("tensor", f_s2, deps=d2)

                tmpo = 4608 + ((tt * 8 + cc_) % 2) * 512

                def f_s2t(e, tt=tt, c=cc_, tmpo=tmpo):
                    return e.tensor_scalar(ar(tmpo, 512), bank8[c % 4](),
                                           0.5, bp("sb2c", c),
                                           ALU.mult, ALU.add)
                ts_op = P.add("vector", f_s2t,
                              deps=[s2mm_ops[(tt, cc_)],
                                    s2ev_ops.get((tt, cc_ - 2)) or
                                    s2ev_ops.get((tt - 1, cc_ + 6))])

                def f_s2e(e, tt=tt, c=cc_, tmpo=tmpo):
                    for half in range(2):
                        rr = 2 * tt + half
                        accsl = AR[:, ACC0 + c * 2048 + rr:ACC0 + (c + 1) * 2048:8]
                        last = e.tensor_tensor(accsl, accsl,
                                               ar(tmpo + half * 256, 256), ALU.add)
                        if half == 0:
                            e.drain()
                    return last
                s2ev_ops[(tt, cc_)] = P.add("vector", f_s2e,
                                            deps=[ts_op, proj_ev[(cc_, tt)]])
        presplit_last = ps_dmas[-1]
        shared_last = s2ev_ops[(3, 7)]
        tap("acc_sh", ACC0, [shared_last, s2ev_ops[(3, 6)]])

        if TRUNC == "p2":
            _finish()
            return nc

        hmm_ops = {}
        gel_ops = {}
        hsp_ops = {}
        ymm_ops = {}
        yev_ops = {}
        wq_lds = {}
        slot_consumer = {0: s2mm_ops[(3, 6)], 1: s2mm_ops[(3, 7)]}

        moe_split_prev = [None]

        def moe_load(ex, qi, extra_deps):
            slot = W0 + (qi % 2) * 2048
            src = dw1[ex][:, (qi % 4) * 2048:(qi % 4) * 2048 + 2048] if qi < 4 \
                else dw2[ex][:, (qi - 4) * 2048:(qi - 4) * 2048 + 2048]
            dmadeps = [moe_split_prev[0]] + list(extra_deps)
            op = P.add("sync", (lambda e, slot=slot, src=src:
                                e.dma_start(out=ar(STG3_O, 2048), in_=src)),
                       deps=dmadeps, is_dma=True)
            def f_sp_hi(e, slot=slot):
                return e.copy(bf(slot, 2048), ar(STG3_O, 2048))

            def f_sp(e, slot=slot):
                return e.tensor_tensor(bf(slot + 1024, 2048), ar(STG3_O, 2048),
                                       bf(slot, 2048), ALU.subtract)
            hi_op = P.add("scalar", f_sp_hi, deps=[op] + list(extra_deps))
            sp = P.add("gpsimd", f_sp, deps=[hi_op, op] + extra_deps)
            moe_split_prev[0] = sp
            wq_lds[(ex, qi)] = sp

        def moe_hmm(ex, m):
            ra, rb = RA[ex], RB[ex]
            slot = W0 + ((m // 2) % 2) * 2048
            toff = (m % 2) * 8 * 64
            bk = bank8[4 + m % 4]

            def f_hmm(e, m=m, slot=slot, toff=toff, ra=ra, rb=rb, bk=bk):
                b = bk()
                first = True
                for kk in range(8):
                    whi = bf(slot + toff + kk * 64, 128)
                    wlo = bf(slot + 1024 + toff + kk * 64, 128)
                    for which, rr in ((0, ra), (1, rb)):
                        rhi = bf(V0 + kk * 1024 + rr * 128, 256)
                        rlo = bf(V0 + 8192 + kk * 1024 + rr * 128, 256)
                        sl = b[:, which * 256:which * 256 + 256]
                        e.matmul(sl, whi, rhi, start=first, stop=False)
                        first = False
                        e.matmul(sl, whi, rlo, start=False, stop=False)
                        last = e.matmul(sl, wlo, rhi, start=False,
                                        stop=(kk == 7 and which == 1))
                return last
            hd = [wq_lds[(ex, m // 2)]]
            if (ex, m - 4) in gel_ops:
                hd.append(gel_ops[(ex, m - 4)])
            elif ex >= 1:
                hd.append(yev_ops[(ex - 1, ((m % 4) + 4, 1))])
            else:
                hd += [silu_ops[(3, 15)], g_ops[(3, 15)]]
            hmm_ops[(ex, m)] = P.add("tensor", f_hmm, deps=hd)
            slot_consumer[(m // 2) % 2] = hmm_ops[(ex, m)]

            def f_gel(e, ex=ex, m=m, bk=bk):
                return e.activation(ar(HTMP_O, 512), bk(), AF.Gelu,
                                    bias=bp("b1", ex * 8 + m))
            gdeps = [hmm_ops[(ex, m)]]
            if (ex, m - 1) in hsp_ops:
                gdeps.append(hsp_ops[(ex, m - 1)])
            elif ex >= 1:
                gdeps.append(hsp_ops[(ex - 1, 7)])
            gel_ops[(ex, m)] = P.add("scalar", f_gel, deps=gdeps)

            def f_hhi(e, m=m):
                return e.copy(bf(HBH_O + m * 256, 512), ar(HTMP_O, 512))
            hspd = [gel_ops[(ex, m)]]
            if ex >= 1:
                hspd.append(ymm_ops[(ex - 1, 7)])
            hhi = P.add("scalar", f_hhi, deps=hspd)

            def f_hlo(e, m=m):
                return e.tensor_tensor(bf(HBL_O + m * 256, 512),
                                       ar(HTMP_O, 512),
                                       bf(HBH_O + m * 256, 512), ALU.subtract)
            hsp_ops[(ex, m)] = P.add("gpsimd", f_hlo, deps=[hhi])

        def moe_ymm(ex, c):
            ra, rb = RA[ex], RB[ex]
            slot = W0 + ((c // 2) % 2) * 2048
            toff = (c % 2) * 8 * 64
            bk = bank8[4 + c % 4]

            def f_ymm(e, slot=slot, toff=toff, bk=bk):
                b = bk()
                first = True
                for kk in range(8):
                    whi = bf(slot + toff + kk * 64, 128)
                    wlo = bf(slot + 1024 + toff + kk * 64, 128)
                    rhi = bf(HBH_O + kk * 256, 512)
                    rlo = bf(HBL_O + kk * 256, 512)
                    e.matmul(b[:, 0:512], whi, rhi, start=first, stop=False)
                    first = False
                    e.matmul(b[:, 0:512], whi, rlo, start=False, stop=False)
                    last = e.matmul(b[:, 0:512], wlo, rhi, start=False,
                                    stop=(kk == 7))
                return last
            yd = [wq_lds[(ex, 4 + c // 2)], hsp_ops[(ex, 7)]]
            if (ex, (c - 4, 1)) in yev_ops:
                yd.append(yev_ops[(ex, (c - 4, 1))])
            else:
                yd.append(gel_ops[(ex, 4 + (c % 4))])
            ymm_ops[(ex, c)] = P.add("tensor", f_ymm, deps=yd)
            slot_consumer[(c // 2) % 2] = ymm_ops[(ex, c)]
            for which, rr in ((0, ra), (1, rb)):
                tmpo = 6656 + (((ex * 16 + c * 2 + which) % 4)) * 256

                def f_yt(e, c=c, which=which, bk=bk, ex=ex, tmpo=tmpo):
                    b = bk()
                    return e.tensor_scalar(ar(tmpo, 256),
                                           b[:, which * 256:which * 256 + 256],
                                           0.25, bp("b2q", ex * 8 + c),
                                           ALU.mult, ALU.add)
                ytdeps = [ymm_ops[(ex, c)]]
                prevkey = (ex, (c - 2, which)) if c >= 2 else                     ((ex - 1, (c + 6, which)) if ex >= 1 else None)
                if prevkey and prevkey[0] >= 0 and prevkey in yev_ops:
                    ytdeps.append(yev_ops[prevkey])
                yt_op = P.add("vector", f_yt, deps=ytdeps)

                def f_yev(e, c=c, rr=rr, tmpo=tmpo):
                    accsl = AR[:, ACC0 + c * 2048 + rr:ACC0 + (c + 1) * 2048:8]
                    return e.tensor_tensor(accsl, accsl, ar(tmpo, 256), ALU.add)
                yev_ops[(ex, (c, which))] = P.add(
                    "vector", f_yev, deps=[yt_op, shared_last])

        x2_ld = []
        for j in range(8):
            x2_ld.append(P.add("scalar", (lambda e, j=j: e.dma_start(
                out=ar(GHI_O + j * 1024, 1024),
                in_=dx[j * 128:(j + 1) * 128, :])),
                deps=[s2mm_ops[(3, 7)]], is_dma=True))

        for ex in range(E):
            moe_load(ex, 0, [slot_consumer[0]] +
                     ([shared_last, s2ev_ops[(3, 6)], presplit_last,
                       ps_ops[-1]] if ex == 0 else []))
            moe_load(ex, 1, [slot_consumer[1]])
            if ex == 0:
                tap("w1q0", W0, [wq_lds[(0, 0)]], n=2048)
            moe_hmm(ex, 0)
            moe_hmm(ex, 1)
            moe_load(ex, 2, [hmm_ops[(ex, 1)]] +
                     ([tap("w1q0_dummy", 0, [], n=0)] if False else []))
            moe_hmm(ex, 2)
            moe_hmm(ex, 3)
            moe_load(ex, 3, [hmm_ops[(ex, 3)]])
            moe_hmm(ex, 4)
            moe_hmm(ex, 5)
            moe_load(ex, 4, [hmm_ops[(ex, 5)]])
            moe_hmm(ex, 6)
            moe_hmm(ex, 7)
            moe_load(ex, 5, [hmm_ops[(ex, 7)]])
            if ex == 0:
                tap("hb0", HBH_O, [hsp_ops[(0, 7)]], n=4096)
            moe_ymm(ex, 0)
            moe_ymm(ex, 1)
            moe_load(ex, 6, [ymm_ops[(ex, 1)]])
            moe_ymm(ex, 2)
            moe_ymm(ex, 3)
            moe_load(ex, 7, [ymm_ops[(ex, 3)]])
            moe_ymm(ex, 4)
            moe_ymm(ex, 5)
            moe_ymm(ex, 6)
            moe_ymm(ex, 7)
            tap("acc_e%d" % ex, ACC0, [yev_ops[(ex, (7, 1))], yev_ops[(ex, (7, 0))]])
        moe_vlast = yev_ops[(E - 1, (7, 1))]
        moe_tlast = ymm_ops[(E - 1, 7)]
        tap("acc_moe", ACC0, [moe_vlast, yev_ops[(E - 1, (7, 0))]])

        if TRUNC == "p3":
            _finish()
            return nc

        out_dma = []
        p7_v = []
        p7_t = []
        for j in range(16):

            def f_tr7(e, j=j):
                for q in range(8):
                    if j % 2 == 0:
                        ps = PZA if q < 4 else PZB
                        dst = ps[:, (q % 4) * 128:(q % 4) * 128 + 128]
                    else:
                        dst = PF[:, q * 128:q * 128 + 128]
                    last = e.transpose(dst, acc_ap(q, j * 128, 128), IDN[:])
                return last
            tdeps = [moe_vlast]
            if j >= 2:
                tdeps.append(p7_v[j - 2])
            ttr = P.add("tensor", f_tr7, deps=tdeps)
            p7_t.append(ttr)

            def f_add(e, j=j):
                xsl = GHI_O + (j % 8) * 1024
                if j % 2 == 0:
                    e.tensor_tensor(ar(P7O_O + (j % 2) * 1024, 512), PZA[:, 0:512],
                                    ar(xsl, 512), ALU.add)
                    return e.tensor_tensor(ar(P7O_O + (j % 2) * 1024 + 512, 512),
                                           PZB[:, 0:512],
                                           ar(xsl + 512, 512),
                                           ALU.add)
                return e.tensor_tensor(ar(P7O_O + (j % 2) * 1024, 1024),
                                       PF[:, 0:1024],
                                       ar(xsl, 1024), ALU.add)
            vdeps = [ttr, x2_ld[j], moe_tlast]
            if j >= 2:
                vdeps.append(out_dma[j - 2])
            vop = P.add("vector", f_add, deps=vdeps)
            p7_v.append(vop)
            if j + 8 < 16:
                x2_ld.append(P.add("scalar", (lambda e, j=j: e.dma_start(
                    out=ar(GHI_O + (j % 8) * 1024, 1024),
                    in_=dx[(j + 8) * 128:(j + 9) * 128, :])),
                    deps=[vop], is_dma=True))
            out_dma.append(P.add("sync", (lambda e, j=j: e.dma_start(
                out=dout[j * 128:(j + 1) * 128, :],
                in_=ar(P7O_O + (j % 2) * 1024, 1024))), deps=[vop], is_dma=True))
        P.add("sync", lambda e: None, deps=[out_dma[15], out_dma[14]])

        P.emit(blk, sems)
    return nc


def _lhsT_cols(w, nk, nm):
    t = w.reshape(nk, 128, nm, 128)
    return np.ascontiguousarray(t.transpose(2, 1, 0, 3).reshape(nm, 128, nk * 128))


def _pm(a, n):
    X = a.shape[1]
    return np.ascontiguousarray(a.reshape(n, 128, X).transpose(1, 0, 2)
                                .reshape(128, n * X))


def _colpack(b):
    n = b.shape[0] // 128
    return np.ascontiguousarray(b.reshape(n, 128).T)


def kernel(L0_attn_x, L0_attn_attn_W, L0_attn_attn_b, L0_attn_freq_W,
           L0_attn_freq_b, L0_attn_out_W, L0_attn_out_b, L0_attn_proj_W,
           L0_attn_proj_b, L0_attn_identity, L0_ffn_t_emb, L0_ffn_sw1,
           L0_ffn_sb1, L0_ffn_sw3, L0_ffn_sb3, L0_ffn_sw2, L0_ffn_sb2,
           L0_ffn_w1, L0_ffn_b1, L0_ffn_w2, L0_ffn_b2):
    f32 = np.float32
    x = np.asarray(L0_attn_x, f32)
    attn_W = np.asarray(L0_attn_attn_W, f32)
    attn_b = np.asarray(L0_attn_attn_b, f32)
    freq_W = np.asarray(L0_attn_freq_W, f32)
    freq_b = np.asarray(L0_attn_freq_b, f32)
    out_W = np.asarray(L0_attn_out_W, f32)
    out_b = np.asarray(L0_attn_out_b, f32)
    proj_W = np.asarray(L0_attn_proj_W, f32)
    proj_b = np.asarray(L0_attn_proj_b, f32)
    ident_v = np.asarray(L0_attn_identity, f32)
    sw1 = np.asarray(L0_ffn_sw1, f32)
    sb1 = np.asarray(L0_ffn_sb1, f32)
    sw3 = np.asarray(L0_ffn_sw3, f32)
    sb3 = np.asarray(L0_ffn_sb3, f32)
    sw2 = np.asarray(L0_ffn_sw2, f32)
    sb2 = np.asarray(L0_ffn_sb2, f32)
    w1 = np.asarray(L0_ffn_w1, f32)
    b1 = np.asarray(L0_ffn_b1, f32)
    w2 = np.asarray(L0_ffn_w2, f32)
    b2 = np.asarray(L0_ffn_b2, f32)

    Wq = attn_W[:, :C]
    Wv = attn_W[:, C:]
    bq = attn_b[:C]
    bv = attn_b[C:]
    Wfy = freq_W[:C, :]
    Wfx = freq_W[C:, :]

    shared = {
        "wv": _lhsT_cols(Wv, 8, 8),
        "wqT": _lhsT_cols(np.ascontiguousarray(Wq.T), 8, 8),
        "wfy": _pm(Wfy, 8),
        "wfx": _pm(Wfx, 8),
        "wfy2": _pm(np.concatenate([Wfy.reshape(8, 128, 64)] * 2, axis=2)
                    .reshape(1024, 128), 8),
        "wfx2": _pm(np.concatenate([Wfx.reshape(8, 128, 64)] * 2, axis=2)
                    .reshape(1024, 128), 8),
        "outw": np.ascontiguousarray(out_W),
        "outwT": _pm(np.ascontiguousarray(out_W.T), 8),
        "projw": _lhsT_cols(proj_W, 8, 8),
        "sw1": _lhsT_cols(sw1, 8, 16),
        "sw3": _lhsT_cols(sw3, 8, 16),
        "sw2": _lhsT_cols(sw2, 16, 8),
        "w1": np.stack([_lhsT_cols(w1[e], 8, 8).transpose(1, 0, 2)
                        .reshape(128, 8192) for e in range(E)]),
        "w2": np.stack([_lhsT_cols(w2[e], 8, 8).transpose(1, 0, 2)
                        .reshape(128, 8192) for e in range(E)]),
        "ident": np.eye(128, dtype=f32),
        "ones1": np.ones((1, 128), dtype=f32),
        "ones1b": np.ones((1, 128), dtype=ml_dtypes.bfloat16),
    }
    bpk = np.zeros((128, NBIAS), f32)
    bpk[:, BC["bv"]:BC["bv"] + 8] = _colpack(bv)
    bpk[:, BC["projb"]:BC["projb"] + 8] = _colpack(proj_b)
    bpk[:, BC["sb1"]:BC["sb1"] + 16] = _colpack(sb1)
    bpk[:, BC["sb3"]:BC["sb3"] + 16] = _colpack(sb3)
    bpk[:, BC["sb2c"]:BC["sb2c"] + 8] = _colpack(sb2)
    bpk[:, BC["b1"]:BC["b1"] + 64] = np.concatenate(
        [_colpack(b1[e]) for e in range(E)], axis=1)
    bpk[:, BC["b2q"]:BC["b2q"] + 64] = np.concatenate(
        [_colpack(b2[e]) for e in range(E)], axis=1)
    bpk[:, BC["ob"]:BC["ob"] + 8] = _colpack(out_b)
    bpk[:, BC["bq"]:BC["bq"] + 8] = _colpack(bq)
    bpk[:, BC["id"]:BC["id"] + 8] = _colpack(ident_v)
    bpk[:, BC["obc"]:BC["obc"] + 8] = _colpack(out_b)
    bpk[:, BC["fb2"]] = np.concatenate([freq_b, freq_b])
    bpk[:, BC["pihalf"]] = np.concatenate([np.zeros(64, f32),
                                           np.full(64, PI / 2, f32)])
    bpk[:, BC["twopi"]] = 2 * PI
    bpk[:, BC["invC"]] = 1.0 / C
    bpk[:, BC["epsln"]] = EPS_LN
    shared["biaspack"] = bpk

    nc = build_nc()
    in_maps = []
    for b in range(8):
        m = dict(shared)
        m["x"] = np.ascontiguousarray(x[b])
        in_maps.append(m)
    res = run_bass_kernel_spmd(nc, in_maps, core_ids=list(range(8)))
    globals()["_last_results"] = res.results
    out = np.stack([res.results[b]["out"] for b in range(8)])
    return out.astype(np.float32)


# version 33
# speedup vs baseline: 1.0892x; 1.0892x over previous
import contextlib
import numpy as np
import ml_dtypes
import concourse.bass as bass
import concourse.mybir as mybir
from concourse.bass_utils import run_bass_kernel_spmd

F32 = mybir.dt.float32
I32 = mybir.dt.int32
AF = mybir.ActivationFunctionType
ALU = mybir.AluOpType
AX = mybir.AxisListType

C = 1024
T = 2048
E = 8
EPS_LN = 1e-5
EPS_MM = 1e-6
PI = float(np.pi)
INV2PI = float(1.0 / (2.0 * np.pi))

DEBUG_TAPS = []
TRUNC = None

XN0 = 0
ACC0 = 16384
V0 = 32768
W0 = 49152
ARENA_N = 53248
SC_O = XN0 + 0
SSC_O = XN0 + 2048
MT_O = XN0 + 2048
ST_O = XN0 + 2064
SRS_O = XN0 + 2080
S11_O = XN0 + 4096
U_O = XN0 + 8192
A0_O = XN0 + 10240
A1_O = XN0 + 11264
M0_O = XN0 + 12288
M1_O = XN0 + 13312
SROW_O = XN0 + 14336
SSCH_O = XN0 + 4096
SSCL_O = XN0 + 5120
SROWH_O = XN0 + 6144
SROWL_O = XN0 + 7168
GHI_O = XN0 + 8192
GLO_O = XN0 + 12288
H1S_O = XN0 + 4096
GT_O = XN0 + 6656
STG3_O = XN0 + 4608
W2T_O = XN0 + 6656
HTMP_O = XN0 + 7680
HBH_O = XN0 + 0
HBL_O = XN0 + 2048
P7X_O = XN0 + 0
P7O_O = XN0 + 2048
CW2_O = ACC0 + 0
P1_O = ACC0 + 2048
PSP_O = ACC0 + 2048
XSP_O = ACC0 + 4096
OSP_O = ACC0 + 6144
OS2_O = ACC0 + 8192
WFY_O = V0 + 0
WFX_O = V0 + 512
WFY2_O = V0 + 1024
WFX2_O = V0 + 2048
OWT_O = V0 + 3072

_BIAS_ORDER = [("bv", 8), ("projb", 8), ("sb1", 16), ("sb3", 16), ("sb2c", 8),
               ("b1", 64), ("b2q", 64), ("ob", 8), ("bq", 8), ("id", 8),
               ("obc", 8), ("fb2", 1), ("pihalf", 1), ("twopi", 1),
               ("invC", 1), ("epsln", 1)]
BC = {}
_c = 0
for _n, _k in _BIAS_ORDER:
    BC[_n] = _c
    _c += _k
NBIAS = _c

RA = [(3 * e) % 8 for e in range(8)]
RB = [(3 * e + 3) % 8 for e in range(8)]


class Op:
    __slots__ = ("eng", "fn", "deps", "chan", "val", "needed", "idx")


class Prog:
    def __init__(self):
        self.ops = []

    def add(self, eng, fn, deps=(), is_dma=False):
        op = Op()
        op.eng = eng
        op.fn = fn
        op.deps = [d for d in deps if d is not None]
        if is_dma:
            op.chan = {"sync": "d", "gpsimd": "g", "scalar": "s2"}[eng]
        else:
            op.chan = {"tensor": "t", "vector": "v", "scalar": "s",
                       "gpsimd": "p", "sync": "y"}[eng]
        op.val = None
        op.needed = is_dma
        op.idx = len(self.ops)
        self.ops.append(op)
        return op

    def emit(self, blk, sems):
        dma_ch = ("d", "g", "s2")
        for op in self.ops:
            for dp in op.deps:
                assert dp.idx < op.idx
                if dp.eng != op.eng or dp.chan in dma_ch:
                    dp.needed = True
        counters = {}
        for op in self.ops:
            if op.needed:
                step = 16 if op.chan in dma_ch else 1
                counters[op.chan] = counters.get(op.chan, 0) + step
                op.val = counters[op.chan]
        watermark = {}
        secmap = {"sync": blk.sync, "tensor": blk.tensor, "vector": blk.vector,
                  "scalar": blk.scalar, "gpsimd": blk.gpsimd}
        by_eng = {}
        for op in self.ops:
            by_eng.setdefault(op.eng, []).append(op)
        for eng, run in by_eng.items():
            def section(e, run=run, eng=eng):
                for op in run:
                    for dp in op.deps:
                        if dp.eng == op.eng and dp.chan not in dma_ch:
                            continue
                        key = (eng, dp.chan)
                        if watermark.get(key, 0) >= dp.val:
                            continue
                        e.wait_ge(sems[dp.chan], dp.val)
                        watermark[key] = dp.val
                    last = op.fn(e)
                    if eng == "vector" and op.chan == "v":
                        last = e.drain()
                    if op.needed:
                        if last is None:
                            last = e.nop()
                        last.then_inc(sems[op.chan], 16 if op.chan in dma_ch else 1)
            secmap[eng](section)
        return counters


def build_nc():
    nc = bass.Bass("TRN2", target_bir_lowering=False,
                   dynamic_dma_scratch_size=256)

    dx = nc.dram_tensor("x", [T, C], F32, kind="ExternalInput").ap()
    dout = nc.dram_tensor("out", [T, C], F32, kind="ExternalOutput").ap()
    dwv = nc.dram_tensor("wv", [8, 128, 1024], F32, kind="ExternalInput").ap()
    dwqT = nc.dram_tensor("wqT", [8, 128, 1024], F32, kind="ExternalInput").ap()
    dwfy = nc.dram_tensor("wfy", [128, 512], F32, kind="ExternalInput").ap()
    dwfx = nc.dram_tensor("wfx", [128, 512], F32, kind="ExternalInput").ap()
    dwfy2 = nc.dram_tensor("wfy2", [128, 1024], F32, kind="ExternalInput").ap()
    dwfx2 = nc.dram_tensor("wfx2", [128, 1024], F32, kind="ExternalInput").ap()
    doutw = nc.dram_tensor("outw", [128, 1024], F32, kind="ExternalInput").ap()
    doutwT = nc.dram_tensor("outwT", [128, 1024], F32, kind="ExternalInput").ap()
    dprojw = nc.dram_tensor("projw", [8, 128, 1024], F32, kind="ExternalInput").ap()
    dsw1 = nc.dram_tensor("sw1", [16, 128, 1024], F32, kind="ExternalInput").ap()
    dsw3 = nc.dram_tensor("sw3", [16, 128, 1024], F32, kind="ExternalInput").ap()
    dsw2 = nc.dram_tensor("sw2", [8, 128, 2048], F32, kind="ExternalInput").ap()
    dw1 = nc.dram_tensor("w1", [8, 128, 8192], F32, kind="ExternalInput").ap()
    dw2 = nc.dram_tensor("w2", [8, 128, 8192], F32, kind="ExternalInput").ap()
    dbias = nc.dram_tensor("biaspack", [128, NBIAS], F32, kind="ExternalInput").ap()
    dident = nc.dram_tensor("ident", [128, 128], F32, kind="ExternalInput").ap()
    dones = nc.dram_tensor("ones1", [1, 128], F32, kind="ExternalInput").ap()
    donesb = nc.dram_tensor("ones1b", [1, 128], mybir.dt.bfloat16,
                            kind="ExternalInput").ap()
    dxnsp = nc.dram_tensor("xnsp", [128, 16384], F32, kind="Internal").ap()
    dswsp = nc.dram_tensor("swsp", [128, 49152], F32, kind="Internal").ap()
    dtaps = {}
    for tp in DEBUG_TAPS:
        dtaps[tp] = nc.dram_tensor("tap_" + tp, [128, 16384], F32,
                                   kind="ExternalOutput").ap()

    st = contextlib.ExitStack()
    with st:
        AR = st.enter_context(nc.sbuf_tensor("AR", [128, ARENA_N], F32))
        ART = AR
        KI = st.enter_context(nc.sbuf_tensor("KI", [128, 1024], I32))
        BP = st.enter_context(nc.sbuf_tensor("BP", [128, NBIAS], F32))
        IDN = st.enter_context(nc.sbuf_tensor("IDN", [128, 128], F32))
        ONER = st.enter_context(nc.sbuf_tensor("ONER", [1, 128], F32))
        ONEB = st.enter_context(nc.sbuf_tensor("ONEB", [1, 128],
                                               mybir.dt.bfloat16))
        OUTW = st.enter_context(nc.sbuf_tensor("OUTW", [128, 1024], F32))
        OW2Y = st.enter_context(nc.sbuf_tensor("OW2Y", [128, 128], F32))
        OW2X = st.enter_context(nc.sbuf_tensor("OW2X", [128, 128], F32))
        CBY2 = st.enter_context(nc.sbuf_tensor("CBY2", [1, 128], F32))
        CBX2 = st.enter_context(nc.sbuf_tensor("CBX2", [1, 128], F32))
        SM = st.enter_context(nc.sbuf_tensor("SM", [128, 16], F32))
        PF = st.enter_context(nc.psum_tensor("PF", [128, 2048], F32))
        PZA = st.enter_context(nc.psum_tensor("PZA", [128, 1024], F32))
        PZB = st.enter_context(nc.psum_tensor("PZB", [128, 1024], F32))
        sems = {}
        for ch in ["d", "t", "v", "s", "g", "p", "y", "s2"]:
            sems[ch] = st.enter_context(nc.semaphore("sem_" + ch))
        blk = st.enter_context(nc.Block())

        P = Prog()

        def _finish():
            P.emit(blk, sems)


        def ar(off, n, p0=0, p1=128):
            return AR[p0:p1, off:off + n]

        BF16 = mybir.dt.bfloat16

        def bf(off, n_bf16):
            return AR[:, off:off + n_bf16 // 2].bitcast(BF16)

        def bp(name, i=0):
            c0 = BC[name] + i
            return BP[:, c0:c0 + 1]

        def xn_ap(ch, t0, tn):
            return ar(XN0 + ch * 2048 + t0, tn)

        def xn2_ap(ch, t0, tn):
            return ar(V0 + ch * 2048 + t0, tn)

        def v_ap(ch, t0, tn):
            return ar(V0 + ch * 2048 + t0, tn)

        def acc_ap(ch, t0, tn):
            return ar(ACC0 + ch * 2048 + t0, tn)

        bank8 = [lambda: PF[:, 0:512], lambda: PF[:, 512:1024],
                 lambda: PF[:, 1024:1536], lambda: PF[:, 1536:2048],
                 lambda: PZA[:, 0:512], lambda: PZA[:, 512:1024],
                 lambda: PZB[:, 0:512], lambda: PZB[:, 512:1024]]

        def tap(name, off, dep_ops, n=16384):
            if name in DEBUG_TAPS:
                return P.add("sync", (lambda e, name=name, off=off, n=n: e.dma_start(
                    out=dtaps[name][:, 0:n], in_=ar(off, n))), deps=dep_ops,
                    is_dma=True)
            return None

        ld = {}
        for nm, dst, src in [
                ("bias", BP[:], dbias), ("ident", IDN[:], dident),
                ("ones", ONER[:], dones), ("onesb", ONEB[:], donesb),
                ("outw", OUTW[:], doutw),
                ("wfy", ar(WFY_O, 512), dwfy), ("wfx", ar(WFX_O, 512), dwfx),
                ("wfy2", ar(WFY2_O, 1024), dwfy2), ("wfx2", ar(WFX2_O, 1024), dwfx2),
                ("outwT", ar(OWT_O, 1024), doutwT)]:
            ld[nm] = P.add("sync", (lambda e, dst=dst, src=src:
                                    e.dma_start(out=dst, in_=src)), is_dma=True)

        wq_ld = []
        cw_ev = []
        for cch in range(8):
            slot = W0 + (cch % 2) * 1024
            deps = [cw_ev[cch - 2]] if cch >= 2 else []
            wq_ld.append(P.add("sync", (lambda e, s=slot, c=cch: e.dma_start(
                out=ar(s, 1024), in_=dwqT[c])), deps=deps, is_dma=True))

            def cwmm(e, s=slot):
                for d in range(8):
                    e.matmul(PZA[:, 0:64], ar(s + d * 128, 128),
                             ar(WFY_O + d * 64, 64), start=(d == 0), stop=False)
                for d in range(8):
                    last = e.matmul(PZA[:, 64:128], ar(s + d * 128, 128),
                                    ar(WFX_O + d * 64, 64), start=(d == 0), stop=(d == 7))
                return last
            deps = [wq_ld[cch], ld["wfy"], ld["wfx"]]
            if cch >= 1:
                deps.append(cw_ev[cch - 1])
            mm = P.add("tensor", cwmm, deps=deps)

            def cwev(e, c=cch):
                e.copy(ar(CW2_O + c * 128, 64), PZA[:, 0:64])
                e.copy(ar(CW2_O + c * 128 + 64, 64), PZA[:, 0:64])
                e.copy(ar(CW2_O + 1024 + c * 128, 64), PZA[:, 64:128])
                return e.copy(ar(CW2_O + 1024 + c * 128 + 64, 64), PZA[:, 64:128])
            cw_ev.append(P.add("scalar", cwev, deps=[mm]))

        def ow_mm(e):
            for cch in range(8):
                e.matmul(PZB[:, 0:64], ar(OWT_O + cch * 128, 128),
                         ar(WFY_O + cch * 64, 64), start=(cch == 0), stop=False)
            for cch in range(8):
                last = e.matmul(PZB[:, 64:128], ar(OWT_O + cch * 128, 128),
                                ar(WFX_O + cch * 64, 64), start=(cch == 0), stop=(cch == 7))
            return last
        ow_op = P.add("tensor", ow_mm, deps=[ld["outwT"], ld["wfy"], ld["wfx"]])

        def ow_ev(e):
            for half, srccols in ((0, PZB[:, 0:64]), (1, PZB[:, 0:64])):
                e.copy(OW2Y[:, 0:64].bitcast(BF16)[:, half * 64:half * 64 + 64],
                       srccols)
            for half, srccols in ((0, PZB[:, 64:128]), (1, PZB[:, 64:128])):
                last = e.copy(OW2X[:, 0:64].bitcast(BF16)[:, half * 64:half * 64 + 64],
                              srccols)
            return last
        ow_ev_op = P.add("scalar", ow_ev, deps=[ow_op])

        def ow_lo(e):
            for half in range(2):
                e.tensor_tensor(OW2Y[:, 64:128].bitcast(BF16)[:, half * 64:half * 64 + 64],
                                PZB[:, 0:64],
                                OW2Y[:, 0:64].bitcast(BF16)[:, half * 64:half * 64 + 64],
                                ALU.subtract)
                e.drain()
                last = e.tensor_tensor(OW2X[:, 64:128].bitcast(BF16)[:, half * 64:half * 64 + 64],
                                       PZB[:, 64:128],
                                       OW2X[:, 0:64].bitcast(BF16)[:, half * 64:half * 64 + 64],
                                       ALU.subtract)
                if half == 0:
                    e.drain()
            return last
        ow_lo_op = P.add("vector", ow_lo, deps=[ow_ev_op])

        def cb_mm(e):
            for cch in range(8):
                e.matmul(PZB[0:1, 128:256], BP[:, BC["obc"] + cch:BC["obc"] + cch + 1],
                         ar(WFY2_O + cch * 128, 128), start=(cch == 0), stop=False)
            for cch in range(8):
                e.matmul(PZB[0:1, 256:384], BP[:, BC["obc"] + cch:BC["obc"] + cch + 1],
                         ar(WFX2_O + cch * 128, 128), start=(cch == 0), stop=False)
            for cch in range(8):
                e.matmul(PZB[:, 384:385], ar(WFY2_O + cch * 128, 128),
                         BP[:, BC["id"] + cch:BC["id"] + cch + 1],
                         start=(cch == 0), stop=False)
            for cch in range(8):
                e.matmul(PZB[:, 385:386], ar(WFY2_O + cch * 128, 128),
                         BP[:, BC["bq"] + cch:BC["bq"] + cch + 1],
                         start=(cch == 0), stop=False)
            for cch in range(8):
                last = e.matmul(PZB[:, 386:387], ar(WFX2_O + cch * 128, 128),
                                BP[:, BC["bq"] + cch:BC["bq"] + cch + 1],
                                start=(cch == 0), stop=(cch == 7))
            return last
        cb_op = P.add("tensor", cb_mm,
                      deps=[ld["bias"], ld["wfy2"], ld["wfx2"], ow_ev_op,
                            ow_lo_op])

        def cb_ev(e):
            e.copy(CBY2[:, 0:64].bitcast(BF16), PZB[0:1, 128:256])
            e.copy(CBX2[:, 0:64].bitcast(BF16), PZB[0:1, 256:384])
            e.copy(SM[:, 2:3], PZB[:, 384:385])
            e.copy(SM[:, 4:5], PZB[:, 385:386])
            return e.copy(SM[:, 5:6], PZB[:, 386:387])
        cb_ev_op = P.add("scalar", cb_ev, deps=[cb_op])

        def cb_lo(e):
            e.tensor_tensor(CBY2[:, 64:128].bitcast(BF16), PZB[0:1, 128:256],
                            CBY2[:, 0:64].bitcast(BF16), ALU.subtract)
            e.drain()
            return e.tensor_tensor(CBX2[:, 64:128].bitcast(BF16),
                                   PZB[0:1, 256:384],
                                   CBX2[:, 0:64].bitcast(BF16), ALU.subtract)
        cb_lo_op = P.add("vector", cb_lo, deps=[cb_ev_op])

        def cc_build(e):
            e.tensor_tensor(SM[:, 0:1], bp("fb2"), bp("pihalf"), ALU.add)
            e.drain()
            e.tensor_tensor(SM[:, 1:2], SM[:, 0:1], SM[:, 4:5], ALU.add)
            e.drain()
            e.tensor_tensor(SM[:, 1:2], SM[:, 1:2], SM[:, 5:6], ALU.add)
            e.drain()
            e.tensor_tensor(SM[:, 3:4], SM[:, 2:3], SM[:, 4:5], ALU.subtract)
            e.scalar_tensor_tensor(BP[:, BC["sb2c"]:BC["sb2c"] + 8],
                                   BP[:, BC["sb2c"]:BC["sb2c"] + 8], 0.5,
                                   BP[:, BC["projb"]:BC["projb"] + 8],
                                   ALU.mult, ALU.add)
            return e.tensor_scalar(BP[:, BC["b2q"]:BC["b2q"] + 64],
                                   BP[:, BC["b2q"]:BC["b2q"] + 64],
                                   0.25, None, ALU.mult)
        cc_op = P.add("vector", cc_build, deps=[cb_ev_op, cb_lo_op, ld["bias"]])

        if TRUNC == "p0":
            _finish()
            return nc

        XT_O = [P1_O, P1_O + 1024]
        XCB_O = [P1_O + 2048, P1_O + 3072]
        XNT_O = [P1_O + 4096, P1_O + 5120]
        TRH_O = P1_O + 6144
        MC = [8, 9]
        VC = [10, 11]
        x_ld, ln_vst, ln_vn, ln_ttr, ln_ev = [], [], [], [], []

        def emit_xld(j, deps):
            x_ld.append(P.add("sync", (lambda e, j=j: e.dma_start(
                out=ar(XT_O[j % 2], 1024), in_=dx[j * 128:(j + 1) * 128, :])),
                deps=deps, is_dma=True))
        emit_xld(0, [])
        emit_xld(1, [])
        for j in range(16):
            xt_o, xc_o, xnt_o = XT_O[j % 2], XCB_O[j % 2], XNT_O[j % 2]
            mcol, vcol = MC[j % 2], VC[j % 2]

            def v_stats(e, xt_o=xt_o, xc_o=xc_o, mcol=mcol):
                e.tensor_reduce(SM[:, mcol:mcol + 1], ar(xt_o, 1024), AX.X, ALU.add)
                e.drain()
                e.tensor_scalar(SM[:, mcol:mcol + 1], SM[:, mcol:mcol + 1],
                                1.0 / C, None, ALU.mult)
                e.drain()
                return e.tensor_scalar(ar(xc_o, 1024), ar(xt_o, 1024),
                                       SM[:, mcol:mcol + 1], None, ALU.subtract)
            deps = [x_ld[j], cc_op]
            if j >= 2:
                deps += [ln_vn[j - 2]]
            vst = P.add("vector", v_stats, deps=deps)
            ln_vst.append(vst)
            if j + 2 < 16:
                emit_xld(j + 2, [vst])

            def s_sq(e, xc_o=xc_o, vcol=vcol):
                e.activation(ar(TRH_O, 1024), ar(xc_o, 1024), AF.Square,
                             accum_out=SM[:, vcol:vcol + 1])
                return e.activation(SM[:, vcol:vcol + 1], SM[:, vcol:vcol + 1],
                                    AF.Sqrt, bias=bp("epsln"), scale=bp("invC"))
            sq = P.add("scalar", s_sq, deps=[vst])

            def v_norm(e, xc_o=xc_o, xnt_o=xnt_o, vcol=vcol):
                e.reciprocal(SM[:, vcol:vcol + 1], SM[:, vcol:vcol + 1])
                e.drain()
                return e.tensor_scalar(ar(xnt_o, 1024), ar(xc_o, 1024),
                                       SM[:, vcol:vcol + 1], None, ALU.mult)
            vdeps = [sq]
            if j >= 2:
                vdeps.append(ln_ttr[j - 2])
            vn = P.add("vector", v_norm, deps=vdeps)
            ln_vn.append(vn)

            def t_tr(e, xnt_o=xnt_o, j=j):
                for q in range(8):
                    if j % 2 == 0:
                        ps = PZA if q < 4 else PZB
                        dst = ps[:, (q % 4) * 128:(q % 4) * 128 + 128]
                    else:
                        dst = PF[:, q * 128:q * 128 + 128]
                    last = e.transpose(dst, ar(xnt_o + q * 128, 128), IDN[:])
                return last
            tdeps = [vn, ld["ident"]]
            if j >= 2:
                tdeps.append(ln_ev[j - 2])
            if j <= 1:
                tdeps += [cw_ev[7], cb_ev_op, ow_ev_op, cb_lo_op]
            ttr = P.add("tensor", t_tr, deps=tdeps)
            ln_ttr.append(ttr)

            def s_ev(e, j=j):
                for h in range(2):
                    if j % 2 == 0:
                        src = (PZA if h == 0 else PZB)[:, 0:512] \
                            .rearrange("p (a b) -> p a b", b=128)
                    else:
                        src = PF[:, h * 512:h * 512 + 512] \
                            .rearrange("p (a b) -> p a b", b=128)
                    dst = bass.AP(ART, XN0 + h * 8192 + j * 128,
                                  [[ARENA_N, 128], [2048, 4], [1, 128]])
                    last = e.copy(dst, src)
                return last
            ln_ev.append(P.add("scalar", s_ev, deps=[ttr]))
        p1_last = ln_ev[15]
        tap_xn = tap("xn", XN0, [p1_last, ln_ev[14]])

        if TRUNC == "p1":
            _finish()
            return nc

        wv_ld = []
        vmm = {}
        v_ev = {}
        for m in range(8):
            deps = [cw_ev[7]]
            if m >= 2:
                deps.append(vmm[(m - 2, 3)])
            wv_ld.append(P.add("sync", (lambda e, m=m: e.dma_start(
                out=ar(W0 + (m % 2) * 1024, 1024), in_=dwv[m])),
                deps=deps, is_dma=True))
            for q in range(4):
                bk = bank8[4 + q]

                def f_mm(e, m=m, q=q, bk=bk):
                    for k in range(8):
                        last = e.matmul(bk(), ar(W0 + (m % 2) * 1024 + k * 128, 128),
                                        xn_ap(k, q * 512, 512),
                                        start=(k == 0), stop=(k == 7))
                    return last
                deps = [wv_ld[m], p1_last, ln_ev[14]]
                if (m - 1, q) in v_ev:
                    deps.append(v_ev[(m - 1, q)])
                vmm[(m, q)] = P.add("tensor", f_mm, deps=deps)

                def f_ev(e, m=m, q=q, bk=bk):
                    return e.activation(v_ap(m, q * 512, 512), bk(), AF.Identity,
                                        bias=bp("bv", m))
                dps = [vmm[(m, q)]]
                if m <= 1:
                    dps.append(cc_op)
                v_ev[(m, q)] = P.add("scalar", f_ev, deps=dps)
        tap("v", V0, [v_ev[(7, 3)], v_ev[(6, 3)]])

        q0_mm = []
        for b in range(4):
            def f_q0(e, b=b):
                lo, hi = b * 512, (b + 1) * 512
                for d in range(8):
                    e.matmul(PF[:, lo:hi], ar(CW2_O + 1024 + d * 128, 128),
                             xn_ap(d, lo, 512), start=(d == 0), stop=False)
                ylo = max(1, lo)
                for d in range(8):
                    last = e.matmul(PF[:, ylo:hi], ar(CW2_O + d * 128, 128),
                                    xn_ap(d, ylo - 1, hi - ylo),
                                    start=False, stop=(d == 7))
                return last
            q0_mm.append(P.add("tensor", f_q0,
                               deps=[p1_last, ln_ev[14], cw_ev[7]]))

        presplit_done = {}
        reload_gate = [None]
        ps_pieces = []
        for m in range(16):
            ps_pieces.append(("sw1", m, dsw1[m], m * 1024, 1024, False))
            ps_pieces.append(("sw3", m, dsw3[m], 16384 + m * 1024, 1024, False))
        for c_ in range(8):
            ps_pieces.append(("sw2", c_, dsw2[c_][:, 0:1024],
                              32768 + c_ * 2048, 1024, True))
            ps_pieces.append(("sw2b", c_, dsw2[c_][:, 1024:2048],
                              32768 + c_ * 2048 + 512, 1024, True))
        ps_ops = []
        ps_dmas = []
        ps_queue = list(enumerate(ps_pieces))
        ps_pos = [0]

        def emit_presplit(nmax, slot_guard=()):
            done = 0
            while ps_pos[0] < len(ps_queue) and done < nmax:
                pi, (nm_, idx_, src_, dsto_, n_, half_) = ps_queue[ps_pos[0]]
                ps_pos[0] += 1
                done += 1
                stg = None if (pi % 2 == 0) else 0
                if pi < 32:
                    m_ = pi // 2
                    slot = W0 + (m_ % 2) * 2048
                    ohi = slot + (pi % 2) * 1024
                    olo = ohi + 512
                else:
                    c_ = (pi - 32) // 2
                    slot = W0 + (c_ % 2) * 2048
                    ohi = slot + (pi % 2) * 512
                    olo = ohi + 1024

                def stg_ap(stg):
                    return KI[:].bitcast(F32) if stg is None else ar(stg, 1024)
                din_deps = [ln_vst[15], reload_gate[0]]
                if pi >= 2:
                    din_deps.append(ps_ops[pi - 2])
                din = P.add("scalar", (lambda e, stg=stg, src_=src_:
                                       e.dma_start(out=stg_ap(stg), in_=src_)),
                            deps=din_deps, is_dma=True)

                def f_ps_hi(e, stg=stg, ohi=ohi):
                    return e.copy(bf(ohi, 1024), stg_ap(stg))
                hi_op = P.add("scalar", f_ps_hi, deps=[din] + list(slot_guard))

                def f_ps(e, stg=stg, ohi=ohi, olo=olo):
                    return e.tensor_tensor(bf(olo, 1024), stg_ap(stg),
                                           bf(ohi, 1024), ALU.subtract)
                vop = P.add("gpsimd", f_ps, deps=[din, hi_op])
                ps_ops.append(vop)
                if nm_ in ("sw2", "sw2b"):
                    P.add("sync", (lambda e, ohi=ohi, dsto_=dsto_:
                                   e.dma_start(out=dswsp[:, dsto_:dsto_ + 512],
                                               in_=ar(ohi, 512))),
                          deps=[vop], is_dma=True)
                    dout_ = P.add("sync", (lambda e, olo=olo, dsto_=dsto_:
                                  e.dma_start(out=dswsp[:, dsto_ + 1024:dsto_ + 1536],
                                              in_=ar(olo, 512))),
                                  deps=[vop], is_dma=True)
                else:
                    dout_ = P.add("sync", (lambda e, ohi=ohi, dsto_=dsto_:
                                  e.dma_start(out=dswsp[:, dsto_:dsto_ + 1024],
                                              in_=ar(ohi, 1024))),
                                  deps=[vop], is_dma=True)
                ps_dmas.append(dout_)
                key = ("sw2", idx_) if nm_ in ("sw2", "sw2b") else (nm_, idx_)
                presplit_done[key] = dout_


        xs_ops = []
        xs_dmas = []
        for kk in range(8):
            def f_xs(e, kk=kk):
                hi = bf(XSP_O, 2048).rearrange("p (r g) -> p g r", g=256)
                lo = bf(XSP_O + 1024, 2048).rearrange("p (r g) -> p g r", g=256)
                xsrc = ar(XN0 + kk * 2048, 2048).rearrange("p (g r) -> p g r", r=8)
                e.tensor_copy(hi, xsrc)
                e.drain()
                return e.scalar_tensor_tensor(lo, hi, -1.0, xsrc,
                                              ALU.mult, ALU.add)
            vdeps = [p1_last, ln_ev[14]]
            if kk >= 1:
                vdeps.append(xs_dmas[-1])
            vop = P.add("vector", f_xs, deps=vdeps)
            xs_ops.append(vop)
            P.add("sync", (lambda e, kk=kk: e.dma_start(
                out=dxnsp[:, kk * 1024:kk * 1024 + 1024],
                in_=ar(XSP_O, 1024))), deps=[vop, tap_xn], is_dma=True)
            xs_dmas.append(P.add("sync", (lambda e, kk=kk: e.dma_start(
                out=dxnsp[:, 8192 + kk * 1024:8192 + kk * 1024 + 1024],
                in_=ar(XSP_O + 1024, 1024))), deps=[vop, tap_xn], is_dma=True))
        spill = P.add("sync", lambda e: None,
                      deps=[xs_dmas[-1], q0_mm[3]], is_dma=False)

        if TRUNC == "p4":
            _finish()
            return nc

        state = {"ssch": None, "sscl": None, "srh": None, "srl": None,
                 "pf_free": None, "sc": None,
                 "sc_cons": [], "s11": None, "ystt": []}

        def emit_iter(k):
            n = 1 << (k - 1)
            first = (k == 1)
            last = (k == 11)
            if first:
                fmm = q0_mm
                hb = SM[:, 3:4]
                cc = SM[:, 1:2]
            else:
                hb = SM[:, 2:3]
                cc = SM[:, 0:1]
                fmm = []
                SSCH = bf(SSCH_O, 2048)
                SSCL = bf(SSCL_O, 2048)
                SRH = AR[0:1, SROWH_O:SROWH_O + 1024].bitcast(BF16)
                SRL = AR[0:1, SROWL_O:SROWL_O + 1024].bitcast(BF16)
                OWXH = OW2X[:, 0:64].bitcast(BF16)
                OWXL = OW2X[:, 64:128].bitcast(BF16)
                OWYH = OW2Y[:, 0:64].bitcast(BF16)
                OWYL = OW2Y[:, 64:128].bitcast(BF16)
                CBXH = CBX2[:, 0:64].bitcast(BF16)
                CBXL = CBX2[:, 64:128].bitcast(BF16)
                CBYH = CBY2[:, 0:64].bitcast(BF16)
                CBYL = CBY2[:, 64:128].bitcast(BF16)
                fmm = []
                fh_prev = []
                for bp_ in range(2):
                    def f_ab_h(e, n=n, bp_=bp_):
                        last = None
                        for b in (2 * bp_, 2 * bp_ + 1):
                            lo, hi = b * 512, (b + 1) * 512
                            ylo = max(n, lo)
                            e.matmul(PF[:, lo:hi], OWXH, SSCH[:, lo:hi],
                                     start=True, stop=False)
                            last = e.matmul(PF[:, lo:hi], CBXH, SRH[:, lo:hi],
                                            start=False, stop=False)
                            if ylo < hi:
                                e.matmul(PF[:, ylo:hi], OWYH,
                                         SSCH[:, ylo - n:hi - n],
                                         start=False, stop=False)
                                last = e.matmul(PF[:, ylo:hi], CBYH,
                                                SRH[:, ylo - n:hi - n],
                                                start=False, stop=False)
                        return last
                    hdeps = [state["ssch"][bp_], state["srh"]] \
                        + state["pf_free"] + fh_prev
                    if bp_ == 1:
                        hdeps.append(state["ssch"][0])
                    fh = P.add("tensor", f_ab_h, deps=hdeps)
                    fh_prev = [fh]
                    fmm.append(fh)

                def f_ab_l(e, n=n):
                    last = None
                    for b in range(4):
                        lo, hi = b * 512, (b + 1) * 512
                        ylo = max(n, lo)
                        e.matmul(PF[:, lo:hi], OWXH, SSCL[:, lo:hi],
                                 start=False, stop=False)
                        e.matmul(PF[:, lo:hi], OWXL, SSCH[:, lo:hi],
                                 start=False, stop=False)
                        e.matmul(PF[:, lo:hi], CBXH, SRL[:, lo:hi],
                                 start=False, stop=False)
                        last = e.matmul(PF[:, lo:hi], CBXL, SRH[:, lo:hi],
                                        start=False, stop=(ylo >= hi))
                        if ylo < hi:
                            e.matmul(PF[:, ylo:hi], OWYH,
                                     SSCL[:, ylo - n:hi - n],
                                     start=False, stop=False)
                            e.matmul(PF[:, ylo:hi], OWYL,
                                     SSCH[:, ylo - n:hi - n],
                                     start=False, stop=False)
                            e.matmul(PF[:, ylo:hi], CBYH,
                                     SRL[:, ylo - n:hi - n],
                                     start=False, stop=False)
                            last = e.matmul(PF[:, ylo:hi], CBYL,
                                            SRH[:, ylo - n:hi - n],
                                            start=False, stop=True)
                    return last
                fl = P.add("tensor", f_ab_l,
                           deps=fmm + state["sscl"] + [state["srl"]])
                fmm = fmm + [fl]
            def f_head(e, n=n, hb=hb):
                return e.activation(PF[:, 0:n], PF[:, 0:n], AF.Identity, bias=hb)
            head = P.add("scalar", f_head, deps=list(fmm))

            vr_ops = []
            for h in range(2):
                def f_u(e, h=h, cc=cc):
                    return e.tensor_scalar(ar(U_O + h * 1024, 1024),
                                           PF[:, h * 1024:(h + 1) * 1024],
                                           cc, INV2PI, ALU.add, ALU.mult)
                du = [head] + list(fmm)
                if first:
                    du.append(spill)
                uop = P.add("vector", f_u, deps=du)

                def f_k(e, h=h):
                    return e.tensor_copy(KI[:], ar(U_O + h * 1024, 1024))
                kop = P.add("vector", f_k, deps=[uop])

                def f_w(e, h=h):
                    return e.scalar_tensor_tensor(ar(U_O + h * 1024, 1024), KI[:],
                                                  -1.0, ar(U_O + h * 1024, 1024),
                                                  ALU.mult, ALU.add)
                vr_ops.append(P.add("vector", f_w, deps=[kop]))

            sin_h = []
            for h in range(2):
                def f_sin(e, h=h):
                    return e.activation(ar(SC_O + h * 1024, 1024),
                                        ar(U_O + h * 1024, 1024), AF.Sin,
                                        scale=bp("twopi"))
                sdeps = [vr_ops[h]] + state["sc_cons"]
                if first:
                    sdeps.append(spill)
                sin_h.append(P.add("scalar", f_sin, deps=sdeps))
            new_sc = sin_h[1]

            sc_cons = []
            piece_free = {}
            vmax = {}
            aev = {}
            for c in range(8):
                for h in range(2):
                    p = c * 2 + h
                    pz = PZA if p % 2 == 0 else PZB

                    def f_z2(e, c=c, h=h, pz=pz):
                        e.matmul(pz[:, 0:512], OUTW[:, c * 128:(c + 1) * 128],
                                 ar(SC_O + h * 1024, 512), start=True, stop=True)
                        return e.matmul(pz[:, 512:1024],
                                        OUTW[:, c * 128:(c + 1) * 128],
                                        ar(SC_O + h * 1024 + 512, 512),
                                        start=True, stop=True)
                    zdeps = [sin_h[h]]
                    if p >= 2:
                        zdeps += piece_free[p - 2]
                    zmm = P.add("tensor", f_z2, deps=zdeps)
                    sc_cons.append(zmm)

                    def f_abs(e, c=c, h=h, pz=pz):
                        dst = (M0_O if h == 0 else M1_O) if c == 0 else \
                              (A0_O if h == 0 else A1_O)
                        return e.activation(ar(dst, 1024), pz[:], AF.Abs,
                                            bias=bp("ob", c))
                    adeps = [zmm]
                    if c >= 2:
                        adeps.append(vmax[(c - 1, h)])
                    if first and c == 0:
                        adeps.append(spill)
                    a = P.add("scalar", f_abs, deps=adeps)
                    aev[(c, h)] = a
                    frees = [a]
                    if last:
                        def f_y(e, c=c, h=h, pz=pz):
                            return e.scalar_tensor_tensor(
                                v_ap(c, h * 1024, 1024), pz[:], bp("ob", c),
                                v_ap(c, h * 1024, 1024), ALU.add, ALU.mult)
                        yop = P.add("vector", f_y,
                                    deps=[zmm, a, v_ev[(c, 2 * h)],
                                          v_ev[(c, 2 * h + 1)]])
                        state["ystt"].append(yop)
                        frees.append(yop)
                    piece_free[p] = frees
                    if c >= 1:
                        def f_max(e, h=h):
                            mo = M0_O if h == 0 else M1_O
                            ao = A0_O if h == 0 else A1_O
                            return e.tensor_tensor(ar(mo, 1024), ar(mo, 1024),
                                                   ar(ao, 1024), ALU.max)
                        mdeps = [a, aev[(c - 1, h)] if c == 1 else vmax[(c - 1, h)]]
                        vmax[(c, h)] = P.add("vector", f_max, deps=mdeps)
            state["sc_cons"] = sc_cons

            rd_ops = []
            for g in range(4):
                def f_trg(e, g=g):
                    for q in range(4):
                        jt = g * 4 + q
                        h, jb = jt // 8, jt % 8
                        mo = M0_O if h == 0 else M1_O
                        last = e.transpose(PF[:, jt * 128:jt * 128 + 128],
                                           ar(mo + jb * 128, 128), IDN[:])
                    return last
                tdeps = [vmax[(7, g // 2)]] + vr_ops
                trg = P.add("tensor", f_trg, deps=tdeps)

                def f_rdg(e, g=g):
                    return e.tensor_reduce(
                        ar(MT_O + g * 4, 4),
                        PF[:, g * 512:(g + 1) * 512].rearrange(
                            "p (j q) -> p j q", q=128),
                        AX.X, ALU.max)
                rd_ops.append(P.add("vector", f_rdg, deps=[trg]))

            def f_s(e):
                e.tensor_scalar(ar(ST_O, 16), ar(MT_O, 16), EPS_MM, None, ALU.add)
                e.drain()
                return e.reciprocal(ar(ST_O, 16), ar(ST_O, 16))
            sop = P.add("vector", f_s, deps=rd_ops)

            def f_str(e):
                return e.transpose(PF[0:16, 0:128], ar(ST_O, 16), IDN[:])
            strop = P.add("tensor", f_str, deps=[sop] + rd_ops)

            def f_sre(e):
                return e.copy(AR[0:16, SRS_O:SRS_O + 64].bitcast(BF16),
                              PF[0:16, 0:128])
            sre = P.add("scalar", f_sre, deps=[strop])

            def f_srl(e):
                return e.scalar_tensor_tensor(
                    AR[0:16, SRS_O + 64:SRS_O + 128].bitcast(BF16),
                    AR[0:16, SRS_O:SRS_O + 64].bitcast(BF16), -1.0,
                    PF[0:16, 0:128], ALU.mult, ALU.add)
            srl_v = P.add("vector", f_srl, deps=[sre])

            dma_h = P.add("scalar", (lambda e: e.dma_start(
                out=AR[0:1, SROWH_O:SROWH_O + 1024].bitcast(BF16),
                in_=AR[0:16, SRS_O:SRS_O + 64].bitcast(BF16))),
                deps=[sre], is_dma=True)
            dma_l = P.add("sync", (lambda e: e.dma_start(
                out=AR[0:1, SROWL_O:SROWL_O + 1024].bitcast(BF16),
                in_=AR[0:16, SRS_O + 64:SRS_O + 128].bitcast(BF16))),
                deps=[srl_v], is_dma=True)

            spk_h = []
            for hh in range(2):
                def f_spk(e, hh=hh):
                    SRH = AR[0:1, SROWH_O:SROWH_O + 1024].bitcast(BF16)
                    SRL = AR[0:1, SROWL_O:SROWL_O + 1024].bitcast(BF16)
                    for b in (2 * hh, 2 * hh + 1):
                        sl = slice(b * 512, (b + 1) * 512)
                        e.matmul(PF[:, sl], ONEB[:], SRH[:, sl],
                                 start=True, stop=False)
                        last_ = e.matmul(PF[:, sl], ONEB[:], SRL[:, sl],
                                         start=False, stop=True)
                    return last_
                spk_h.append(P.add("tensor", f_spk,
                                   deps=[dma_h, dma_l, ld["onesb"]]))

            if not last:
                ssc32_h = []
                ssch_h = []
                sscl_h = []
                for hh in range(2):
                    def f_ssc(e, hh=hh):
                        return e.tensor_tensor(ar(SSC_O + hh * 1024, 1024),
                                               ar(SC_O + hh * 1024, 1024),
                                               PF[:, hh * 1024:(hh + 1) * 1024],
                                               ALU.mult)
                    ssc32_h.append(P.add("vector", f_ssc,
                                         deps=[spk_h[hh]] + sc_cons))

                    def f_ssch(e, hh=hh):
                        return e.copy(bf(SSCH_O + hh * 512, 1024),
                                      ar(SSC_O + hh * 1024, 1024))
                    ssch_h.append(P.add("scalar", f_ssch, deps=[ssc32_h[hh]]))

                    def f_sscl(e, hh=hh):
                        return e.tensor_tensor(bf(SSCL_O + hh * 512, 1024),
                                               ar(SSC_O + hh * 1024, 1024),
                                               bf(SSCH_O + hh * 512, 1024),
                                               ALU.subtract)
                    sscl_h.append(P.add("gpsimd", f_sscl, deps=[ssch_h[hh]]))
                state["ssch"] = ssch_h
                state["sscl"] = sscl_h
                state["srh"] = dma_h
                state["srl"] = dma_l
                state["pf_free"] = ssc32_h
                state["sc_cons"] = sc_cons + ssc32_h
            else:
                def f_s11(e):
                    return e.copy(ar(S11_O, 2048), PF[:])
                state["s11"] = P.add("scalar", f_s11, deps=spk_h)
                state["pf_free"] = [state["s11"]]
            if ("q%d" % k) in DEBUG_TAPS:
                pass
            tap("sc%d" % k, SC_O, [new_sc] + sc_cons, n=2048)

            state["sc"] = new_sc

        for k in range(1, 12):
            emit_iter(k)
            if TRUNC == ("scan%d" % k):
                _finish()
                return nc
        s11_ev = state["s11"]
        y_done = state["ystt"]

        pw_ld = []
        proj_ev = {}
        pj_ops = {}
        for m in range(8):
            deps = [q0_mm[3]]
            if m >= 2:
                deps.append(pj_ops[(m - 2, 3)])
            pw_ld.append(P.add("sync", (lambda e, m=m: e.dma_start(
                out=ar(W0 + 2048 + (m % 2) * 1024, 1024), in_=dprojw[m])),
                deps=deps, is_dma=True))
            for tt in range(4):
                bk = bank8[(m % 2) * 4 + tt]

                def f_pj(e, tt=tt, m=m, bk=bk):
                    for cch in range(8):
                        last = e.matmul(bk(), ar(W0 + 2048 + (m % 2) * 1024 + cch * 128, 128),
                                        v_ap(cch, tt * 512, 512),
                                        start=(cch == 0), stop=(cch == 7))
                    return last
                deps = [pw_ld[m], s11_ev] + y_done
                if (m - 2, tt) in proj_ev:
                    deps.append(proj_ev[(m - 2, tt)])
                pj_ops[(m, tt)] = P.add("tensor", f_pj, deps=deps)

                def f_pje(e, tt=tt, m=m, bk=bk):
                    return e.tensor_tensor(acc_ap(m, tt * 512, 512), bk(),
                                           ar(S11_O + tt * 512, 512), ALU.mult)
                proj_ev[(m, tt)] = P.add("vector", f_pje,
                                         deps=[pj_ops[(m, tt)], s11_ev])
        proj_last = proj_ev[(7, 3)]
        pj_last = pj_ops[(7, 3)]
        tap("attn", ACC0, [proj_last, proj_ev[(7, 2)]])

        if TRUNC == "p6":
            _finish()
            return nc

        reload1 = P.add("sync", (lambda e: e.dma_start(
            out=ar(V0, 8192), in_=dxnsp[:, 0:8192])), deps=[proj_last], is_dma=True)
        reload2 = P.add("sync", (lambda e: e.dma_start(
            out=ar(V0 + 8192, 8192), in_=dxnsp[:, 8192:16384])),
            deps=[proj_last], is_dma=True)
        reload_gate[0] = reload2

        def xnh(k, t0, tn):
            return bf(V0 + k * 1024, 2048)[:, t0:t0 + tn]

        def xnl(k, t0, tn):
            return bf(V0 + 8192 + k * 1024, 2048)[:, t0:t0 + tn]

        def mm3(e, psum, wslot, nk, rhs_hi, rhs_lo, tsz):
            first = True
            last = None
            for kk in range(nk):
                whi = bf(wslot + kk * 64, 128)
                wlo = bf(wslot + nk * 64 + kk * 64, 128)
                last = e.matmul(psum, whi, rhs_hi(kk), start=first, stop=False)
                first = False
                e.matmul(psum, whi, rhs_lo(kk), start=False, stop=False)
                last = e.matmul(psum, wlo, rhs_hi(kk), start=False,
                                stop=(kk == nk - 1))
            return last

        sw_ld = {}
        g_ops = {}
        silu_ops = {}
        s2mm_ops = {}
        s2ev_ops = {}
        for tt in range(4):
            for m in range(16):
                slot = W0 + (m % 2) * 2048
                if tt == 0:
                    if m == 0:
                        sg = [vmm[(7, 3)]]
                    elif m == 1:
                        sg = [pj_last]
                    else:
                        sg = [g_ops[(0, m - 2)], ps_dmas[2 * (m - 2)],
                              ps_dmas[2 * (m - 2) + 1]]
                    emit_presplit(2, slot_guard=sg)
                else:
                    ldd = []
                    if m >= 2:
                        ldd.append(g_ops[(tt, m - 2)])
                    else:
                        ldd.append(s2mm_ops[(tt - 1, 7)])
                    ldd.append(presplit_done[("sw1", m)])
                    ldd.append(presplit_done[("sw3", m)])
                    sw_ld[(tt, m, 1)] = P.add("sync", (lambda e, m=m, slot=slot:
                                                       e.dma_start(out=ar(slot, 1024),
                                                                   in_=dswsp[:, m * 1024:(m + 1) * 1024])),
                                              deps=ldd, is_dma=True)
                    sw_ld[(tt, m, 3)] = P.add("sync", (lambda e, m=m, slot=slot:
                                                       e.dma_start(out=ar(slot + 1024, 1024),
                                                                   in_=dswsp[:, 16384 + m * 1024:16384 + (m + 1) * 1024])),
                                              deps=ldd, is_dma=True)

                def f_h1(e, tt=tt, m=m, slot=slot):
                    return mm3(e, PZA[:, 0:512], slot, 8,
                               lambda k: xnh(k, tt * 512, 512),
                               lambda k: xnl(k, tt * 512, 512), 512)
                d1 = [ps_ops[2 * m] if tt == 0 else sw_ld[(tt, m, 1)],
                      reload1, reload2]
                if (tt, m - 1) in silu_ops:
                    d1.append(silu_ops[(tt, m - 1)])
                elif tt >= 1:
                    d1.append(silu_ops[(tt - 1, 15)])
                h1 = P.add("tensor", f_h1, deps=d1)

                def f_h3(e, tt=tt, m=m, slot=slot):
                    return mm3(e, PZB[:, 0:512], slot + 1024, 8,
                               lambda k: xnh(k, tt * 512, 512),
                               lambda k: xnl(k, tt * 512, 512), 512)
                d3 = [ps_ops[2 * m + 1] if tt == 0 else sw_ld[(tt, m, 3)],
                      reload1, reload2]
                if (tt, m - 1) in g_ops:
                    d3.append(g_ops[(tt, m - 1)])
                elif tt >= 1:
                    d3.append(g_ops[(tt - 1, 15)])
                h3 = P.add("tensor", f_h3, deps=d3)

                def f_silu(e, m=m):
                    return e.activation(ar(H1S_O, 512), PZA[:, 0:512], AF.Silu,
                                        bias=bp("sb1", m))
                sdeps = [h1]
                if (tt, m - 1) in g_ops:
                    sdeps.append(g_ops[(tt, m - 1)])
                silu_ops[(tt, m)] = P.add("scalar", f_silu, deps=sdeps)

                def f_g(e, tt=tt, m=m):
                    e.scalar_tensor_tensor(ar(GT_O, 512), PZB[:, 0:512],
                                           bp("sb3", m), ar(H1S_O, 512),
                                           ALU.add, ALU.mult)
                    e.drain()
                    e.tensor_copy(bf(GHI_O + m * 256, 512), ar(GT_O, 512))
                    e.drain()
                    return e.scalar_tensor_tensor(bf(GLO_O + m * 256, 512),
                                                  bf(GHI_O + m * 256, 512), -1.0,
                                                  ar(GT_O, 512), ALU.mult, ALU.add)
                gd = [h3, silu_ops[(tt, m)]]
                if tt >= 1:
                    gd.append(s2mm_ops[(tt - 1, 7)])
                if (tt, m - 1) in g_ops:
                    gd.append(g_ops[(tt, m - 1)])
                g_ops[(tt, m)] = P.add("vector", f_g, deps=gd)
            for cc_ in range(8):
                slot = W0 + (cc_ % 2) * 2048
                if tt == 0:
                    if cc_ <= 1:
                        sg = [g_ops[(0, 14 + cc_)], ps_dmas[28 + 2 * cc_],
                              ps_dmas[29 + 2 * cc_]]
                    else:
                        sg = [s2mm_ops[(0, cc_ - 2)], ps_dmas[32 + 2 * (cc_ - 2)],
                              ps_dmas[33 + 2 * (cc_ - 2)]]
                    emit_presplit(2, slot_guard=sg)
                else:
                    ldd = [g_ops[(tt, 15)] if cc_ <= 1 else s2mm_ops[(tt, cc_ - 2)],
                           presplit_done[("sw2", cc_)]]
                    sw_ld[(tt, cc_, 2)] = P.add("sync", (lambda e, c=cc_, slot=slot:
                                                         e.dma_start(out=ar(slot, 2048),
                                                                     in_=dswsp[:, 32768 + c * 2048:32768 + (c + 1) * 2048])),
                                                deps=ldd, is_dma=True)

                def f_s2(e, tt=tt, c=cc_, slot=slot):
                    return mm3(e, bank8[c % 4](), slot, 16,
                               lambda k: bf(GHI_O + k * 256, 512),
                               lambda k: bf(GLO_O + k * 256, 512), 512)
                d2 = [ps_ops[33 + 2 * cc_] if tt == 0 else sw_ld[(tt, cc_, 2)],
                      g_ops[(tt, 15)]]
                if (tt, cc_ - 4) in s2ev_ops:
                    d2.append(s2ev_ops[(tt, cc_ - 4)])
                elif tt >= 1 and (tt - 1, cc_ + 4) in s2ev_ops:
                    d2.append(s2ev_ops[(tt - 1, cc_ + 4)])
                s2mm_ops[(tt, cc_)] = P.add("tensor", f_s2, deps=d2)

                tmpo = 4608 + ((tt * 8 + cc_) % 2) * 512

                def f_s2t(e, tt=tt, c=cc_, tmpo=tmpo):
                    return e.tensor_scalar(ar(tmpo, 512), bank8[c % 4](),
                                           0.5, bp("sb2c", c),
                                           ALU.mult, ALU.add)
                ts_op = P.add("vector", f_s2t,
                              deps=[s2mm_ops[(tt, cc_)],
                                    s2ev_ops.get((tt, cc_ - 2)) or
                                    s2ev_ops.get((tt - 1, cc_ + 6))])

                def f_s2e(e, tt=tt, c=cc_, tmpo=tmpo):
                    for half in range(2):
                        rr = 2 * tt + half
                        accsl = AR[:, ACC0 + c * 2048 + rr:ACC0 + (c + 1) * 2048:8]
                        last = e.tensor_tensor(accsl, accsl,
                                               ar(tmpo + half * 256, 256), ALU.add)
                        if half == 0:
                            e.drain()
                    return last
                s2ev_ops[(tt, cc_)] = P.add("vector", f_s2e,
                                            deps=[ts_op, proj_ev[(cc_, tt)]])
        presplit_last = ps_dmas[-1]
        shared_last = s2ev_ops[(3, 7)]
        tap("acc_sh", ACC0, [shared_last, s2ev_ops[(3, 6)]])

        if TRUNC == "p2":
            _finish()
            return nc

        hmm_ops = {}
        gel_ops = {}
        hsp_ops = {}
        ymm_ops = {}
        yev_ops = {}
        wq_lds = {}
        slot_consumer = {0: s2mm_ops[(3, 6)], 1: s2mm_ops[(3, 7)]}

        moe_split_prev = [[]]

        def moe_load(ex, qi, extra_deps):
            slot = W0 + (qi % 2) * 2048
            src = dw1[ex][:, (qi % 4) * 2048:(qi % 4) * 2048 + 2048] if qi < 4 \
                else dw2[ex][:, (qi - 4) * 2048:(qi - 4) * 2048 + 2048]
            dmadeps = list(moe_split_prev[0]) + list(extra_deps)
            op = P.add("sync", (lambda e, slot=slot, src=src:
                                e.dma_start(out=ar(STG3_O, 2048), in_=src)),
                       deps=dmadeps, is_dma=True)
            def f_sp_hi(e, slot=slot):
                return e.copy(bf(slot, 2048), ar(STG3_O, 2048))
            hi_op = P.add("scalar", f_sp_hi, deps=[op] + list(extra_deps))

            def f_sp_a(e, slot=slot):
                return e.tensor_tensor(bf(slot + 1024, 1024),
                                       ar(STG3_O, 1024),
                                       bf(slot, 1024), ALU.subtract)
            sp_a = P.add("gpsimd", f_sp_a, deps=[hi_op, op] + extra_deps)

            def f_sp_b(e, slot=slot):
                return e.tensor_tensor(bf(slot + 1536, 1024),
                                       ar(STG3_O + 1024, 1024),
                                       bf(slot + 512, 1024), ALU.subtract)
            sp_b = P.add("vector", f_sp_b, deps=[hi_op, op] + extra_deps)
            moe_split_prev[0] = [sp_a, sp_b]
            wq_lds[(ex, qi)] = [sp_a, sp_b]

        def moe_hmm(ex, m):
            ra, rb = RA[ex], RB[ex]
            slot = W0 + ((m // 2) % 2) * 2048
            toff = (m % 2) * 8 * 64
            bk = bank8[4 + m % 4]

            def f_hmm(e, m=m, slot=slot, toff=toff, ra=ra, rb=rb, bk=bk):
                b = bk()
                first = True
                for kk in range(8):
                    whi = bf(slot + toff + kk * 64, 128)
                    wlo = bf(slot + 1024 + toff + kk * 64, 128)
                    for which, rr in ((0, ra), (1, rb)):
                        rhi = bf(V0 + kk * 1024 + rr * 128, 256)
                        rlo = bf(V0 + 8192 + kk * 1024 + rr * 128, 256)
                        sl = b[:, which * 256:which * 256 + 256]
                        e.matmul(sl, whi, rhi, start=first, stop=False)
                        first = False
                        e.matmul(sl, whi, rlo, start=False, stop=False)
                        last = e.matmul(sl, wlo, rhi, start=False,
                                        stop=(kk == 7 and which == 1))
                return last
            hd = list(wq_lds[(ex, m // 2)])
            if (ex, m - 4) in gel_ops:
                hd.append(gel_ops[(ex, m - 4)])
            elif ex >= 1:
                hd.append(yev_ops[(ex - 1, ((m % 4) + 4, 1))])
            else:
                hd += [silu_ops[(3, 15)], g_ops[(3, 15)]]
            hmm_ops[(ex, m)] = P.add("tensor", f_hmm, deps=hd)
            slot_consumer[(m // 2) % 2] = hmm_ops[(ex, m)]

            def f_gel(e, ex=ex, m=m, bk=bk):
                return e.activation(ar(HTMP_O, 512), bk(), AF.Gelu,
                                    bias=bp("b1", ex * 8 + m))
            gdeps = [hmm_ops[(ex, m)]]
            if (ex, m - 1) in hsp_ops:
                gdeps.append(hsp_ops[(ex, m - 1)])
            elif ex >= 1:
                gdeps.append(hsp_ops[(ex - 1, 7)])
            gel_ops[(ex, m)] = P.add("scalar", f_gel, deps=gdeps)

            def f_hhi(e, m=m):
                return e.copy(bf(HBH_O + m * 256, 512), ar(HTMP_O, 512))
            hspd = [gel_ops[(ex, m)]]
            if ex >= 1:
                hspd.append(ymm_ops[(ex - 1, 7)])
            hhi = P.add("scalar", f_hhi, deps=hspd)

            def f_hlo(e, m=m):
                return e.tensor_tensor(bf(HBL_O + m * 256, 512),
                                       ar(HTMP_O, 512),
                                       bf(HBH_O + m * 256, 512), ALU.subtract)
            hsp_ops[(ex, m)] = P.add("gpsimd", f_hlo, deps=[hhi])

        def moe_ymm(ex, c):
            ra, rb = RA[ex], RB[ex]
            slot = W0 + ((c // 2) % 2) * 2048
            toff = (c % 2) * 8 * 64
            bk = bank8[4 + c % 4]

            def f_ymm(e, slot=slot, toff=toff, bk=bk):
                b = bk()
                first = True
                for kk in range(8):
                    whi = bf(slot + toff + kk * 64, 128)
                    wlo = bf(slot + 1024 + toff + kk * 64, 128)
                    rhi = bf(HBH_O + kk * 256, 512)
                    rlo = bf(HBL_O + kk * 256, 512)
                    e.matmul(b[:, 0:512], whi, rhi, start=first, stop=False)
                    first = False
                    e.matmul(b[:, 0:512], whi, rlo, start=False, stop=False)
                    last = e.matmul(b[:, 0:512], wlo, rhi, start=False,
                                    stop=(kk == 7))
                return last
            yd = list(wq_lds[(ex, 4 + c // 2)]) + [hsp_ops[(ex, 7)]]
            if (ex, (c - 4, 1)) in yev_ops:
                yd.append(yev_ops[(ex, (c - 4, 1))])
            else:
                yd.append(gel_ops[(ex, 4 + (c % 4))])
            ymm_ops[(ex, c)] = P.add("tensor", f_ymm, deps=yd)
            slot_consumer[(c // 2) % 2] = ymm_ops[(ex, c)]
            for which, rr in ((0, ra), (1, rb)):
                tmpo = 6656 + (((ex * 16 + c * 2 + which) % 4)) * 256

                def f_yt(e, c=c, which=which, bk=bk, ex=ex, tmpo=tmpo):
                    b = bk()
                    return e.tensor_scalar(ar(tmpo, 256),
                                           b[:, which * 256:which * 256 + 256],
                                           0.25, bp("b2q", ex * 8 + c),
                                           ALU.mult, ALU.add)
                ytdeps = [ymm_ops[(ex, c)]]
                prevkey = (ex, (c - 2, which)) if c >= 2 else                     ((ex - 1, (c + 6, which)) if ex >= 1 else None)
                if prevkey and prevkey[0] >= 0 and prevkey in yev_ops:
                    ytdeps.append(yev_ops[prevkey])
                yt_op = P.add("vector", f_yt, deps=ytdeps)

                def f_yev(e, c=c, rr=rr, tmpo=tmpo):
                    accsl = AR[:, ACC0 + c * 2048 + rr:ACC0 + (c + 1) * 2048:8]
                    return e.tensor_tensor(accsl, accsl, ar(tmpo, 256), ALU.add)
                yev_ops[(ex, (c, which))] = P.add(
                    "vector", f_yev, deps=[yt_op, shared_last])

        x2_ld = []
        for j in range(8):
            x2_ld.append(P.add("scalar", (lambda e, j=j: e.dma_start(
                out=ar(GHI_O + j * 1024, 1024),
                in_=dx[j * 128:(j + 1) * 128, :])),
                deps=[s2mm_ops[(3, 7)]], is_dma=True))

        for ex in range(E):
            moe_load(ex, 0, [slot_consumer[0]] +
                     ([shared_last, s2ev_ops[(3, 6)], presplit_last,
                       ps_ops[-1]] if ex == 0 else []))
            moe_load(ex, 1, [slot_consumer[1]])
            if ex == 0:
                tap("w1q0", W0, list(wq_lds[(0, 0)]), n=2048)
            moe_hmm(ex, 0)
            moe_hmm(ex, 1)
            moe_load(ex, 2, [hmm_ops[(ex, 1)]] +
                     ([tap("w1q0_dummy", 0, [], n=0)] if False else []))
            moe_hmm(ex, 2)
            moe_hmm(ex, 3)
            moe_load(ex, 3, [hmm_ops[(ex, 3)]])
            moe_hmm(ex, 4)
            moe_hmm(ex, 5)
            moe_load(ex, 4, [hmm_ops[(ex, 5)]])
            moe_hmm(ex, 6)
            moe_hmm(ex, 7)
            moe_load(ex, 5, [hmm_ops[(ex, 7)]])
            if ex == 0:
                tap("hb0", HBH_O, [hsp_ops[(0, 7)]], n=4096)
            moe_ymm(ex, 0)
            moe_ymm(ex, 1)
            moe_load(ex, 6, [ymm_ops[(ex, 1)]])
            moe_ymm(ex, 2)
            moe_ymm(ex, 3)
            moe_load(ex, 7, [ymm_ops[(ex, 3)]])
            moe_ymm(ex, 4)
            moe_ymm(ex, 5)
            moe_ymm(ex, 6)
            moe_ymm(ex, 7)
            tap("acc_e%d" % ex, ACC0, [yev_ops[(ex, (7, 1))], yev_ops[(ex, (7, 0))]])
        moe_vlast = yev_ops[(E - 1, (7, 1))]
        moe_tlast = ymm_ops[(E - 1, 7)]
        tap("acc_moe", ACC0, [moe_vlast, yev_ops[(E - 1, (7, 0))]])

        if TRUNC == "p3":
            _finish()
            return nc

        out_dma = []
        p7_v = []
        p7_t = []
        for j in range(16):

            def f_tr7(e, j=j):
                for q in range(8):
                    if j % 2 == 0:
                        ps = PZA if q < 4 else PZB
                        dst = ps[:, (q % 4) * 128:(q % 4) * 128 + 128]
                    else:
                        dst = PF[:, q * 128:q * 128 + 128]
                    last = e.transpose(dst, acc_ap(q, j * 128, 128), IDN[:])
                return last
            tdeps = [moe_vlast]
            if j >= 2:
                tdeps.append(p7_v[j - 2])
            ttr = P.add("tensor", f_tr7, deps=tdeps)
            p7_t.append(ttr)

            def f_add(e, j=j):
                xsl = GHI_O + (j % 8) * 1024
                osl = XN0 + (j % 4) * 1024
                if j % 2 == 0:
                    e.tensor_tensor(ar(osl, 512), PZA[:, 0:512],
                                    ar(xsl, 512), ALU.add)
                    return e.tensor_tensor(ar(osl + 512, 512),
                                           PZB[:, 0:512],
                                           ar(xsl + 512, 512),
                                           ALU.add)
                return e.tensor_tensor(ar(osl, 1024),
                                       PF[:, 0:1024],
                                       ar(xsl, 1024), ALU.add)
            vdeps = [ttr, x2_ld[j], moe_tlast]
            if j >= 4:
                vdeps.append(out_dma[j - 4])
            vop = P.add("vector", f_add, deps=vdeps)
            p7_v.append(vop)
            if j + 8 < 16:
                x2_ld.append(P.add("scalar", (lambda e, j=j: e.dma_start(
                    out=ar(GHI_O + (j % 8) * 1024, 1024),
                    in_=dx[(j + 8) * 128:(j + 9) * 128, :])),
                    deps=[vop], is_dma=True))
            out_dma.append(P.add("sync" if j % 2 == 0 else "scalar",
                                 (lambda e, j=j: e.dma_start(
                                     out=dout[j * 128:(j + 1) * 128, :],
                                     in_=ar(XN0 + (j % 4) * 1024, 1024))),
                                 deps=[vop], is_dma=True))
        P.add("sync", lambda e: None, deps=[out_dma[15], out_dma[14]])

        P.emit(blk, sems)
    return nc


def _lhsT_cols(w, nk, nm):
    t = w.reshape(nk, 128, nm, 128)
    return np.ascontiguousarray(t.transpose(2, 1, 0, 3).reshape(nm, 128, nk * 128))


def _pm(a, n):
    X = a.shape[1]
    return np.ascontiguousarray(a.reshape(n, 128, X).transpose(1, 0, 2)
                                .reshape(128, n * X))


def _colpack(b):
    n = b.shape[0] // 128
    return np.ascontiguousarray(b.reshape(n, 128).T)


def kernel(L0_attn_x, L0_attn_attn_W, L0_attn_attn_b, L0_attn_freq_W,
           L0_attn_freq_b, L0_attn_out_W, L0_attn_out_b, L0_attn_proj_W,
           L0_attn_proj_b, L0_attn_identity, L0_ffn_t_emb, L0_ffn_sw1,
           L0_ffn_sb1, L0_ffn_sw3, L0_ffn_sb3, L0_ffn_sw2, L0_ffn_sb2,
           L0_ffn_w1, L0_ffn_b1, L0_ffn_w2, L0_ffn_b2):
    f32 = np.float32
    x = np.asarray(L0_attn_x, f32)
    attn_W = np.asarray(L0_attn_attn_W, f32)
    attn_b = np.asarray(L0_attn_attn_b, f32)
    freq_W = np.asarray(L0_attn_freq_W, f32)
    freq_b = np.asarray(L0_attn_freq_b, f32)
    out_W = np.asarray(L0_attn_out_W, f32)
    out_b = np.asarray(L0_attn_out_b, f32)
    proj_W = np.asarray(L0_attn_proj_W, f32)
    proj_b = np.asarray(L0_attn_proj_b, f32)
    ident_v = np.asarray(L0_attn_identity, f32)
    sw1 = np.asarray(L0_ffn_sw1, f32)
    sb1 = np.asarray(L0_ffn_sb1, f32)
    sw3 = np.asarray(L0_ffn_sw3, f32)
    sb3 = np.asarray(L0_ffn_sb3, f32)
    sw2 = np.asarray(L0_ffn_sw2, f32)
    sb2 = np.asarray(L0_ffn_sb2, f32)
    w1 = np.asarray(L0_ffn_w1, f32)
    b1 = np.asarray(L0_ffn_b1, f32)
    w2 = np.asarray(L0_ffn_w2, f32)
    b2 = np.asarray(L0_ffn_b2, f32)

    Wq = attn_W[:, :C]
    Wv = attn_W[:, C:]
    bq = attn_b[:C]
    bv = attn_b[C:]
    Wfy = freq_W[:C, :]
    Wfx = freq_W[C:, :]

    shared = {
        "wv": _lhsT_cols(Wv, 8, 8),
        "wqT": _lhsT_cols(np.ascontiguousarray(Wq.T), 8, 8),
        "wfy": _pm(Wfy, 8),
        "wfx": _pm(Wfx, 8),
        "wfy2": _pm(np.concatenate([Wfy.reshape(8, 128, 64)] * 2, axis=2)
                    .reshape(1024, 128), 8),
        "wfx2": _pm(np.concatenate([Wfx.reshape(8, 128, 64)] * 2, axis=2)
                    .reshape(1024, 128), 8),
        "outw": np.ascontiguousarray(out_W),
        "outwT": _pm(np.ascontiguousarray(out_W.T), 8),
        "projw": _lhsT_cols(proj_W, 8, 8),
        "sw1": _lhsT_cols(sw1, 8, 16),
        "sw3": _lhsT_cols(sw3, 8, 16),
        "sw2": _lhsT_cols(sw2, 16, 8),
        "w1": np.stack([_lhsT_cols(w1[e], 8, 8).transpose(1, 0, 2)
                        .reshape(128, 8192) for e in range(E)]),
        "w2": np.stack([_lhsT_cols(w2[e], 8, 8).transpose(1, 0, 2)
                        .reshape(128, 8192) for e in range(E)]),
        "ident": np.eye(128, dtype=f32),
        "ones1": np.ones((1, 128), dtype=f32),
        "ones1b": np.ones((1, 128), dtype=ml_dtypes.bfloat16),
    }
    bpk = np.zeros((128, NBIAS), f32)
    bpk[:, BC["bv"]:BC["bv"] + 8] = _colpack(bv)
    bpk[:, BC["projb"]:BC["projb"] + 8] = _colpack(proj_b)
    bpk[:, BC["sb1"]:BC["sb1"] + 16] = _colpack(sb1)
    bpk[:, BC["sb3"]:BC["sb3"] + 16] = _colpack(sb3)
    bpk[:, BC["sb2c"]:BC["sb2c"] + 8] = _colpack(sb2)
    bpk[:, BC["b1"]:BC["b1"] + 64] = np.concatenate(
        [_colpack(b1[e]) for e in range(E)], axis=1)
    bpk[:, BC["b2q"]:BC["b2q"] + 64] = np.concatenate(
        [_colpack(b2[e]) for e in range(E)], axis=1)
    bpk[:, BC["ob"]:BC["ob"] + 8] = _colpack(out_b)
    bpk[:, BC["bq"]:BC["bq"] + 8] = _colpack(bq)
    bpk[:, BC["id"]:BC["id"] + 8] = _colpack(ident_v)
    bpk[:, BC["obc"]:BC["obc"] + 8] = _colpack(out_b)
    bpk[:, BC["fb2"]] = np.concatenate([freq_b, freq_b])
    bpk[:, BC["pihalf"]] = np.concatenate([np.zeros(64, f32),
                                           np.full(64, PI / 2, f32)])
    bpk[:, BC["twopi"]] = 2 * PI
    bpk[:, BC["invC"]] = 1.0 / C
    bpk[:, BC["epsln"]] = EPS_LN
    shared["biaspack"] = bpk

    nc = build_nc()
    in_maps = []
    for b in range(8):
        m = dict(shared)
        m["x"] = np.ascontiguousarray(x[b])
        in_maps.append(m)
    res = run_bass_kernel_spmd(nc, in_maps, core_ids=list(range(8)))
    globals()["_last_results"] = res.results
    out = np.stack([res.results[b]["out"] for b in range(8)])
    return out.astype(np.float32)


# version 35
# speedup vs baseline: 1.0999x; 1.0098x over previous
import contextlib
import numpy as np
import ml_dtypes
import concourse.bass as bass
import concourse.mybir as mybir
from concourse.bass_utils import run_bass_kernel_spmd

F32 = mybir.dt.float32
I32 = mybir.dt.int32
AF = mybir.ActivationFunctionType
ALU = mybir.AluOpType
AX = mybir.AxisListType

C = 1024
T = 2048
E = 8
EPS_LN = 1e-5
EPS_MM = 1e-6
PI = float(np.pi)
INV2PI = float(1.0 / (2.0 * np.pi))

DEBUG_TAPS = []
TRUNC = None

XN0 = 0
ACC0 = 16384
V0 = 32768
W0 = 49152
ARENA_N = 53248
SC_O = XN0 + 0
SSC_O = XN0 + 2048
MT_O = XN0 + 2048
ST_O = XN0 + 2064
SRS_O = XN0 + 2080
S11_O = XN0 + 4096
U_O = XN0 + 8192
A0_O = XN0 + 10240
A1_O = XN0 + 11264
M0_O = XN0 + 12288
M1_O = XN0 + 13312
SROW_O = XN0 + 14336
SSCH_O = XN0 + 4096
SSCL_O = XN0 + 5120
SROWH_O = XN0 + 6144
SROWL_O = XN0 + 7168
GHI_O = XN0 + 8192
GLO_O = XN0 + 12288
H1S_O = XN0 + 4096
GT_O = XN0 + 6656
STG3_O = XN0 + 4608
W2T_O = XN0 + 6656
HTMP_O = XN0 + 7680
HBH_O = XN0 + 0
HBL_O = XN0 + 2048
P7X_O = XN0 + 0
P7O_O = XN0 + 2048
CW2_O = ACC0 + 0
P1_O = ACC0 + 2048
PSP_O = ACC0 + 2048
XSP_O = ACC0 + 4096
OSP_O = ACC0 + 6144
OS2_O = ACC0 + 8192
WFY_O = V0 + 0
WFX_O = V0 + 512
WFY2_O = V0 + 1024
WFX2_O = V0 + 2048
OWT_O = V0 + 3072

_BIAS_ORDER = [("bv", 8), ("projb", 8), ("sb1", 16), ("sb3", 16), ("sb2c", 8),
               ("b1", 64), ("b2q", 64), ("ob", 8), ("bq", 8), ("id", 8),
               ("obc", 8), ("fb2", 1), ("pihalf", 1), ("twopi", 1),
               ("invC", 1), ("epsln", 1)]
BC = {}
_c = 0
for _n, _k in _BIAS_ORDER:
    BC[_n] = _c
    _c += _k
NBIAS = _c

RA = [(3 * e) % 8 for e in range(8)]
RB = [(3 * e + 3) % 8 for e in range(8)]


class Op:
    __slots__ = ("eng", "fn", "deps", "chan", "val", "needed", "idx")


class Prog:
    def __init__(self):
        self.ops = []

    def add(self, eng, fn, deps=(), is_dma=False):
        op = Op()
        op.eng = eng
        op.fn = fn
        op.deps = [d for d in deps if d is not None]
        if is_dma:
            op.chan = {"sync": "d", "gpsimd": "g", "scalar": "s2"}[eng]
        else:
            op.chan = {"tensor": "t", "vector": "v", "scalar": "s",
                       "gpsimd": "p", "sync": "y"}[eng]
        op.val = None
        op.needed = is_dma
        op.idx = len(self.ops)
        self.ops.append(op)
        return op

    def emit(self, blk, sems):
        dma_ch = ("d", "g", "s2")
        for op in self.ops:
            for dp in op.deps:
                assert dp.idx < op.idx
                if dp.eng != op.eng or dp.chan in dma_ch:
                    dp.needed = True
        counters = {}
        for op in self.ops:
            if op.needed:
                step = 16 if op.chan in dma_ch else 1
                counters[op.chan] = counters.get(op.chan, 0) + step
                op.val = counters[op.chan]
        watermark = {}
        secmap = {"sync": blk.sync, "tensor": blk.tensor, "vector": blk.vector,
                  "scalar": blk.scalar, "gpsimd": blk.gpsimd}
        by_eng = {}
        for op in self.ops:
            by_eng.setdefault(op.eng, []).append(op)
        for eng, run in by_eng.items():
            def section(e, run=run, eng=eng):
                for op in run:
                    for dp in op.deps:
                        if dp.eng == op.eng and dp.chan not in dma_ch:
                            continue
                        key = (eng, dp.chan)
                        if watermark.get(key, 0) >= dp.val:
                            continue
                        e.wait_ge(sems[dp.chan], dp.val)
                        watermark[key] = dp.val
                    last = op.fn(e)
                    if eng == "vector" and op.chan == "v":
                        last = e.drain()
                    if op.needed:
                        if last is None:
                            last = e.nop()
                        last.then_inc(sems[op.chan], 16 if op.chan in dma_ch else 1)
            secmap[eng](section)
        return counters


def build_nc():
    nc = bass.Bass("TRN2", target_bir_lowering=False,
                   dynamic_dma_scratch_size=256)

    dx = nc.dram_tensor("x", [T, C], F32, kind="ExternalInput").ap()
    dout = nc.dram_tensor("out", [T, C], F32, kind="ExternalOutput").ap()
    dwv = nc.dram_tensor("wv", [8, 128, 1024], F32, kind="ExternalInput").ap()
    dwqT = nc.dram_tensor("wqT", [8, 128, 1024], F32, kind="ExternalInput").ap()
    dwfy = nc.dram_tensor("wfy", [128, 512], F32, kind="ExternalInput").ap()
    dwfx = nc.dram_tensor("wfx", [128, 512], F32, kind="ExternalInput").ap()
    dwfy2 = nc.dram_tensor("wfy2", [128, 1024], F32, kind="ExternalInput").ap()
    dwfx2 = nc.dram_tensor("wfx2", [128, 1024], F32, kind="ExternalInput").ap()
    doutw = nc.dram_tensor("outw", [128, 1024], F32, kind="ExternalInput").ap()
    doutwT = nc.dram_tensor("outwT", [128, 1024], F32, kind="ExternalInput").ap()
    dprojw = nc.dram_tensor("projw", [8, 128, 1024], F32, kind="ExternalInput").ap()
    dsw1 = nc.dram_tensor("sw1", [16, 128, 1024], F32, kind="ExternalInput").ap()
    dsw3 = nc.dram_tensor("sw3", [16, 128, 1024], F32, kind="ExternalInput").ap()
    dsw2 = nc.dram_tensor("sw2", [8, 128, 2048], F32, kind="ExternalInput").ap()
    dw1 = nc.dram_tensor("w1", [8, 128, 8192], F32, kind="ExternalInput").ap()
    dw2 = nc.dram_tensor("w2", [8, 128, 8192], F32, kind="ExternalInput").ap()
    dbias = nc.dram_tensor("biaspack", [128, NBIAS], F32, kind="ExternalInput").ap()
    dident = nc.dram_tensor("ident", [128, 128], F32, kind="ExternalInput").ap()
    dones = nc.dram_tensor("ones1", [1, 128], F32, kind="ExternalInput").ap()
    donesb = nc.dram_tensor("ones1b", [1, 128], mybir.dt.bfloat16,
                            kind="ExternalInput").ap()
    dxnsp = nc.dram_tensor("xnsp", [128, 16384], F32, kind="Internal").ap()
    dswsp = nc.dram_tensor("swsp", [128, 49152], F32, kind="Internal").ap()
    dtaps = {}
    for tp in DEBUG_TAPS:
        dtaps[tp] = nc.dram_tensor("tap_" + tp, [128, 16384], F32,
                                   kind="ExternalOutput").ap()

    st = contextlib.ExitStack()
    with st:
        AR = st.enter_context(nc.sbuf_tensor("AR", [128, ARENA_N], F32))
        ART = AR
        KI = st.enter_context(nc.sbuf_tensor("KI", [128, 1024], I32))
        BP = st.enter_context(nc.sbuf_tensor("BP", [128, NBIAS], F32))
        IDN = st.enter_context(nc.sbuf_tensor("IDN", [128, 128], F32))
        ONER = st.enter_context(nc.sbuf_tensor("ONER", [1, 128], F32))
        ONEB = st.enter_context(nc.sbuf_tensor("ONEB", [1, 128],
                                               mybir.dt.bfloat16))
        OUTW = st.enter_context(nc.sbuf_tensor("OUTW", [128, 1024], F32))
        OW2Y = st.enter_context(nc.sbuf_tensor("OW2Y", [128, 128], F32))
        OW2X = st.enter_context(nc.sbuf_tensor("OW2X", [128, 128], F32))
        CBY2 = st.enter_context(nc.sbuf_tensor("CBY2", [1, 128], F32))
        CBX2 = st.enter_context(nc.sbuf_tensor("CBX2", [1, 128], F32))
        SM = st.enter_context(nc.sbuf_tensor("SM", [128, 16], F32))
        PF = st.enter_context(nc.psum_tensor("PF", [128, 2048], F32))
        PZA = st.enter_context(nc.psum_tensor("PZA", [128, 1024], F32))
        PZB = st.enter_context(nc.psum_tensor("PZB", [128, 1024], F32))
        sems = {}
        for ch in ["d", "t", "v", "s", "g", "p", "y", "s2"]:
            sems[ch] = st.enter_context(nc.semaphore("sem_" + ch))
        blk = st.enter_context(nc.Block())

        P = Prog()

        def _finish():
            P.emit(blk, sems)


        def ar(off, n, p0=0, p1=128):
            return AR[p0:p1, off:off + n]

        BF16 = mybir.dt.bfloat16

        def bf(off, n_bf16):
            return AR[:, off:off + n_bf16 // 2].bitcast(BF16)

        def bp(name, i=0):
            c0 = BC[name] + i
            return BP[:, c0:c0 + 1]

        def xn_ap(ch, t0, tn):
            return ar(XN0 + ch * 2048 + t0, tn)

        def xn2_ap(ch, t0, tn):
            return ar(V0 + ch * 2048 + t0, tn)

        def v_ap(ch, t0, tn):
            return ar(V0 + ch * 2048 + t0, tn)

        def acc_ap(ch, t0, tn):
            return ar(ACC0 + ch * 2048 + t0, tn)

        bank8 = [lambda: PF[:, 0:512], lambda: PF[:, 512:1024],
                 lambda: PF[:, 1024:1536], lambda: PF[:, 1536:2048],
                 lambda: PZA[:, 0:512], lambda: PZA[:, 512:1024],
                 lambda: PZB[:, 0:512], lambda: PZB[:, 512:1024]]

        def tap(name, off, dep_ops, n=16384):
            if name in DEBUG_TAPS:
                return P.add("sync", (lambda e, name=name, off=off, n=n: e.dma_start(
                    out=dtaps[name][:, 0:n], in_=ar(off, n))), deps=dep_ops,
                    is_dma=True)
            return None

        ld = {}
        for nm, dst, src in [
                ("bias", BP[:], dbias), ("ident", IDN[:], dident),
                ("ones", ONER[:], dones), ("onesb", ONEB[:], donesb),
                ("outw", OUTW[:], doutw),
                ("wfy", ar(WFY_O, 512), dwfy), ("wfx", ar(WFX_O, 512), dwfx),
                ("wfy2", ar(WFY2_O, 1024), dwfy2), ("wfx2", ar(WFX2_O, 1024), dwfx2),
                ("outwT", ar(OWT_O, 1024), doutwT)]:
            ld[nm] = P.add("sync", (lambda e, dst=dst, src=src:
                                    e.dma_start(out=dst, in_=src)), is_dma=True)

        wq_ld = []
        cw_ev = []
        for cch in range(8):
            slot = W0 + (cch % 2) * 1024
            deps = [cw_ev[cch - 2]] if cch >= 2 else []
            wq_ld.append(P.add("sync", (lambda e, s=slot, c=cch: e.dma_start(
                out=ar(s, 1024), in_=dwqT[c])), deps=deps, is_dma=True))

            def cwmm(e, s=slot):
                for d in range(8):
                    e.matmul(PZA[:, 0:64], ar(s + d * 128, 128),
                             ar(WFY_O + d * 64, 64), start=(d == 0), stop=False)
                for d in range(8):
                    last = e.matmul(PZA[:, 64:128], ar(s + d * 128, 128),
                                    ar(WFX_O + d * 64, 64), start=(d == 0), stop=(d == 7))
                return last
            deps = [wq_ld[cch], ld["wfy"], ld["wfx"]]
            if cch >= 1:
                deps.append(cw_ev[cch - 1])
            mm = P.add("tensor", cwmm, deps=deps)

            def cwev(e, c=cch):
                e.copy(ar(CW2_O + c * 128, 64), PZA[:, 0:64])
                e.copy(ar(CW2_O + c * 128 + 64, 64), PZA[:, 0:64])
                e.copy(ar(CW2_O + 1024 + c * 128, 64), PZA[:, 64:128])
                return e.copy(ar(CW2_O + 1024 + c * 128 + 64, 64), PZA[:, 64:128])
            cw_ev.append(P.add("scalar", cwev, deps=[mm]))

        def ow_mm(e):
            for cch in range(8):
                e.matmul(PZB[:, 0:64], ar(OWT_O + cch * 128, 128),
                         ar(WFY_O + cch * 64, 64), start=(cch == 0), stop=False)
            for cch in range(8):
                last = e.matmul(PZB[:, 64:128], ar(OWT_O + cch * 128, 128),
                                ar(WFX_O + cch * 64, 64), start=(cch == 0), stop=(cch == 7))
            return last
        ow_op = P.add("tensor", ow_mm, deps=[ld["outwT"], ld["wfy"], ld["wfx"]])

        def ow_ev(e):
            for half, srccols in ((0, PZB[:, 0:64]), (1, PZB[:, 0:64])):
                e.copy(OW2Y[:, 0:64].bitcast(BF16)[:, half * 64:half * 64 + 64],
                       srccols)
            for half, srccols in ((0, PZB[:, 64:128]), (1, PZB[:, 64:128])):
                last = e.copy(OW2X[:, 0:64].bitcast(BF16)[:, half * 64:half * 64 + 64],
                              srccols)
            return last
        ow_ev_op = P.add("scalar", ow_ev, deps=[ow_op])

        def ow_lo(e):
            for half in range(2):
                e.tensor_tensor(OW2Y[:, 64:128].bitcast(BF16)[:, half * 64:half * 64 + 64],
                                PZB[:, 0:64],
                                OW2Y[:, 0:64].bitcast(BF16)[:, half * 64:half * 64 + 64],
                                ALU.subtract)
                e.drain()
                last = e.tensor_tensor(OW2X[:, 64:128].bitcast(BF16)[:, half * 64:half * 64 + 64],
                                       PZB[:, 64:128],
                                       OW2X[:, 0:64].bitcast(BF16)[:, half * 64:half * 64 + 64],
                                       ALU.subtract)
                if half == 0:
                    e.drain()
            return last
        ow_lo_op = P.add("vector", ow_lo, deps=[ow_ev_op])

        def cb_mm(e):
            for cch in range(8):
                e.matmul(PZB[0:1, 128:256], BP[:, BC["obc"] + cch:BC["obc"] + cch + 1],
                         ar(WFY2_O + cch * 128, 128), start=(cch == 0), stop=False)
            for cch in range(8):
                e.matmul(PZB[0:1, 256:384], BP[:, BC["obc"] + cch:BC["obc"] + cch + 1],
                         ar(WFX2_O + cch * 128, 128), start=(cch == 0), stop=False)
            for cch in range(8):
                e.matmul(PZB[:, 384:385], ar(WFY2_O + cch * 128, 128),
                         BP[:, BC["id"] + cch:BC["id"] + cch + 1],
                         start=(cch == 0), stop=False)
            for cch in range(8):
                e.matmul(PZB[:, 385:386], ar(WFY2_O + cch * 128, 128),
                         BP[:, BC["bq"] + cch:BC["bq"] + cch + 1],
                         start=(cch == 0), stop=False)
            for cch in range(8):
                last = e.matmul(PZB[:, 386:387], ar(WFX2_O + cch * 128, 128),
                                BP[:, BC["bq"] + cch:BC["bq"] + cch + 1],
                                start=(cch == 0), stop=(cch == 7))
            return last
        cb_op = P.add("tensor", cb_mm,
                      deps=[ld["bias"], ld["wfy2"], ld["wfx2"], ow_ev_op,
                            ow_lo_op])

        def cb_ev(e):
            e.copy(CBY2[:, 0:64].bitcast(BF16), PZB[0:1, 128:256])
            e.copy(CBX2[:, 0:64].bitcast(BF16), PZB[0:1, 256:384])
            e.copy(SM[:, 2:3], PZB[:, 384:385])
            e.copy(SM[:, 4:5], PZB[:, 385:386])
            return e.copy(SM[:, 5:6], PZB[:, 386:387])
        cb_ev_op = P.add("scalar", cb_ev, deps=[cb_op])

        def cb_lo(e):
            e.tensor_tensor(CBY2[:, 64:128].bitcast(BF16), PZB[0:1, 128:256],
                            CBY2[:, 0:64].bitcast(BF16), ALU.subtract)
            e.drain()
            return e.tensor_tensor(CBX2[:, 64:128].bitcast(BF16),
                                   PZB[0:1, 256:384],
                                   CBX2[:, 0:64].bitcast(BF16), ALU.subtract)
        cb_lo_op = P.add("vector", cb_lo, deps=[cb_ev_op])

        def cc_build(e):
            e.tensor_tensor(SM[:, 0:1], bp("fb2"), bp("pihalf"), ALU.add)
            e.drain()
            e.tensor_tensor(SM[:, 1:2], SM[:, 0:1], SM[:, 4:5], ALU.add)
            e.drain()
            e.tensor_tensor(SM[:, 1:2], SM[:, 1:2], SM[:, 5:6], ALU.add)
            e.drain()
            e.tensor_tensor(SM[:, 3:4], SM[:, 2:3], SM[:, 4:5], ALU.subtract)
            e.scalar_tensor_tensor(BP[:, BC["sb2c"]:BC["sb2c"] + 8],
                                   BP[:, BC["sb2c"]:BC["sb2c"] + 8], 0.5,
                                   BP[:, BC["projb"]:BC["projb"] + 8],
                                   ALU.mult, ALU.add)
            return e.tensor_scalar(BP[:, BC["b2q"]:BC["b2q"] + 64],
                                   BP[:, BC["b2q"]:BC["b2q"] + 64],
                                   0.25, None, ALU.mult)
        cc_op = P.add("vector", cc_build, deps=[cb_ev_op, cb_lo_op, ld["bias"]])

        if TRUNC == "p0":
            _finish()
            return nc

        XT_O = [P1_O, P1_O + 1024]
        XCB_O = [P1_O + 2048, P1_O + 3072]
        XNT_O = [P1_O + 4096, P1_O + 5120]
        TRH_O = P1_O + 6144
        MC = [8, 9]
        VC = [10, 11]
        x_ld, ln_vst, ln_vn, ln_ttr, ln_ev = [], [], [], [], []

        def emit_xld(j, deps):
            x_ld.append(P.add("sync", (lambda e, j=j: e.dma_start(
                out=ar(XT_O[j % 2], 1024), in_=dx[j * 128:(j + 1) * 128, :])),
                deps=deps, is_dma=True))
        emit_xld(0, [])
        emit_xld(1, [])
        for j in range(16):
            xt_o, xc_o, xnt_o = XT_O[j % 2], XCB_O[j % 2], XNT_O[j % 2]
            mcol, vcol = MC[j % 2], VC[j % 2]

            def v_stats(e, xt_o=xt_o, xc_o=xc_o, mcol=mcol):
                e.tensor_reduce(SM[:, mcol:mcol + 1], ar(xt_o, 1024), AX.X, ALU.add)
                e.drain()
                e.tensor_scalar(SM[:, mcol:mcol + 1], SM[:, mcol:mcol + 1],
                                1.0 / C, None, ALU.mult)
                e.drain()
                return e.tensor_scalar(ar(xc_o, 1024), ar(xt_o, 1024),
                                       SM[:, mcol:mcol + 1], None, ALU.subtract)
            deps = [x_ld[j], cc_op]
            if j >= 2:
                deps += [ln_vn[j - 2]]
            vst = P.add("vector", v_stats, deps=deps)
            ln_vst.append(vst)
            if j + 2 < 16:
                emit_xld(j + 2, [vst])

            def s_sq(e, xc_o=xc_o, vcol=vcol):
                e.activation(ar(TRH_O, 1024), ar(xc_o, 1024), AF.Square,
                             accum_out=SM[:, vcol:vcol + 1])
                return e.activation(SM[:, vcol:vcol + 1], SM[:, vcol:vcol + 1],
                                    AF.Sqrt, bias=bp("epsln"), scale=bp("invC"))
            sq = P.add("scalar", s_sq, deps=[vst])

            def v_norm(e, xc_o=xc_o, xnt_o=xnt_o, vcol=vcol):
                e.reciprocal(SM[:, vcol:vcol + 1], SM[:, vcol:vcol + 1])
                e.drain()
                return e.tensor_scalar(ar(xnt_o, 1024), ar(xc_o, 1024),
                                       SM[:, vcol:vcol + 1], None, ALU.mult)
            vdeps = [sq]
            if j >= 2:
                vdeps.append(ln_ttr[j - 2])
            vn = P.add("vector", v_norm, deps=vdeps)
            ln_vn.append(vn)

            def t_tr(e, xnt_o=xnt_o, j=j):
                for q in range(8):
                    if j % 2 == 0:
                        ps = PZA if q < 4 else PZB
                        dst = ps[:, (q % 4) * 128:(q % 4) * 128 + 128]
                    else:
                        dst = PF[:, q * 128:q * 128 + 128]
                    last = e.transpose(dst, ar(xnt_o + q * 128, 128), IDN[:])
                return last
            tdeps = [vn, ld["ident"]]
            if j >= 2:
                tdeps.append(ln_ev[j - 2])
            if j <= 1:
                tdeps += [cw_ev[7], cb_ev_op, ow_ev_op, cb_lo_op]
            ttr = P.add("tensor", t_tr, deps=tdeps)
            ln_ttr.append(ttr)

            def s_ev(e, j=j):
                for h in range(2):
                    if j % 2 == 0:
                        src = (PZA if h == 0 else PZB)[:, 0:512] \
                            .rearrange("p (a b) -> p a b", b=128)
                    else:
                        src = PF[:, h * 512:h * 512 + 512] \
                            .rearrange("p (a b) -> p a b", b=128)
                    dst = bass.AP(ART, XN0 + h * 8192 + j * 128,
                                  [[ARENA_N, 128], [2048, 4], [1, 128]])
                    last = e.copy(dst, src)
                return last
            ln_ev.append(P.add("scalar", s_ev, deps=[ttr]))
        p1_last = ln_ev[15]
        tap_xn = tap("xn", XN0, [p1_last, ln_ev[14]])

        if TRUNC == "p1":
            _finish()
            return nc

        wv_ld = []
        vmm = {}
        v_ev = {}
        for m in range(8):
            deps = [cw_ev[7]]
            if m >= 2:
                deps.append(vmm[(m - 2, 3)])
            wv_ld.append(P.add("sync", (lambda e, m=m: e.dma_start(
                out=ar(W0 + (m % 2) * 1024, 1024), in_=dwv[m])),
                deps=deps, is_dma=True))
            for q in range(4):
                bk = bank8[4 + q]

                def f_mm(e, m=m, q=q, bk=bk):
                    for k in range(8):
                        last = e.matmul(bk(), ar(W0 + (m % 2) * 1024 + k * 128, 128),
                                        xn_ap(k, q * 512, 512),
                                        start=(k == 0), stop=(k == 7))
                    return last
                deps = [wv_ld[m], p1_last, ln_ev[14]]
                if (m - 1, q) in v_ev:
                    deps.append(v_ev[(m - 1, q)])
                vmm[(m, q)] = P.add("tensor", f_mm, deps=deps)

                def f_ev(e, m=m, q=q, bk=bk):
                    return e.activation(v_ap(m, q * 512, 512), bk(), AF.Identity,
                                        bias=bp("bv", m))
                dps = [vmm[(m, q)]]
                if m <= 1:
                    dps.append(cc_op)
                v_ev[(m, q)] = P.add("scalar", f_ev, deps=dps)
        tap("v", V0, [v_ev[(7, 3)], v_ev[(6, 3)]])

        q0_mm = []
        for b in range(4):
            def f_q0(e, b=b):
                lo, hi = b * 512, (b + 1) * 512
                for d in range(8):
                    e.matmul(PF[:, lo:hi], ar(CW2_O + 1024 + d * 128, 128),
                             xn_ap(d, lo, 512), start=(d == 0), stop=False)
                ylo = max(1, lo)
                for d in range(8):
                    last = e.matmul(PF[:, ylo:hi], ar(CW2_O + d * 128, 128),
                                    xn_ap(d, ylo - 1, hi - ylo),
                                    start=False, stop=(d == 7))
                return last
            q0_mm.append(P.add("tensor", f_q0,
                               deps=[p1_last, ln_ev[14], cw_ev[7]]))

        presplit_done = {}
        reload_gate = [None]
        ps_pieces = []
        for m in range(16):
            ps_pieces.append(("sw1", m, dsw1[m], m * 1024, 1024, False))
            ps_pieces.append(("sw3", m, dsw3[m], 16384 + m * 1024, 1024, False))
        for c_ in range(8):
            ps_pieces.append(("sw2", c_, dsw2[c_][:, 0:1024],
                              32768 + c_ * 2048, 1024, True))
            ps_pieces.append(("sw2b", c_, dsw2[c_][:, 1024:2048],
                              32768 + c_ * 2048 + 512, 1024, True))
        ps_ops = []
        ps_dmas = []
        ps_queue = list(enumerate(ps_pieces))
        ps_pos = [0]

        def emit_presplit(nmax, slot_guard=()):
            done = 0
            while ps_pos[0] < len(ps_queue) and done < nmax:
                pi, (nm_, idx_, src_, dsto_, n_, half_) = ps_queue[ps_pos[0]]
                ps_pos[0] += 1
                done += 1
                stg = None if (pi % 2 == 0) else 0
                if pi < 32:
                    m_ = pi // 2
                    slot = W0 + (m_ % 2) * 2048
                    ohi = slot + (pi % 2) * 1024
                    olo = ohi + 512
                else:
                    c_ = (pi - 32) // 2
                    slot = W0 + (c_ % 2) * 2048
                    ohi = slot + (pi % 2) * 512
                    olo = ohi + 1024

                def stg_ap(stg):
                    return KI[:].bitcast(F32) if stg is None else ar(stg, 1024)
                din_deps = [ln_vst[15], reload_gate[0]]
                if pi >= 2:
                    din_deps.append(ps_ops[pi - 2])
                din = P.add("scalar", (lambda e, stg=stg, src_=src_:
                                       e.dma_start(out=stg_ap(stg), in_=src_)),
                            deps=din_deps, is_dma=True)

                def f_ps_hi(e, stg=stg, ohi=ohi):
                    return e.copy(bf(ohi, 1024), stg_ap(stg))
                hi_op = P.add("scalar", f_ps_hi, deps=[din] + list(slot_guard))

                def f_ps(e, stg=stg, ohi=ohi, olo=olo):
                    return e.tensor_tensor(bf(olo, 1024), stg_ap(stg),
                                           bf(ohi, 1024), ALU.subtract)
                vop = P.add("gpsimd", f_ps, deps=[din, hi_op])
                ps_ops.append(vop)
                if nm_ in ("sw2", "sw2b"):
                    P.add("sync", (lambda e, ohi=ohi, dsto_=dsto_:
                                   e.dma_start(out=dswsp[:, dsto_:dsto_ + 512],
                                               in_=ar(ohi, 512))),
                          deps=[vop], is_dma=True)
                    dout_ = P.add("sync", (lambda e, olo=olo, dsto_=dsto_:
                                  e.dma_start(out=dswsp[:, dsto_ + 1024:dsto_ + 1536],
                                              in_=ar(olo, 512))),
                                  deps=[vop], is_dma=True)
                else:
                    dout_ = P.add("sync", (lambda e, ohi=ohi, dsto_=dsto_:
                                  e.dma_start(out=dswsp[:, dsto_:dsto_ + 1024],
                                              in_=ar(ohi, 1024))),
                                  deps=[vop], is_dma=True)
                ps_dmas.append(dout_)
                key = ("sw2", idx_) if nm_ in ("sw2", "sw2b") else (nm_, idx_)
                presplit_done[key] = dout_


        xs_ops = []
        xs_dmas = []
        for kk in range(8):
            def f_xs(e, kk=kk):
                hi = bf(XSP_O, 2048).rearrange("p (r g) -> p g r", g=256)
                lo = bf(XSP_O + 1024, 2048).rearrange("p (r g) -> p g r", g=256)
                xsrc = ar(XN0 + kk * 2048, 2048).rearrange("p (g r) -> p g r", r=8)
                e.tensor_copy(hi, xsrc)
                e.drain()
                return e.scalar_tensor_tensor(lo, hi, -1.0, xsrc,
                                              ALU.mult, ALU.add)
            vdeps = [p1_last, ln_ev[14]]
            if kk >= 1:
                vdeps.append(xs_dmas[-1])
            vop = P.add("vector", f_xs, deps=vdeps)
            xs_ops.append(vop)
            P.add("sync", (lambda e, kk=kk: e.dma_start(
                out=dxnsp[:, kk * 1024:kk * 1024 + 1024],
                in_=ar(XSP_O, 1024))), deps=[vop, tap_xn], is_dma=True)
            xs_dmas.append(P.add("sync", (lambda e, kk=kk: e.dma_start(
                out=dxnsp[:, 8192 + kk * 1024:8192 + kk * 1024 + 1024],
                in_=ar(XSP_O + 1024, 1024))), deps=[vop, tap_xn], is_dma=True))
        spill = P.add("sync", lambda e: None,
                      deps=[xs_dmas[-1], q0_mm[3]], is_dma=False)

        if TRUNC == "p4":
            _finish()
            return nc

        state = {"ssch": None, "sscl": None, "srh": None, "srl": None,
                 "pf_free": None, "sc": None,
                 "sc_cons": [], "s11": None, "ystt": []}

        def emit_iter(k):
            n = 1 << (k - 1)
            first = (k == 1)
            last = (k == 11)
            if first:
                fmm = q0_mm
                hb = SM[:, 3:4]
                cc = SM[:, 1:2]
            else:
                hb = SM[:, 2:3]
                cc = SM[:, 0:1]
                fmm = []
                SSCH = bf(SSCH_O, 2048)
                SSCL = bf(SSCL_O, 2048)
                SRH = AR[0:1, SROWH_O:SROWH_O + 1024].bitcast(BF16)
                SRL = AR[0:1, SROWL_O:SROWL_O + 1024].bitcast(BF16)
                OWXH = OW2X[:, 0:64].bitcast(BF16)
                OWXL = OW2X[:, 64:128].bitcast(BF16)
                OWYH = OW2Y[:, 0:64].bitcast(BF16)
                OWYL = OW2Y[:, 64:128].bitcast(BF16)
                CBXH = CBX2[:, 0:64].bitcast(BF16)
                CBXL = CBX2[:, 64:128].bitcast(BF16)
                CBYH = CBY2[:, 0:64].bitcast(BF16)
                CBYL = CBY2[:, 64:128].bitcast(BF16)
                fmm = []
                fh_prev = []
                for bp_ in range(2):
                    def f_ab_h(e, n=n, bp_=bp_):
                        last = None
                        for b in (2 * bp_, 2 * bp_ + 1):
                            lo, hi = b * 512, (b + 1) * 512
                            ylo = max(n, lo)
                            e.matmul(PF[:, lo:hi], OWXH, SSCH[:, lo:hi],
                                     start=True, stop=False)
                            last = e.matmul(PF[:, lo:hi], CBXH, SRH[:, lo:hi],
                                            start=False, stop=False)
                            if ylo < hi:
                                e.matmul(PF[:, ylo:hi], OWYH,
                                         SSCH[:, ylo - n:hi - n],
                                         start=False, stop=False)
                                last = e.matmul(PF[:, ylo:hi], CBYH,
                                                SRH[:, ylo - n:hi - n],
                                                start=False, stop=False)
                        return last
                    hdeps = [state["ssch"][bp_], state["srh"]] \
                        + state["pf_free"] + fh_prev
                    if bp_ == 1:
                        hdeps.append(state["ssch"][0])
                    fh = P.add("tensor", f_ab_h, deps=hdeps)
                    fh_prev = [fh]
                    fmm.append(fh)

                def f_ab_l(e, n=n):
                    last = None
                    for b in range(4):
                        lo, hi = b * 512, (b + 1) * 512
                        ylo = max(n, lo)
                        e.matmul(PF[:, lo:hi], OWXH, SSCL[:, lo:hi],
                                 start=False, stop=False)
                        e.matmul(PF[:, lo:hi], OWXL, SSCH[:, lo:hi],
                                 start=False, stop=False)
                        e.matmul(PF[:, lo:hi], CBXH, SRL[:, lo:hi],
                                 start=False, stop=False)
                        last = e.matmul(PF[:, lo:hi], CBXL, SRH[:, lo:hi],
                                        start=False, stop=(ylo >= hi))
                        if ylo < hi:
                            e.matmul(PF[:, ylo:hi], OWYH,
                                     SSCL[:, ylo - n:hi - n],
                                     start=False, stop=False)
                            e.matmul(PF[:, ylo:hi], OWYL,
                                     SSCH[:, ylo - n:hi - n],
                                     start=False, stop=False)
                            e.matmul(PF[:, ylo:hi], CBYH,
                                     SRL[:, ylo - n:hi - n],
                                     start=False, stop=False)
                            last = e.matmul(PF[:, ylo:hi], CBYL,
                                            SRH[:, ylo - n:hi - n],
                                            start=False, stop=True)
                    return last
                fl = P.add("tensor", f_ab_l,
                           deps=fmm + state["sscl"] + [state["srl"]])
                fmm = fmm + [fl]
            def f_head(e, n=n, hb=hb):
                return e.activation(PF[:, 0:n], PF[:, 0:n], AF.Identity, bias=hb)
            head = P.add("scalar", f_head, deps=list(fmm))

            vr_ops = []
            for h in range(2):
                def f_u(e, h=h, cc=cc):
                    return e.tensor_scalar(ar(U_O + h * 1024, 1024),
                                           PF[:, h * 1024:(h + 1) * 1024],
                                           cc, INV2PI, ALU.add, ALU.mult)
                du = [head] + list(fmm)
                if first:
                    du.append(spill)
                uop = P.add("vector", f_u, deps=du)

                def f_k(e, h=h):
                    return e.tensor_copy(KI[:], ar(U_O + h * 1024, 1024))
                kop = P.add("vector", f_k, deps=[uop])

                def f_w(e, h=h):
                    return e.scalar_tensor_tensor(ar(U_O + h * 1024, 1024), KI[:],
                                                  -1.0, ar(U_O + h * 1024, 1024),
                                                  ALU.mult, ALU.add)
                vr_ops.append(P.add("vector", f_w, deps=[kop]))

            sin_h = []
            for h in range(2):
                def f_sin(e, h=h):
                    return e.activation(ar(SC_O + h * 1024, 1024),
                                        ar(U_O + h * 1024, 1024), AF.Sin,
                                        scale=bp("twopi"))
                sdeps = [vr_ops[h]] + state["sc_cons"]
                if first:
                    sdeps.append(spill)
                sin_h.append(P.add("scalar", f_sin, deps=sdeps))
            new_sc = sin_h[1]

            sc_cons = []
            piece_free = {}
            vmax = {}
            aev = {}
            for c in range(8):
                for h in range(2):
                    p = c * 2 + h
                    pz = PZA if p % 2 == 0 else PZB

                    def f_z2(e, c=c, h=h, pz=pz):
                        e.matmul(pz[:, 0:512], OUTW[:, c * 128:(c + 1) * 128],
                                 ar(SC_O + h * 1024, 512), start=True, stop=True)
                        return e.matmul(pz[:, 512:1024],
                                        OUTW[:, c * 128:(c + 1) * 128],
                                        ar(SC_O + h * 1024 + 512, 512),
                                        start=True, stop=True)
                    zdeps = [sin_h[h]]
                    if p >= 2:
                        zdeps += piece_free[p - 2]
                    zmm = P.add("tensor", f_z2, deps=zdeps)
                    sc_cons.append(zmm)

                    def f_abs(e, c=c, h=h, pz=pz):
                        dst = (M0_O if h == 0 else M1_O) if c == 0 else \
                              (A0_O if h == 0 else A1_O)
                        return e.activation(ar(dst, 1024), pz[:], AF.Abs,
                                            bias=bp("ob", c))
                    adeps = [zmm]
                    if c >= 2:
                        adeps.append(vmax[(c - 1, h)])
                    if first and c == 0:
                        adeps.append(spill)
                    a = P.add("scalar", f_abs, deps=adeps)
                    aev[(c, h)] = a
                    frees = [a]
                    if last:
                        def f_y(e, c=c, h=h, pz=pz):
                            return e.scalar_tensor_tensor(
                                v_ap(c, h * 1024, 1024), pz[:], bp("ob", c),
                                v_ap(c, h * 1024, 1024), ALU.add, ALU.mult)
                        yop = P.add("vector", f_y,
                                    deps=[zmm, a, v_ev[(c, 2 * h)],
                                          v_ev[(c, 2 * h + 1)]])
                        state["ystt"].append(yop)
                        frees.append(yop)
                    piece_free[p] = frees
                    if c >= 1:
                        def f_max(e, h=h):
                            mo = M0_O if h == 0 else M1_O
                            ao = A0_O if h == 0 else A1_O
                            return e.tensor_tensor(ar(mo, 1024), ar(mo, 1024),
                                                   ar(ao, 1024), ALU.max)
                        mdeps = [a, aev[(c - 1, h)] if c == 1 else vmax[(c - 1, h)]]
                        vmax[(c, h)] = P.add("vector", f_max, deps=mdeps)
            state["sc_cons"] = sc_cons

            rd_ops = []
            for g in range(4):
                def f_trg(e, g=g):
                    for q in range(4):
                        jt = g * 4 + q
                        h, jb = jt // 8, jt % 8
                        mo = M0_O if h == 0 else M1_O
                        last = e.transpose(PF[:, jt * 128:jt * 128 + 128],
                                           ar(mo + jb * 128, 128), IDN[:])
                    return last
                tdeps = [vmax[(7, g // 2)]] + vr_ops
                trg = P.add("tensor", f_trg, deps=tdeps)

                def f_rdg(e, g=g):
                    return e.tensor_reduce(
                        ar(MT_O + g * 4, 4),
                        PF[:, g * 512:(g + 1) * 512].rearrange(
                            "p (j q) -> p j q", q=128),
                        AX.X, ALU.max)
                rd_ops.append(P.add("vector", f_rdg, deps=[trg]))

            def f_s(e):
                e.tensor_scalar(ar(ST_O, 16), ar(MT_O, 16), EPS_MM, None, ALU.add)
                e.drain()
                return e.reciprocal(ar(ST_O, 16), ar(ST_O, 16))
            sop = P.add("vector", f_s, deps=rd_ops)

            def f_str(e):
                return e.transpose(PF[0:16, 0:128], ar(ST_O, 16), IDN[:])
            strop = P.add("tensor", f_str, deps=[sop] + rd_ops)

            def f_sre(e):
                return e.copy(AR[0:16, SRS_O:SRS_O + 64].bitcast(BF16),
                              PF[0:16, 0:128])
            sre = P.add("scalar", f_sre, deps=[strop])

            def f_srl(e):
                return e.scalar_tensor_tensor(
                    AR[0:16, SRS_O + 64:SRS_O + 128].bitcast(BF16),
                    AR[0:16, SRS_O:SRS_O + 64].bitcast(BF16), -1.0,
                    PF[0:16, 0:128], ALU.mult, ALU.add)
            srl_v = P.add("vector", f_srl, deps=[sre])

            dma_h = P.add("scalar", (lambda e: e.dma_start(
                out=AR[0:1, SROWH_O:SROWH_O + 1024].bitcast(BF16),
                in_=AR[0:16, SRS_O:SRS_O + 64].bitcast(BF16))),
                deps=[sre], is_dma=True)
            dma_l = P.add("sync", (lambda e: e.dma_start(
                out=AR[0:1, SROWL_O:SROWL_O + 1024].bitcast(BF16),
                in_=AR[0:16, SRS_O + 64:SRS_O + 128].bitcast(BF16))),
                deps=[srl_v], is_dma=True)

            spk_h = []
            for hh in range(2):
                def f_spk(e, hh=hh):
                    SRH = AR[0:1, SROWH_O:SROWH_O + 1024].bitcast(BF16)
                    SRL = AR[0:1, SROWL_O:SROWL_O + 1024].bitcast(BF16)
                    for b in (2 * hh, 2 * hh + 1):
                        sl = slice(b * 512, (b + 1) * 512)
                        e.matmul(PF[:, sl], ONEB[:], SRH[:, sl],
                                 start=True, stop=False)
                        last_ = e.matmul(PF[:, sl], ONEB[:], SRL[:, sl],
                                         start=False, stop=True)
                    return last_
                spk_h.append(P.add("tensor", f_spk,
                                   deps=[dma_h, dma_l, ld["onesb"]]))

            if not last:
                ssc32_h = []
                ssch_h = []
                sscl_h = []
                for hh in range(2):
                    def f_ssc(e, hh=hh):
                        return e.tensor_tensor(ar(SSC_O + hh * 1024, 1024),
                                               ar(SC_O + hh * 1024, 1024),
                                               PF[:, hh * 1024:(hh + 1) * 1024],
                                               ALU.mult)
                    ssc32_h.append(P.add("vector", f_ssc,
                                         deps=[spk_h[hh]] + sc_cons))

                    def f_ssch(e, hh=hh):
                        return e.copy(bf(SSCH_O + hh * 512, 1024),
                                      ar(SSC_O + hh * 1024, 1024))
                    ssch_h.append(P.add("scalar", f_ssch, deps=[ssc32_h[hh]]))

                    def f_sscl(e, hh=hh):
                        return e.tensor_tensor(bf(SSCL_O + hh * 512, 1024),
                                               ar(SSC_O + hh * 1024, 1024),
                                               bf(SSCH_O + hh * 512, 1024),
                                               ALU.subtract)
                    sscl_h.append(P.add("gpsimd", f_sscl, deps=[ssch_h[hh]]))
                state["ssch"] = ssch_h
                state["sscl"] = sscl_h
                state["srh"] = dma_h
                state["srl"] = dma_l
                state["pf_free"] = ssc32_h
                state["sc_cons"] = sc_cons + ssc32_h
            else:
                def f_s11(e):
                    return e.copy(ar(S11_O, 2048), PF[:])
                state["s11"] = P.add("scalar", f_s11, deps=spk_h)
                state["pf_free"] = [state["s11"]]
            if ("q%d" % k) in DEBUG_TAPS:
                pass
            tap("sc%d" % k, SC_O, [new_sc] + sc_cons, n=2048)

            state["sc"] = new_sc

        for k in range(1, 12):
            emit_iter(k)
            if TRUNC == ("scan%d" % k):
                _finish()
                return nc
        s11_ev = state["s11"]
        y_done = state["ystt"]

        pw_ld = []
        proj_ev = {}
        pj_ops = {}
        for m in range(8):
            deps = [q0_mm[3]]
            if m >= 2:
                deps.append(pj_ops[(m - 2, 3)])
            pw_ld.append(P.add("sync", (lambda e, m=m: e.dma_start(
                out=ar(W0 + 2048 + (m % 2) * 1024, 1024), in_=dprojw[m])),
                deps=deps, is_dma=True))
            for tt in range(4):
                bk = bank8[(m % 2) * 4 + tt]

                def f_pj(e, tt=tt, m=m, bk=bk):
                    for cch in range(8):
                        last = e.matmul(bk(), ar(W0 + 2048 + (m % 2) * 1024 + cch * 128, 128),
                                        v_ap(cch, tt * 512, 512),
                                        start=(cch == 0), stop=(cch == 7))
                    return last
                deps = [pw_ld[m], s11_ev] + y_done
                if (m - 2, tt) in proj_ev:
                    deps.append(proj_ev[(m - 2, tt)])
                pj_ops[(m, tt)] = P.add("tensor", f_pj, deps=deps)

                def f_pje(e, tt=tt, m=m, bk=bk):
                    return e.tensor_tensor(acc_ap(m, tt * 512, 512), bk(),
                                           ar(S11_O + tt * 512, 512), ALU.mult)
                proj_ev[(m, tt)] = P.add("vector", f_pje,
                                         deps=[pj_ops[(m, tt)], s11_ev])
        proj_last = proj_ev[(7, 3)]
        pj_last = pj_ops[(7, 3)]
        tap("attn", ACC0, [proj_last, proj_ev[(7, 2)]])

        if TRUNC == "p6":
            _finish()
            return nc

        reload1 = P.add("sync", (lambda e: e.dma_start(
            out=ar(V0, 8192), in_=dxnsp[:, 0:8192])), deps=[proj_last], is_dma=True)
        reload2 = P.add("sync", (lambda e: e.dma_start(
            out=ar(V0 + 8192, 8192), in_=dxnsp[:, 8192:16384])),
            deps=[proj_last], is_dma=True)
        reload_gate[0] = reload2

        def xnh(k, t0, tn):
            return bf(V0 + k * 1024, 2048)[:, t0:t0 + tn]

        def xnl(k, t0, tn):
            return bf(V0 + 8192 + k * 1024, 2048)[:, t0:t0 + tn]

        def mm3(e, psum, wslot, nk, rhs_hi, rhs_lo, tsz):
            first = True
            last = None
            for kk in range(nk):
                whi = bf(wslot + kk * 64, 128)
                wlo = bf(wslot + nk * 64 + kk * 64, 128)
                last = e.matmul(psum, whi, rhs_hi(kk), start=first, stop=False)
                first = False
                e.matmul(psum, whi, rhs_lo(kk), start=False, stop=False)
                last = e.matmul(psum, wlo, rhs_hi(kk), start=False,
                                stop=(kk == nk - 1))
            return last

        sw_ld = {}
        g_ops = {}
        silu_ops = {}
        s2mm_ops = {}
        s2ev_ops = {}
        for tt in range(4):
            for m in range(16):
                slot = W0 + (m % 2) * 2048
                if tt == 0:
                    if m == 0:
                        sg = [vmm[(7, 3)]]
                    elif m == 1:
                        sg = [pj_last]
                    else:
                        sg = [g_ops[(0, m - 2)], ps_dmas[2 * (m - 2)],
                              ps_dmas[2 * (m - 2) + 1]]
                    emit_presplit(2, slot_guard=sg)
                else:
                    ldd = []
                    if m >= 2:
                        ldd.append(g_ops[(tt, m - 2)])
                    else:
                        ldd.append(s2mm_ops[(tt - 1, 7)])
                    ldd.append(presplit_done[("sw1", m)])
                    ldd.append(presplit_done[("sw3", m)])
                    sw_ld[(tt, m, 1)] = P.add("sync", (lambda e, m=m, slot=slot:
                                                       e.dma_start(out=ar(slot, 1024),
                                                                   in_=dswsp[:, m * 1024:(m + 1) * 1024])),
                                              deps=ldd, is_dma=True)
                    sw_ld[(tt, m, 3)] = P.add("sync", (lambda e, m=m, slot=slot:
                                                       e.dma_start(out=ar(slot + 1024, 1024),
                                                                   in_=dswsp[:, 16384 + m * 1024:16384 + (m + 1) * 1024])),
                                              deps=ldd, is_dma=True)

                def f_h1(e, tt=tt, m=m, slot=slot):
                    return mm3(e, PZA[:, 0:512], slot, 8,
                               lambda k: xnh(k, tt * 512, 512),
                               lambda k: xnl(k, tt * 512, 512), 512)
                d1 = [ps_ops[2 * m] if tt == 0 else sw_ld[(tt, m, 1)],
                      reload1, reload2]
                if (tt, m - 1) in silu_ops:
                    d1.append(silu_ops[(tt, m - 1)])
                elif tt >= 1:
                    d1.append(silu_ops[(tt - 1, 15)])
                h1 = P.add("tensor", f_h1, deps=d1)

                def f_h3(e, tt=tt, m=m, slot=slot):
                    return mm3(e, PZB[:, 0:512], slot + 1024, 8,
                               lambda k: xnh(k, tt * 512, 512),
                               lambda k: xnl(k, tt * 512, 512), 512)
                d3 = [ps_ops[2 * m + 1] if tt == 0 else sw_ld[(tt, m, 3)],
                      reload1, reload2]
                if (tt, m - 1) in g_ops:
                    d3.append(g_ops[(tt, m - 1)])
                elif tt >= 1:
                    d3.append(g_ops[(tt - 1, 15)])
                h3 = P.add("tensor", f_h3, deps=d3)

                def f_silu(e, m=m):
                    return e.activation(ar(H1S_O, 512), PZA[:, 0:512], AF.Silu,
                                        bias=bp("sb1", m))
                sdeps = [h1]
                if (tt, m - 1) in g_ops:
                    sdeps.append(g_ops[(tt, m - 1)])
                silu_ops[(tt, m)] = P.add("scalar", f_silu, deps=sdeps)

                def f_g(e, tt=tt, m=m):
                    e.scalar_tensor_tensor(ar(GT_O, 512), PZB[:, 0:512],
                                           bp("sb3", m), ar(H1S_O, 512),
                                           ALU.add, ALU.mult)
                    e.drain()
                    e.tensor_copy(bf(GHI_O + m * 256, 512), ar(GT_O, 512))
                    e.drain()
                    return e.scalar_tensor_tensor(bf(GLO_O + m * 256, 512),
                                                  bf(GHI_O + m * 256, 512), -1.0,
                                                  ar(GT_O, 512), ALU.mult, ALU.add)
                gd = [h3, silu_ops[(tt, m)]]
                if tt >= 1:
                    gd.append(s2mm_ops[(tt - 1, 7)])
                if (tt, m - 1) in g_ops:
                    gd.append(g_ops[(tt, m - 1)])
                g_ops[(tt, m)] = P.add("vector", f_g, deps=gd)
            for cc_ in range(8):
                slot = W0 + (cc_ % 2) * 2048
                if tt == 0:
                    if cc_ <= 1:
                        sg = [g_ops[(0, 14 + cc_)], ps_dmas[28 + 2 * cc_],
                              ps_dmas[29 + 2 * cc_]]
                    else:
                        sg = [s2mm_ops[(0, cc_ - 2)], ps_dmas[32 + 2 * (cc_ - 2)],
                              ps_dmas[33 + 2 * (cc_ - 2)]]
                    emit_presplit(2, slot_guard=sg)
                else:
                    ldd = [g_ops[(tt, 15)] if cc_ <= 1 else s2mm_ops[(tt, cc_ - 2)],
                           presplit_done[("sw2", cc_)]]
                    sw_ld[(tt, cc_, 2)] = P.add("sync", (lambda e, c=cc_, slot=slot:
                                                         e.dma_start(out=ar(slot, 2048),
                                                                     in_=dswsp[:, 32768 + c * 2048:32768 + (c + 1) * 2048])),
                                                deps=ldd, is_dma=True)

                def f_s2(e, tt=tt, c=cc_, slot=slot):
                    return mm3(e, bank8[c % 4](), slot, 16,
                               lambda k: bf(GHI_O + k * 256, 512),
                               lambda k: bf(GLO_O + k * 256, 512), 512)
                d2 = [ps_ops[33 + 2 * cc_] if tt == 0 else sw_ld[(tt, cc_, 2)],
                      g_ops[(tt, 15)]]
                if (tt, cc_ - 4) in s2ev_ops:
                    d2.append(s2ev_ops[(tt, cc_ - 4)])
                elif tt >= 1 and (tt - 1, cc_ + 4) in s2ev_ops:
                    d2.append(s2ev_ops[(tt - 1, cc_ + 4)])
                s2mm_ops[(tt, cc_)] = P.add("tensor", f_s2, deps=d2)

                tmpo = 4608 + ((tt * 8 + cc_) % 2) * 512

                def f_s2t(e, tt=tt, c=cc_, tmpo=tmpo):
                    return e.tensor_scalar(ar(tmpo, 512), bank8[c % 4](),
                                           0.5, bp("sb2c", c),
                                           ALU.mult, ALU.add)
                ts_op = P.add("vector", f_s2t,
                              deps=[s2mm_ops[(tt, cc_)],
                                    s2ev_ops.get((tt, cc_ - 2)) or
                                    s2ev_ops.get((tt - 1, cc_ + 6))])

                def f_s2e(e, tt=tt, c=cc_, tmpo=tmpo):
                    for half in range(2):
                        rr = 2 * tt + half
                        accsl = AR[:, ACC0 + c * 2048 + rr:ACC0 + (c + 1) * 2048:8]
                        last = e.tensor_tensor(accsl, accsl,
                                               ar(tmpo + half * 256, 256), ALU.add)
                        if half == 0:
                            e.drain()
                    return last
                s2ev_ops[(tt, cc_)] = P.add("vector", f_s2e,
                                            deps=[ts_op, proj_ev[(cc_, tt)]])
        presplit_last = ps_dmas[-1]
        shared_last = s2ev_ops[(3, 7)]
        tap("acc_sh", ACC0, [shared_last, s2ev_ops[(3, 6)]])

        if TRUNC == "p2":
            _finish()
            return nc

        hmm_ops = {}
        gel_ops = {}
        hsp_ops = {}
        ymm_ops = {}
        yev_ops = {}
        wq_lds = {}
        slot_consumer = {0: s2mm_ops[(3, 6)], 1: s2mm_ops[(3, 7)]}

        moe_split_cur = [None]

        def moe_load(ex, qi, extra_deps):
            slot = W0 + (qi % 2) * 2048
            src = dw1[ex][:, (qi % 4) * 2048:(qi % 4) * 2048 + 2048] if qi < 4 \
                else dw2[ex][:, (qi - 4) * 2048:(qi - 4) * 2048 + 2048]
            prev_a, prev_b = moe_split_cur[0] if moe_split_cur[0] else ([], [])
            din_a = P.add("sync", (lambda e, src=src:
                                   e.dma_start(out=ar(STG3_O, 1024),
                                               in_=src[:, 0:1024])),
                          deps=list(prev_a) + list(extra_deps), is_dma=True)
            din_b = P.add("sync", (lambda e, src=src:
                                   e.dma_start(out=ar(STG3_O + 1024, 1024),
                                               in_=src[:, 1024:2048])),
                          deps=list(prev_b) + list(extra_deps), is_dma=True)

            def f_hi_a(e, slot=slot):
                return e.copy(bf(slot, 1024), ar(STG3_O, 1024))
            hi_a = P.add("scalar", f_hi_a, deps=[din_a] + list(extra_deps))

            def f_hi_b(e, slot=slot):
                return e.copy(bf(slot + 512, 1024), ar(STG3_O + 1024, 1024))
            hi_b = P.add("scalar", f_hi_b, deps=[din_b] + list(extra_deps))

            def f_sp_a(e, slot=slot):
                return e.tensor_tensor(bf(slot + 1024, 1024),
                                       ar(STG3_O, 1024),
                                       bf(slot, 1024), ALU.subtract)
            sp_a = P.add("gpsimd", f_sp_a, deps=[hi_a])

            def f_sp_b(e, slot=slot):
                return e.tensor_tensor(bf(slot + 1536, 1024),
                                       ar(STG3_O + 1024, 1024),
                                       bf(slot + 512, 1024), ALU.subtract)
            sp_b = P.add("vector", f_sp_b, deps=[hi_b])
            moe_split_cur[0] = ([hi_a, sp_a], [hi_b, sp_b])
            wq_lds[(ex, qi)] = [sp_a, sp_b]

        def moe_hmm(ex, m):
            ra, rb = RA[ex], RB[ex]
            slot = W0 + ((m // 2) % 2) * 2048
            toff = (m % 2) * 8 * 64
            bk = bank8[4 + m % 4]

            def f_hmm(e, m=m, slot=slot, toff=toff, ra=ra, rb=rb, bk=bk):
                b = bk()
                first = True
                for kk in range(8):
                    whi = bf(slot + toff + kk * 64, 128)
                    wlo = bf(slot + 1024 + toff + kk * 64, 128)
                    for which, rr in ((0, ra), (1, rb)):
                        rhi = bf(V0 + kk * 1024 + rr * 128, 256)
                        rlo = bf(V0 + 8192 + kk * 1024 + rr * 128, 256)
                        sl = b[:, which * 256:which * 256 + 256]
                        e.matmul(sl, whi, rhi, start=first, stop=False)
                        first = False
                        e.matmul(sl, whi, rlo, start=False, stop=False)
                        last = e.matmul(sl, wlo, rhi, start=False,
                                        stop=(kk == 7 and which == 1))
                return last
            hd = list(wq_lds[(ex, m // 2)])
            if (ex, m - 4) in gel_ops:
                hd.append(gel_ops[(ex, m - 4)])
            elif ex >= 1:
                hd.append(yev_ops[(ex - 1, ((m % 4) + 4, 1))])
            else:
                hd += [silu_ops[(3, 15)], g_ops[(3, 15)]]
            hmm_ops[(ex, m)] = P.add("tensor", f_hmm, deps=hd)
            slot_consumer[(m // 2) % 2] = hmm_ops[(ex, m)]

            def f_gel(e, ex=ex, m=m, bk=bk):
                return e.activation(ar(HTMP_O, 512), bk(), AF.Gelu,
                                    bias=bp("b1", ex * 8 + m))
            gdeps = [hmm_ops[(ex, m)]]
            if (ex, m - 1) in hsp_ops:
                gdeps.append(hsp_ops[(ex, m - 1)])
            elif ex >= 1:
                gdeps.append(hsp_ops[(ex - 1, 7)])
            gel_ops[(ex, m)] = P.add("scalar", f_gel, deps=gdeps)

            def f_hhi(e, m=m):
                return e.copy(bf(HBH_O + m * 256, 512), ar(HTMP_O, 512))
            hspd = [gel_ops[(ex, m)]]
            if ex >= 1:
                hspd.append(ymm_ops[(ex - 1, 7)])
            hhi = P.add("scalar", f_hhi, deps=hspd)

            def f_hlo(e, m=m):
                return e.tensor_tensor(bf(HBL_O + m * 256, 512),
                                       ar(HTMP_O, 512),
                                       bf(HBH_O + m * 256, 512), ALU.subtract)
            hsp_ops[(ex, m)] = P.add("gpsimd", f_hlo, deps=[hhi])

        def moe_ymm(ex, c):
            ra, rb = RA[ex], RB[ex]
            slot = W0 + ((c // 2) % 2) * 2048
            toff = (c % 2) * 8 * 64
            bk = bank8[4 + c % 4]

            def f_ymm(e, slot=slot, toff=toff, bk=bk):
                b = bk()
                first = True
                for kk in range(8):
                    whi = bf(slot + toff + kk * 64, 128)
                    wlo = bf(slot + 1024 + toff + kk * 64, 128)
                    rhi = bf(HBH_O + kk * 256, 512)
                    rlo = bf(HBL_O + kk * 256, 512)
                    e.matmul(b[:, 0:512], whi, rhi, start=first, stop=False)
                    first = False
                    e.matmul(b[:, 0:512], whi, rlo, start=False, stop=False)
                    last = e.matmul(b[:, 0:512], wlo, rhi, start=False,
                                    stop=(kk == 7))
                return last
            yd = list(wq_lds[(ex, 4 + c // 2)]) + [hsp_ops[(ex, 7)]]
            if (ex, (c - 4, 1)) in yev_ops:
                yd.append(yev_ops[(ex, (c - 4, 1))])
            else:
                yd.append(gel_ops[(ex, 4 + (c % 4))])
            ymm_ops[(ex, c)] = P.add("tensor", f_ymm, deps=yd)
            slot_consumer[(c // 2) % 2] = ymm_ops[(ex, c)]
            for which, rr in ((0, ra), (1, rb)):
                tmpo = 6656 + (((ex * 16 + c * 2 + which) % 4)) * 256

                def f_yt(e, c=c, which=which, bk=bk, ex=ex, tmpo=tmpo):
                    b = bk()
                    return e.tensor_scalar(ar(tmpo, 256),
                                           b[:, which * 256:which * 256 + 256],
                                           0.25, bp("b2q", ex * 8 + c),
                                           ALU.mult, ALU.add)
                ytdeps = [ymm_ops[(ex, c)]]
                prevkey = (ex, (c - 2, which)) if c >= 2 else                     ((ex - 1, (c + 6, which)) if ex >= 1 else None)
                if prevkey and prevkey[0] >= 0 and prevkey in yev_ops:
                    ytdeps.append(yev_ops[prevkey])
                yt_op = P.add("vector", f_yt, deps=ytdeps)

                def f_yev(e, c=c, rr=rr, tmpo=tmpo):
                    accsl = AR[:, ACC0 + c * 2048 + rr:ACC0 + (c + 1) * 2048:8]
                    return e.tensor_tensor(accsl, accsl, ar(tmpo, 256), ALU.add)
                yev_ops[(ex, (c, which))] = P.add(
                    "vector", f_yev, deps=[yt_op, shared_last])

        x2_ld = []
        for j in range(8):
            x2_ld.append(P.add("scalar", (lambda e, j=j: e.dma_start(
                out=ar(GHI_O + j * 1024, 1024),
                in_=dx[j * 128:(j + 1) * 128, :])),
                deps=[s2mm_ops[(3, 7)]], is_dma=True))

        for ex in range(E):
            moe_load(ex, 0, [slot_consumer[0]] +
                     ([shared_last, s2ev_ops[(3, 6)], presplit_last,
                       ps_ops[-1]] if ex == 0 else []))
            moe_load(ex, 1, [slot_consumer[1]])
            if ex == 0:
                tap("w1q0", W0, list(wq_lds[(0, 0)]), n=2048)
            moe_hmm(ex, 0)
            moe_hmm(ex, 1)
            moe_load(ex, 2, [hmm_ops[(ex, 1)]] +
                     ([tap("w1q0_dummy", 0, [], n=0)] if False else []))
            moe_hmm(ex, 2)
            moe_hmm(ex, 3)
            moe_load(ex, 3, [hmm_ops[(ex, 3)]])
            moe_hmm(ex, 4)
            moe_hmm(ex, 5)
            moe_load(ex, 4, [hmm_ops[(ex, 5)]])
            moe_hmm(ex, 6)
            moe_hmm(ex, 7)
            moe_load(ex, 5, [hmm_ops[(ex, 7)]])
            if ex == 0:
                tap("hb0", HBH_O, [hsp_ops[(0, 7)]], n=4096)
            moe_ymm(ex, 0)
            moe_ymm(ex, 1)
            moe_load(ex, 6, [ymm_ops[(ex, 1)]])
            moe_ymm(ex, 2)
            moe_ymm(ex, 3)
            moe_load(ex, 7, [ymm_ops[(ex, 3)]])
            moe_ymm(ex, 4)
            moe_ymm(ex, 5)
            moe_ymm(ex, 6)
            moe_ymm(ex, 7)
            tap("acc_e%d" % ex, ACC0, [yev_ops[(ex, (7, 1))], yev_ops[(ex, (7, 0))]])
        moe_vlast = yev_ops[(E - 1, (7, 1))]
        moe_tlast = ymm_ops[(E - 1, 7)]
        tap("acc_moe", ACC0, [moe_vlast, yev_ops[(E - 1, (7, 0))]])

        if TRUNC == "p3":
            _finish()
            return nc

        out_dma = []
        p7_v = []
        p7_t = []
        for j in range(16):

            def f_tr7(e, j=j):
                for q in range(8):
                    if j % 2 == 0:
                        ps = PZA if q < 4 else PZB
                        dst = ps[:, (q % 4) * 128:(q % 4) * 128 + 128]
                    else:
                        dst = PF[:, q * 128:q * 128 + 128]
                    last = e.transpose(dst, acc_ap(q, j * 128, 128), IDN[:])
                return last
            tdeps = [moe_vlast]
            if j >= 2:
                tdeps.append(p7_v[j - 2])
            ttr = P.add("tensor", f_tr7, deps=tdeps)
            p7_t.append(ttr)

            def f_add(e, j=j):
                xsl = GHI_O + (j % 8) * 1024
                osl = XN0 + (j % 4) * 1024
                if j % 2 == 0:
                    e.tensor_tensor(ar(osl, 512), PZA[:, 0:512],
                                    ar(xsl, 512), ALU.add)
                    return e.tensor_tensor(ar(osl + 512, 512),
                                           PZB[:, 0:512],
                                           ar(xsl + 512, 512),
                                           ALU.add)
                return e.tensor_tensor(ar(osl, 1024),
                                       PF[:, 0:1024],
                                       ar(xsl, 1024), ALU.add)
            vdeps = [ttr, x2_ld[j], moe_tlast]
            if j >= 4:
                vdeps.append(out_dma[j - 4])
            vop = P.add("vector", f_add, deps=vdeps)
            p7_v.append(vop)
            if j + 8 < 16:
                x2_ld.append(P.add("scalar", (lambda e, j=j: e.dma_start(
                    out=ar(GHI_O + (j % 8) * 1024, 1024),
                    in_=dx[(j + 8) * 128:(j + 9) * 128, :])),
                    deps=[vop], is_dma=True))
            out_dma.append(P.add("sync" if j % 2 == 0 else "scalar",
                                 (lambda e, j=j: e.dma_start(
                                     out=dout[j * 128:(j + 1) * 128, :],
                                     in_=ar(XN0 + (j % 4) * 1024, 1024))),
                                 deps=[vop], is_dma=True))
        P.add("sync", lambda e: None, deps=[out_dma[15], out_dma[14]])

        P.emit(blk, sems)
    return nc


def _lhsT_cols(w, nk, nm):
    t = w.reshape(nk, 128, nm, 128)
    return np.ascontiguousarray(t.transpose(2, 1, 0, 3).reshape(nm, 128, nk * 128))


def _pm(a, n):
    X = a.shape[1]
    return np.ascontiguousarray(a.reshape(n, 128, X).transpose(1, 0, 2)
                                .reshape(128, n * X))


def _colpack(b):
    n = b.shape[0] // 128
    return np.ascontiguousarray(b.reshape(n, 128).T)


def kernel(L0_attn_x, L0_attn_attn_W, L0_attn_attn_b, L0_attn_freq_W,
           L0_attn_freq_b, L0_attn_out_W, L0_attn_out_b, L0_attn_proj_W,
           L0_attn_proj_b, L0_attn_identity, L0_ffn_t_emb, L0_ffn_sw1,
           L0_ffn_sb1, L0_ffn_sw3, L0_ffn_sb3, L0_ffn_sw2, L0_ffn_sb2,
           L0_ffn_w1, L0_ffn_b1, L0_ffn_w2, L0_ffn_b2):
    f32 = np.float32
    x = np.asarray(L0_attn_x, f32)
    attn_W = np.asarray(L0_attn_attn_W, f32)
    attn_b = np.asarray(L0_attn_attn_b, f32)
    freq_W = np.asarray(L0_attn_freq_W, f32)
    freq_b = np.asarray(L0_attn_freq_b, f32)
    out_W = np.asarray(L0_attn_out_W, f32)
    out_b = np.asarray(L0_attn_out_b, f32)
    proj_W = np.asarray(L0_attn_proj_W, f32)
    proj_b = np.asarray(L0_attn_proj_b, f32)
    ident_v = np.asarray(L0_attn_identity, f32)
    sw1 = np.asarray(L0_ffn_sw1, f32)
    sb1 = np.asarray(L0_ffn_sb1, f32)
    sw3 = np.asarray(L0_ffn_sw3, f32)
    sb3 = np.asarray(L0_ffn_sb3, f32)
    sw2 = np.asarray(L0_ffn_sw2, f32)
    sb2 = np.asarray(L0_ffn_sb2, f32)
    w1 = np.asarray(L0_ffn_w1, f32)
    b1 = np.asarray(L0_ffn_b1, f32)
    w2 = np.asarray(L0_ffn_w2, f32)
    b2 = np.asarray(L0_ffn_b2, f32)

    Wq = attn_W[:, :C]
    Wv = attn_W[:, C:]
    bq = attn_b[:C]
    bv = attn_b[C:]
    Wfy = freq_W[:C, :]
    Wfx = freq_W[C:, :]

    shared = {
        "wv": _lhsT_cols(Wv, 8, 8),
        "wqT": _lhsT_cols(np.ascontiguousarray(Wq.T), 8, 8),
        "wfy": _pm(Wfy, 8),
        "wfx": _pm(Wfx, 8),
        "wfy2": _pm(np.concatenate([Wfy.reshape(8, 128, 64)] * 2, axis=2)
                    .reshape(1024, 128), 8),
        "wfx2": _pm(np.concatenate([Wfx.reshape(8, 128, 64)] * 2, axis=2)
                    .reshape(1024, 128), 8),
        "outw": np.ascontiguousarray(out_W),
        "outwT": _pm(np.ascontiguousarray(out_W.T), 8),
        "projw": _lhsT_cols(proj_W, 8, 8),
        "sw1": _lhsT_cols(sw1, 8, 16),
        "sw3": _lhsT_cols(sw3, 8, 16),
        "sw2": _lhsT_cols(sw2, 16, 8),
        "w1": np.stack([_lhsT_cols(w1[e], 8, 8).transpose(1, 0, 2)
                        .reshape(128, 8192) for e in range(E)]),
        "w2": np.stack([_lhsT_cols(w2[e], 8, 8).transpose(1, 0, 2)
                        .reshape(128, 8192) for e in range(E)]),
        "ident": np.eye(128, dtype=f32),
        "ones1": np.ones((1, 128), dtype=f32),
        "ones1b": np.ones((1, 128), dtype=ml_dtypes.bfloat16),
    }
    bpk = np.zeros((128, NBIAS), f32)
    bpk[:, BC["bv"]:BC["bv"] + 8] = _colpack(bv)
    bpk[:, BC["projb"]:BC["projb"] + 8] = _colpack(proj_b)
    bpk[:, BC["sb1"]:BC["sb1"] + 16] = _colpack(sb1)
    bpk[:, BC["sb3"]:BC["sb3"] + 16] = _colpack(sb3)
    bpk[:, BC["sb2c"]:BC["sb2c"] + 8] = _colpack(sb2)
    bpk[:, BC["b1"]:BC["b1"] + 64] = np.concatenate(
        [_colpack(b1[e]) for e in range(E)], axis=1)
    bpk[:, BC["b2q"]:BC["b2q"] + 64] = np.concatenate(
        [_colpack(b2[e]) for e in range(E)], axis=1)
    bpk[:, BC["ob"]:BC["ob"] + 8] = _colpack(out_b)
    bpk[:, BC["bq"]:BC["bq"] + 8] = _colpack(bq)
    bpk[:, BC["id"]:BC["id"] + 8] = _colpack(ident_v)
    bpk[:, BC["obc"]:BC["obc"] + 8] = _colpack(out_b)
    bpk[:, BC["fb2"]] = np.concatenate([freq_b, freq_b])
    bpk[:, BC["pihalf"]] = np.concatenate([np.zeros(64, f32),
                                           np.full(64, PI / 2, f32)])
    bpk[:, BC["twopi"]] = 2 * PI
    bpk[:, BC["invC"]] = 1.0 / C
    bpk[:, BC["epsln"]] = EPS_LN
    shared["biaspack"] = bpk

    nc = build_nc()
    in_maps = []
    for b in range(8):
        m = dict(shared)
        m["x"] = np.ascontiguousarray(x[b])
        in_maps.append(m)
    res = run_bass_kernel_spmd(nc, in_maps, core_ids=list(range(8)))
    globals()["_last_results"] = res.results
    out = np.stack([res.results[b]["out"] for b in range(8)])
    return out.astype(np.float32)
